# version 09
# baseline (speedup 1.0000x reference)
import numpy as np
import ml_dtypes
import concourse.bass as bass
import concourse.mybir as mybir
from concourse.bass_utils import run_bass_kernel_spmd
from contextlib import ExitStack

F32 = mybir.dt.float32
BF16 = mybir.dt.bfloat16
I32 = mybir.dt.int32
AF = mybir.ActivationFunctionType
ALU = mybir.AluOpType
AX = mybir.AxisListType.X
BF = ml_dtypes.bfloat16

D_MODEL = 2048
D_INNER = 4096
N_HEADS = 64
D_HEAD = 64
D_STATE = 128
K_CONV = 4
D_FF = 8192
HID = 5632
CONV_DIM = D_INNER + 2 * D_STATE
NFC = CONV_DIM // 128
EPS = 1e-5
HALO = 16
NT = 1024 + HALO
Q = 128
NCH = 8
KD = D_MODEL // 128
NEG = -30000.0

T9 = [(0, HALO)] + [(HALO + 128 * i, 128) for i in range(8)]
T8 = [(128 * i, 128) for i in range(8)]


def _splits(n):
    out, o = [], 0
    while o < n:
        s = min(512, n - o)
        out.append((o, s))
        o += s
    return out


class Emit:
    ENGS = ("sync", "scalar", "vector", "tensor", "gpsimd")
    CHAIN = {"vector": "vec", "scalar": "act"}

    def __init__(self):
        self.q = {e: [] for e in self.ENGS}
        self.cnt = {}

    def op(self, eng, fn, sem=None, amt=1, chain=True):
        ch = self.CHAIN.get(eng)
        if ch is not None and chain:
            self.wait(eng, ch, self.cnt.get(ch, 0))
        if sem is not None:
            self.cnt[sem] = self.cnt.get(sem, 0) + amt
            self.q[eng].append(("op", fn, sem, amt))
            return self.cnt[sem]
        self.q[eng].append(("op", fn, None, 0))
        return None

    def dma(self, eng, out, in_, sem="al", **kw):
        return self.op(eng, (lambda e, o=out, i=in_, k=kw: e.dma_start(out=o, in_=i, **k)),
                       sem, 16, chain=False)

    def wait(self, eng, sem, val):
        if val:
            self.cnt.setdefault(sem, 0)
            self.q[eng].append(("wait", sem, val))

    def now(self, sem):
        return self.cnt.get(sem, 0)

    def replay(self, eng_obj, eng_name, sems):
        for item in self.q[eng_name]:
            if item[0] == "op":
                _, fn, sem, amt = item
                ins = fn(eng_obj)
                if sem is not None:
                    ins.then_inc(sems[sem], amt)
            else:
                _, sem, val = item
                eng_obj.wait_ge(sems[sem], val)


class Build:
    def __init__(self, debug=(), stages=99):
        self.nc = bass.Bass("TRN2", target_bir_lowering=False)
        self.E = Emit()
        self.stages = stages
        self.debug = list(debug)
        self.slot_use = [0] * 8
        self.rot_i = 0
        self.mk_dram()

    def mk_dram(self):
        nc = self.nc
        self.d = {}
        def din(name, shape, dt):
            self.d[name] = nc.dram_tensor(name, list(shape), dt, kind="ExternalInput")
        def dsc(name, shape, dt):
            self.d[name] = nc.dram_tensor(name, list(shape), dt)

        din("x_in", [NT, D_MODEL], F32)
        din("w_xbc", [NFC, KD, 128, 128], F32)
        din("w_dt", [KD, 128, N_HEADS], F32)
        din("w_z", [D_INNER // 128, KD, 128, 128], F32)
        din("w_gate", [D_FF // 128, KD, 128, 128], F32)
        din("w_up", [D_FF // 128, KD, 128, 128], F32)
        din("w_down", [KD, D_FF // 128, 128, 128], F32)
        din("w_op", [KD, D_INNER // 128, 128, 128], F32)
        din("wq_g", [HID // 128, KD, 128, 128], I32)
        din("wq_u", [HID // 128, KD, 128, 128], I32)
        din("wq_d", [KD, HID // 128, 128, 128], I32)
        din("w_conv", [NFC, K_CONV, 128, 128], F32)
        din("ccols_in", [128, 256], F32)
        din("dcol_in", [64, 2], F32)
        din("msel_in", [9, 8], F32)
        din("onehot", [64, 64 * 128], F32)
        din("ones_f", [1, 128], F32)
        din("consts_in", [128, 512], BF16)
        din("eye_f_in", [128, 128], F32)

        self.out_t = nc.dram_tensor("out", [1024, D_MODEL], F32, kind="ExternalOutput")

        dsc("d_xa", [NT, D_MODEL], F32)
        dsc("d_xn1", [NT, D_MODEL], BF16)
        dsc("d_t1", [NT, D_MODEL], BF16)
        dsc("d_f1", [D_MODEL, 1152], BF16)
        dsc("d_xbc", [CONV_DIM, 1024], BF16)
        dsc("d_y", [1024, D_INNER], BF16)
        dsc("d_mo", [D_MODEL, 1024], BF16)
        dsc("d_h", [1024, D_MODEL], F32)
        dsc("d_h2n", [1024, D_MODEL], BF16)
        dsc("d_sw", [D_MODEL, 1024], BF16)
        dsc("d_xb", [1024, D_MODEL], F32)
        dsc("d_xn3", [1024, D_MODEL], BF16)
        dsc("d_f2", [D_MODEL, 1024], BF16)
        dsc("cc_s_in", [128, D_INNER], BF16)
        dsc("cc_s_out", [8 * 128, D_INNER], BF16)
        dsc("cc_l_in", [1, 64], F32)
        dsc("cc_l_out", [8, 64], F32)
        dsc("d_probe", [128, 2048], BF16)

        self.dbg_t = {}
        for name in self.debug:
            src = self.d[name]
            self.dbg_t[name] = nc.dram_tensor(
                "dbg_" + name, list(src.shape), src.dtype, kind="ExternalOutput")

    def build(self):
        nc, E = self.nc, self.E
        ctx = ExitStack()
        def sb(name, shape, dt):
            return ctx.enter_context(nc.sbuf_tensor(name, list(shape), dt))
        with ctx:
            self.act = sb("act_fm", [128, KD * NT], BF16)
            self.ba = sb("barena", [128, 65536], BF16)
            self.wst = sb("wst", [128, 8192], BF16)
            self.silub = sb("silub", [128, 1040], BF16)
            self.consts = sb("consts", [128, 512], BF16)
            self.eye_f = sb("eye_f", [128, 128], F32)
            self.onesf = sb("onesf", [1, 128], F32)
            self.ccols = sb("ccols", [128, 256], F32)
            self.cols = sb("cols", [128, 96], F32)
            self.rows = sb("rows", [1, 1024], F32)
            self.rowsb = sb("rowsb", [1, 2 * NT], BF16)
            self.dcol = sb("dcol", [64, 8], F32)
            self.tots = sb("tots", [1, 1152], F32)
            self.smallf = sb("smallf", [128, 1088], F32)
            self.wtile = sb("wtile", [128, 64], F32)
            self.ebcT = sb("ebcT", [128, 64], F32)
            self.gcol = sb("gcol", [128, 8], F32)
            self.ldece = sb("ldece", [9, 64], F32)
            self.msel9 = sb("msel9", [9, 8], F32)
            self.bfm = sb("bfm", [128, 128], BF16)
            self.btm = sb("btm", [128, 128], BF16)
            self.bcm = sb("bcm", [128, 128], BF16)
            self.wjT = sb("wjT", [64, 8], F32)
            self.zbuf = sb("zbuf", [64, 128], F32)
            self.ps = [ctx.enter_context(nc.psum_tensor(f"ps{i}", [128, 512], F32))
                       for i in range(8)]
            nc.const_aps.register(F32, EPS, self.ccols[:, 250:251])
            sem_names = ["al", "wl", "mm", "vec", "act", "gp", "st", "cc",
                         "w0", "w1", "w2", "w3", "sl0", "sl1", "lde",
                         "g0", "g1", "ts", "fs0", "fs1", "xs0", "xs1", "ys", "oh"] + \
                        [f"e{i}" for i in range(8)]
            sems = {n: ctx.enter_context(nc.semaphore(n)) for n in sem_names}
            block = ctx.enter_context(nc.Block())

            self.emit_all()

            for name in Emit.ENGS:
                def mk(nm):
                    def f(eng):
                        E.replay(eng, nm, sems)
                    return f
                getattr(block, name)(mk(name))
        return nc

    def bv(self, c0, ncols, dt=BF16):
        ap = self.ba[:, c0:c0 + ncols]
        return ap if dt == BF16 else ap.bitcast(dt)

    def rot(self):
        s = self.rot_i % 4
        self.rot_i += 1
        return s

    def acq(self, s):
        self.E.wait("tensor", f"e{s}", self.slot_use[s])

    def edone(self, eng, s):
        self.E.op(eng, lambda e: e.nop(), f"e{s}")
        self.slot_use[s] += 1

    def job(self, pairs, out_fn, pre=()):
        E = self.E
        s = self.rot()
        self.acq(s)
        for sem, val in pre:
            E.wait("tensor", sem, val)
        n = len(pairs)
        for j, (lh, rh) in enumerate(pairs):
            last = (j == n - 1)
            E.op("tensor", lambda e, lh=lh, rh=rh, s=s, st=(j == 0),
                 sp=last, of=out_fn: e.matmul(of(s), lh, rh,
                                              start=st, stop=sp),
                 "mm" if last else None)
        return s, E.now("mm")

    STORE_SEMS = ("ts", "fs0", "fs1", "xs0", "xs1", "ys")

    def wait_stores(self, eng):
        for s in self.STORE_SEMS:
            self.E.wait(eng, s, self.E.now(s))

    def wload(self, dst_ap, src_ap, last_mm, sem="w0"):
        E = self.E
        E.wait("gpsimd", "mm", last_mm)
        E.wait("gpsimd", "vec", getattr(self, "_phv", 0))
        return (sem, E.dma("gpsimd", dst_ap, src_ap, sem=sem))

    def ln_stats(self, buf, sz, sb0, keep_v=False):
        E = self.E
        m = self.cols[:, sb0:sb0 + 1]
        q = self.cols[:, sb0 + 1:sb0 + 2]
        r = self.cols[:, sb0 + 2:sb0 + 3]
        v = self.cols[:, sb0 + 3:sb0 + 4]
        scr = self.bv(57344, 4096, F32)
        E.op("vector", lambda e: e.reduce_sum(m[0:sz, :], buf[0:sz, :], axis=AX), "vec")
        E.op("vector", lambda e: e.tensor_scalar_mul(m[0:sz, :], m[0:sz, :],
                                                     1.0 / D_MODEL), "vec")
        E.wait("scalar", "vec", E.now("vec"))
        E.op("scalar", lambda e: e.activation(
            scr[0:sz, :], buf[0:sz, :], AF.Square, accum_out=q[0:sz, :]), "act")
        E.wait("vector", "act", E.now("act"))
        E.op("vector", lambda e: e.tensor_tensor(
            v[0:sz, :], m[0:sz, :], m[0:sz, :], ALU.mult), "vec")
        E.op("vector", lambda e: e.scalar_tensor_tensor(
            v[0:sz, :], q[0:sz, :], 1.0 / D_MODEL, v[0:sz, :],
            ALU.mult, ALU.subtract), "vec")
        E.wait("scalar", "vec", E.now("vec"))
        E.op("scalar", lambda e: e.activation(r[0:sz, :], v[0:sz, :], AF.Ln,
                                              bias=EPS), "act")
        E.op("scalar", lambda e: e.activation(r[0:sz, :], r[0:sz, :], AF.Exp,
                                              scale=-0.5), "act")
        return m, q, r, v

    def emit_all(self):
        E = self.E
        n = self.stages
        steps = [
            self.ph_consts,
            self.ph_ln1,
            lambda: setattr(self, "_f", self.ph_fill_act(self.d["d_xn1"], NT, None, E.now("st"))),
            lambda: self.ph_ffnq(NT, self.d["d_f1"], self._f),
            self.ph_xa,
            lambda: setattr(self, "_f", self.ph_fill_act(self.d["d_t1"], NT, 0, E.now("st"))),
            lambda: self.ph_inproj(self._f),
            self.ph_scan,
            self.ph_exchange_z,
            self.ph_combine,
            self.ph_correction,
            self.ph_gating,
            self.ph_outproj,
            self.ph_h,
            lambda: setattr(self, "_f", self.ph_fill_act(self.d["d_h2n"], 1024, 16, E.now("st"))),
            lambda: self.ph_swiglu(self._f),
            self.ph_xb,
            lambda: setattr(self, "_f", self.ph_fill_act(self.d["d_xn3"], 1024, None, E.now("st"))),
            lambda: self.ph_ffnq(1024, self.d["d_f2"], self._f),
            self.ph_final,
        ]
        for i, s in enumerate(steps):
            if i >= n:
                break
            s()
        self.ph_debug()
        self.wait_stores("sync")

    def ph_consts(self):
        E, d = self.E, self.d
        E.dma("sync", self.consts[:, :], d["consts_in"][:, :])
        E.dma("sync", self.eye_f[:, :], d["eye_f_in"][:, :])
        E.dma("sync", self.onesf[:, :], d["ones_f"][:, :])
        E.dma("sync", self.ccols[:, :], d["ccols_in"][:, :])
        E.dma("sync", self.dcol[:, 0:2], d["dcol_in"][:, :])
        E.dma("sync", self.msel9[:, :], d["msel_in"][:, :])
        al = E.now("al")
        E.wait("scalar", "al", al)
        E.op("scalar", lambda e: e.activation(self.dcol[:, 2:3], self.dcol[:, 1:2],
                                              AF.Exp), "act")
        E.wait("vector", "act", E.now("act"))
        E.op("vector", lambda e: e.tensor_scalar_mul(self.dcol[:, 3:4],
                                                     self.dcol[:, 2:3], -1.0), "vec")
        E.wait("vector", "al", al)
        E.op("vector", lambda e: e.tensor_scalar_mul(self.ccols[:, 234:250],
                                                     self.ccols[:, 152:168], 0.5), "vec")
        E.op("vector", lambda e: e.memset(self.zbuf[:, :], 0.0), "vec")
        E.dma("sync", self.ldece[8:9, 0:64], self.d["ones_f"][0:1, 0:64], sem="ts")
        self.c_init = (E.now("vec"), E.now("act"))

    def ph_ln1(self):
        E, d = self.E, self.d
        SA = self.bv(49152, 4096, F32)
        SB = self.bv(53248, 4096, F32)
        B1 = self.bv(61440, 2048)
        E.wait("vector", "vec", self.c_init[0])
        for i, (o, sz) in enumerate(T9):
            buf = SA if i % 2 == 0 else SB
            E.wait("sync", "act", E.now("act"))
            E.wait("sync", "vec", E.now("vec"))
            av = E.dma("sync", buf[0:sz, :], d["x_in"][o:o + sz, :])
            E.wait("vector", "al", av)
            m, q, r, v = self.ln_stats(buf, sz, 84)
            E.wait("vector", "act", E.now("act"))
            E.wait("vector", "ts", E.now("ts"))
            E.op("vector", lambda e, b=buf, sz=sz, m=m, r=r: e.tensor_scalar(
                B1[0:sz, :], b[0:sz, :], m[0:sz, :], r[0:sz, :],
                ALU.subtract, ALU.mult), "vec")
            E.wait("sync", "vec", E.now("vec"))
            E.dma("sync", d["d_xn1"][o:o + sz, :], B1[0:sz, :], sem="ts")

    def ph_fill_act(self, src_tm, n_tok, colw, wait_st):
        E = self.E
        self.wait_stores("sync")
        E.wait("sync", "mm", E.now("mm"))
        for kc in range(KD):
            av = E.dma("sync", self.act[:, kc * NT:kc * NT + n_tok],
                       src_tm[0:n_tok, kc * 128:kc * 128 + 128], transpose=True)
        E.wait("vector", "al", av)
        if colw is not None:
            for kc in range(KD):
                E.op("vector", lambda e, k=kc, n=n_tok: e.tensor_scalar_mul(
                    self.act[:, k * NT:k * NT + n], self.act[:, k * NT:k * NT + n],
                    self.ccols[:, colw + k:colw + k + 1]), "vec")
        return (E.now("al"), E.now("vec"))

    def ph_ffnq(self, n_tok, dst_slab, fills):
        E, d = self.E, self.d
        NH = HID // 128
        sp = _splits(n_tok)
        al_f, vec_f = fills
        ones_c = self.consts[:, 448:449]
        neg128 = self.consts[0:1, 128:256]
        pre_f = [("al", al_f), ("vec", vec_f)]

        for (o, sz) in sp:
            pairs = [(ones_c, self.act[:, kc * NT + o:kc * NT + o + sz])
                     for kc in range(KD)]
            s, mmv = self.job(pairs, lambda s, sz=sz: self.ps[s][0:1, 0:sz], pre=pre_f)
            E.wait("vector", "mm", mmv)
            E.op("vector", lambda e, s=s, o=o, sz=sz: e.tensor_copy(
                self.rowsb[0:1, o:o + sz], self.ps[s][0:1, 0:sz]), "vec")
            self.edone("vector", s)
        vrow = E.now("vec")

        GU = self.bv(0, NH * n_tok)
        self._sil_rd = {}
        self._phv = E.now("vec")
        pan_mm = {0: E.now("mm"), 1: E.now("mm"), 2: E.now("mm"), 3: E.now("mm")}
        for hc in range(NH):
            p = hc % 2
            gp_ = self.wst[:, p * 4096:p * 4096 + 2048]
            up_ = self.wst[:, p * 4096 + 2048:p * 4096 + 4096]
            wlg = self.wload(gp_.rearrange("p (k q) -> p k q", q=128),
                             d["wq_g"][hc].transpose([1, 0, 2]), pan_mm[2 * p],
                             sem=f"w{2 * p}")
            wlu = self.wload(up_.rearrange("p (k q) -> p k q", q=128),
                             d["wq_u"][hc].transpose([1, 0, 2]), pan_mm[2 * p + 1],
                             sem=f"w{2 * p + 1}")
            for si, (o, sz) in enumerate(sp):
                gpairs = [(gp_[:, kc * 128:kc * 128 + 128],
                           self.act[:, kc * NT + o:kc * NT + o + sz])
                          for kc in range(KD)] + \
                         [(neg128, self.rowsb[0:1, o:o + sz])]
                upairs = [(up_[:, kc * 128:kc * 128 + 128],
                           self.act[:, kc * NT + o:kc * NT + o + sz])
                          for kc in range(KD)] + \
                         [(neg128, self.rowsb[0:1, o:o + sz])]
                prew = pre_f + [wlg, ("vec", vrow)]
                sA, mmA = self.job(gpairs, lambda s, sz=sz: self.ps[s][:, 0:sz],
                                   pre=prew)
                sB, mmB = self.job(upairs, lambda s, sz=sz: self.ps[s][:, 0:sz],
                                   pre=[wlu])
                pan_mm[2 * p] = mmA
                pan_mm[2 * p + 1] = mmB
                sgc = self.ccols[:, 64 + hc:64 + hc + 1]
                suc = self.ccols[:, 108 + hc:108 + hc + 1]
                sl = self.silub[:, o:o + sz]
                E.wait("scalar", "mm", mmA)
                E.wait("scalar", "vec", self._sil_rd.get(o, 0))
                E.op("scalar", lambda e, sA=sA, sz=sz, sgc=sgc, sl=sl: e.activation(
                    sl, self.ps[sA][:, 0:sz], AF.Silu, scale=sgc), "act")
                self.edone("scalar", sA)
                av = E.now("act")
                E.wait("vector", "mm", mmB)
                E.wait("vector", "act", av)
                gu_sl = GU[:, hc * n_tok + o:hc * n_tok + o + sz]
                E.op("vector", lambda e, sB=sB, sz=sz, suc=suc, sl=sl, g=gu_sl:
                     e.scalar_tensor_tensor(g, self.ps[sB][:, 0:sz], suc, sl,
                                            ALU.mult, ALU.mult), "vec")
                self._sil_rd[o] = E.now("vec")
                self.edone("vector", sB)
                guv = E.now("vec")
                rs = 4 + si
                if hc == 0:
                    self.acq(rs)
                E.wait("tensor", "vec", guv)
                E.op("tensor", lambda e, rs=rs, g=gu_sl, sz=sz, st=(hc == 0),
                     sp_=(hc == NH - 1): e.matmul(self.ps[rs][0:1, 0:sz], ones_c, g,
                                                  start=st, stop=sp_), "mm")
        mm_sgu = E.now("mm")
        E.wait("vector", "mm", mm_sgu)
        for si, (o, sz) in enumerate(sp):
            E.op("vector", lambda e, si=si, o=o, sz=sz: e.tensor_copy(
                self.rowsb[0:1, NT + o:NT + o + sz], self.ps[4 + si][0:1, 0:sz]),
                "vec")
            self.edone("vector", 4 + si)
        vgu = E.now("vec")

        self._phv = E.now("vec")
        pan_mm2 = {0: mm_sgu, 1: mm_sgu}
        OS = self.wst[:, 0:2048]
        ost_st = {0: 0, 1: 0}
        for dc in range(KD):
            p = dc % 2
            pan = self.act[:, p * 5632:p * 5632 + 5632]
            wld = self.wload(pan.rearrange("p (k q) -> p k q", q=128),
                             d["wq_d"][dc].transpose([1, 0, 2]), pan_mm2[p],
                             sem=f"w{p}")
            for si, (o, sz) in enumerate(sp):
                pairs = [(pan[:, hc * 128:hc * 128 + 128],
                          GU[:, hc * n_tok + o:hc * n_tok + o + sz])
                         for hc in range(NH)] + \
                        [(neg128, self.rowsb[0:1, NT + o:NT + o + sz])]
                s, mmv = self.job(pairs, lambda s, sz=sz: self.ps[s][:, 0:sz],
                                  pre=[wld, ("vec", vgu)])
                pan_mm2[p] = mmv
                sdc = self.ccols[:, 234 + dc:234 + dc + 1]
                ost = OS[:, (si % 2) * 1024:(si % 2) * 1024 + sz]
                E.wait("vector", "mm", mmv)
                E.wait("vector", f"fs{si % 2}", ost_st[si % 2])
                E.op("vector", lambda e, s=s, sz=sz, sdc=sdc, ost=ost:
                     e.tensor_scalar_mul(ost, self.ps[s][:, 0:sz], sdc), "vec")
                self.edone("vector", s)
                E.wait("sync", "vec", E.now("vec"))
                E.dma("sync", dst_slab[dc * 128:dc * 128 + 128, o:o + sz], ost,
                      sem=f"fs{si % 2}")
                ost_st[si % 2] = E.now(f"fs{si % 2}")

    def ph_xa(self):
        E, d = self.E, self.d
        SA = self.bv(49152, 4096, F32)
        SB = self.bv(53248, 4096, F32)
        B1 = self.bv(61440, 2048)
        B2 = self.bv(63488, 2048)
        self.wait_stores("sync")
        for i, (o, sz) in enumerate(T9):
            buf = SA if i % 2 == 0 else SB
            E.wait("sync", "act", E.now("act"))
            E.wait("sync", "vec", E.now("vec"))
            E.wait("sync", "ts", E.now("ts"))
            av = E.dma("sync", buf[0:sz, :], d["x_in"][o:o + sz, :])
            avT = E.dma("sync", B1[0:128, :], d["d_f1"][:, o:o + 128], transpose=True)
            E.wait("vector", "al", avT)
            E.op("vector", lambda e, b=buf, sz=sz: e.tensor_tensor(
                b[0:sz, :], b[0:sz, :], B1[0:sz, :], ALU.add), "vec")
            E.wait("sync", "vec", E.now("vec"))
            E.dma("sync", d["d_xa"][o:o + sz, :], buf[0:sz, :], sem="ts")
            m, q, r, v = self.ln_stats(buf, sz, 4 * i)
            w = self.smallf[:, 1024 + i:1024 + i + 1]
            E.wait("vector", "act", E.now("act"))
            E.op("vector", lambda e, sz=sz, r=r, w=w: e.tensor_tensor(
                w[0:sz, :], r[0:sz, :], r[0:sz, :], ALU.mult), "vec")
            E.op("vector", lambda e, sz=sz, v=v, w=w: e.tensor_tensor(
                w[0:sz, :], w[0:sz, :], v[0:sz, :], ALU.mult), "vec")
            E.wait("scalar", "vec", E.now("vec"))
            E.op("scalar", lambda e, sz=sz, w=w: e.activation(
                w[0:sz, :], w[0:sz, :], AF.Ln, bias=EPS), "act")
            E.op("scalar", lambda e, sz=sz, w=w: e.activation(
                w[0:sz, :], w[0:sz, :], AF.Exp, scale=-0.5), "act")
            E.wait("vector", "act", E.now("act"))
            E.wait("vector", "ts", E.now("ts"))
            E.op("vector", lambda e, sz=sz, r=r, w=w: e.tensor_tensor(
                w[0:sz, :], w[0:sz, :], r[0:sz, :], ALU.mult), "vec")
            E.op("vector", lambda e, b=buf, sz=sz, m=m, w=w: e.tensor_scalar(
                B2[0:sz, :], b[0:sz, :], m[0:sz, :], w[0:sz, :],
                ALU.subtract, ALU.mult), "vec")
            E.wait("sync", "vec", E.now("vec"))
            E.dma("sync", d["d_t1"][o:o + sz, :], B2[0:sz, :], sem="ts")

    def ph_inproj(self, fills):
        E, d = self.E, self.d
        sp = _splits(NT)
        al_f, vec_f = fills
        pre_f = [("al", al_f), ("vec", vec_f)]
        RAW = [self.bv(0, 1040), self.bv(1040, 1040)]
        self._phv = E.now("vec")
        pan_mm = {0: E.now("mm"), 1: E.now("mm")}
        conv_mm = {0: 0, 1: 0}
        cst_st = {0: 0, 1: 0}
        for fc in range(NFC):
            p = fc % 2
            pan = self.wst[:, p * 2560:p * 2560 + 2048]
            dia = self.wst[:, p * 2560 + 2048:p * 2560 + 2560]
            wlp = self.wload(pan.rearrange("p (k q) -> p k q", q=128),
                             d["w_xbc"][fc].transpose([1, 0, 2]), pan_mm[p],
                             sem=f"w{p}")
            wld = self.wload(dia.rearrange("p (k q) -> p k q", q=128),
                             d["w_conv"][fc].transpose([1, 0, 2]), pan_mm[p],
                             sem=f"w{p}")
            raw = RAW[p]
            rawv = []
            for si, (o, sz) in enumerate(sp):
                pairs = [(pan[:, kc * 128:kc * 128 + 128],
                          self.act[:, kc * NT + o:kc * NT + o + sz])
                         for kc in range(KD)]
                s, mmv = self.job(pairs, lambda s, sz=sz: self.ps[s][:, 0:sz],
                                  pre=pre_f + [wld])
                pan_mm[p] = mmv
                E.wait("vector", "mm", mmv)
                E.wait("vector", "mm", conv_mm[p])
                E.op("vector", lambda e, s=s, o=o, sz=sz, raw=raw: e.tensor_copy(
                    raw[:, o:o + sz], self.ps[s][:, 0:sz]), "vec")
                self.edone("vector", s)
            vr = E.now("vec")
            for (o, sz) in _splits(1024):
                pairs = [(dia[:, j * 128:j * 128 + 128],
                          raw[:, o + 13 + j:o + 13 + j + sz]) for j in range(K_CONV)]
                s, mmv = self.job(pairs, lambda s, sz=sz: self.ps[s][:, 0:sz],
                                  pre=[("vec", vr), wld])
                pan_mm[p] = mmv
                conv_mm[p] = mmv
                cp = 0 if o == 0 else 1
                cbc = self.ccols[:, 168 + fc:168 + fc + 1]
                cst = self.silub[:, 0:sz] if o == 0 else self.silub[:, 512:512 + sz]
                E.wait("scalar", "mm", mmv)
                E.wait("scalar", f"xs{cp}", cst_st[cp])
                E.op("scalar", lambda e, s=s, sz=sz, cbc=cbc, cst=cst: e.activation(
                    cst, self.ps[s][:, 0:sz], AF.Silu, bias=cbc), "act")
                self.edone("scalar", s)
                E.wait("sync", "act", E.now("act"))
                E.dma("sync", d["d_xbc"][fc * 128:fc * 128 + 128, o:o + sz], cst,
                      sem=f"xs{cp}")
                cst_st[cp] = E.now(f"xs{cp}")
        DT = self.bv(40960, 10400, F32)
        wdt = self.wst[:, 5120 + 0:5120 + 1024]
        wl = self.wload(wdt.rearrange("p (k q) -> p k q", q=64),
                        d["w_dt"][:, :, :].transpose([1, 0, 2]), max(pan_mm.values()),
                        sem="w2")
        for si, (o, sz) in enumerate(sp):
            pairs = [(wdt[:, kc * 64:kc * 64 + 64],
                      self.act[:, kc * NT + o:kc * NT + o + sz])
                     for kc in range(KD)]
            s, mmv = self.job(pairs, lambda s, sz=sz: self.ps[s][0:64, 0:sz],
                              pre=pre_f + [wl])
            E.wait("scalar", "mm", mmv)
            E.op("scalar", lambda e, s=s, o=o, sz=sz: e.activation(
                DT[0:64, o:o + sz], self.ps[s][0:64, 0:sz], AF.Exp,
                bias=self.dcol[:, 0:1]), "act")
            self.edone("scalar", s)
            E.op("scalar", lambda e, o=o, sz=sz: e.activation(
                DT[0:64, o:o + sz], DT[0:64, o:o + sz], AF.Ln, bias=1.0), "act")
        E.op("scalar", lambda e: e.activation(DT[0:64, 1040:2080], DT[0:64, 0:1040],
                                              AF.Ln), "act")
        adt = E.now("act")
        E.wait("vector", "act", adt)
        E.op("vector", lambda e: e.tensor_scalar_mul(
            DT[0:64, 2080:3120], DT[0:64, 0:1040], self.dcol[:, 3:4]), "vec")
        self.dt_ready = (E.now("vec"), E.now("act"))

    def ph_scan(self):
        E, d = self.E, self.d
        DT = self.bv(40960, 10400, F32)
        ACU = lambda c: DT[0:64, 3120 + c * 128:3120 + (c + 1) * 128]
        LMA = DT[0:64, 4144:5168]
        OH = self.bv(0, 16384, F32)
        XT = self.bv(16384, 4096)
        XW = self.bv(20480, 4096)
        EXD = self.bv(24576, 1024)
        SCO = self.bv(25600, 1024)
        ARG = self.bv(30720, 2048, F32)
        STATE = self.bv(26624, 4096)
        CTI = self.bv(32768, 2048, F32)
        WX = self.bv(34816, 1024, F32)
        CFM = self.bv(35840, 1024)
        YSB = self.bv(36864, 4096)

        E.wait("sync", "vec", self.dt_ready[0])
        E.wait("sync", "mm", E.now("mm"))
        self.wait_stores("sync")
        oh_al = E.dma("sync", OH[0:64, :], d["onehot"][:, :], sem="oh")
        E.wait("tensor", "oh", oh_al)
        E.wait("vector", "oh", oh_al)
        E.op("vector", lambda e: e.memset(STATE[:, :], 0.0), "vec")
        E.op("vector", lambda e: e.memset(self.tots[:, :], 0.0), "vec")
        for c in range(NCH):
            E.op("vector", lambda e, c=c: e.tensor_tensor_scan(
                ACU(c), DT[0:64, 2080 + 16 + c * 128:2080 + 16 + (c + 1) * 128],
                self.zbuf[:, :], 0.0, ALU.add, ALU.add), "vec")
            E.op("vector", lambda e, c=c: e.reduce_sum(
                self.smallf[0:64, 1024 + c:1024 + c + 1],
                DT[0:64, 2080 + 16 + c * 128:2080 + 16 + (c + 1) * 128],
                axis=AX), "vec")
        vq = E.now("vec")
        E.op("vector", lambda e: e.tensor_tensor(
            LMA, DT[0:64, 1040 + 16:1040 + 16 + 1024], DT[0:64, 3120:4144],
            ALU.subtract), "vec")
        E.wait("tensor", "vec", vq)
        self.acq(6)
        for c in range(NCH):
            E.op("tensor", lambda e, c=c: e.transpose(
                self.ps[6][0:1, c * 64:c * 64 + 64],
                self.smallf[0:64, 1024 + c:1024 + c + 1],
                self.eye_f[0:64, 0:64]), "mm")
        mmv = E.now("mm")
        E.wait("vector", "mm", mmv)
        E.op("vector", lambda e: e.tensor_copy(self.tots[0:1, 0:512],
                                               self.ps[6][0:1, 0:512]), "vec")
        self.edone("vector", 6)
        for c in range(1, NCH):
            E.op("vector", lambda e, c=c: e.tensor_tensor(
                self.tots[0:1, 512 + c * 64:512 + (c + 1) * 64],
                self.tots[0:1, 512 + (c - 1) * 64:512 + c * 64],
                self.tots[0:1, (c - 1) * 64:c * 64], ALU.add), "vec")
        E.op("vector", lambda e: e.tensor_tensor(
            self.tots[0:1, 1088:1152], self.tots[0:1, 512 + 448:512 + 512],
            self.tots[0:1, 448:512], ALU.add), "vec")
        vpre = E.now("vec")

        vch_prev = E.now("vec")
        st_prev = E.now("ys")
        for c in range(NCH):
            self.acq(6)
            E.wait("tensor", "vec", vpre)
            E.op("tensor", lambda e, c=c: e.transpose(
                self.ps[6][0:128, 0:64], ACU(c), self.eye_f[0:64, 0:64]), "mm")
            E.op("tensor", lambda e, c=c: e.transpose(
                self.ps[6][0:128, 64:128], LMA[:, c * 128:(c + 1) * 128],
                self.eye_f[0:64, 0:64]), "mm")
            mmv = E.now("mm")
            E.wait("vector", "mm", mmv)
            E.op("vector", lambda e, c=c: e.tensor_copy(
                CTI[:, c * 128:(c + 1) * 128], self.ps[6][:, 0:128]), "vec")
            self.edone("vector", 6)
            vc = E.now("vec")
            E.wait("scalar", "vec", vc)
            E.op("scalar", lambda e, c=c: e.activation(
                WX[:, c * 64:(c + 1) * 64], CTI[:, c * 128:c * 128 + 64], AF.Exp),
                "act")
            self.acq(6)
            E.op("tensor", lambda e, c=c: e.matmul(
                self.ps[6][:, 0:64], self.onesf[:, :],
                self.tots[0:1, c * 64:(c + 1) * 64], start=True, stop=True), "mm")
            mmv = E.now("mm")
            E.wait("vector", "mm", mmv)
            E.wait("vector", "vec", vc)
            E.op("vector", lambda e, c=c: e.tensor_tensor(
                self.smallf[:, 0:64], self.ps[6][:, 0:64],
                CTI[:, c * 128 + 64:c * 128 + 128], ALU.add), "vec")
            vv = E.now("vec")
            E.wait("scalar", "vec", vv)
            E.wait("scalar", "vec", vch_prev)
            E.op("scalar", lambda e: e.activation(self.ebcT[:, :], self.ps[6][:, 0:64],
                                                  AF.Exp), "act")
            self.edone("scalar", 6)
            E.op("scalar", lambda e: e.activation(self.wtile[:, :],
                                                  self.smallf[:, 0:64], AF.Exp), "act")
            a_pre = E.now("act")
            E.wait("sync", "mm", E.now("mm"))
            E.wait("sync", "vec", vch_prev)
            alx = E.dma("sync", XT[:, :], d["d_xbc"][0:4096, c * 128:(c + 1) * 128],
                        transpose=True)
            alb = E.dma("sync", self.bfm[:, :],
                        d["d_xbc"][4096:4224, c * 128:(c + 1) * 128])
            alc = E.dma("sync", CFM[:, c * 128:(c + 1) * 128],
                        d["d_xbc"][4224:4352, c * 128:(c + 1) * 128])
            albt = E.dma("sync", self.btm[:, :],
                         d["d_xbc"][4096:4224, c * 128:(c + 1) * 128], transpose=True)
            alx = alb = alc = albt
            E.wait("vector", "act", a_pre)
            E.wait("vector", "al", albt)
            for g in range(8):
                E.op("vector", lambda e, g=g: e.tensor_tensor(
                    XW[:, g * 512:(g + 1) * 512].rearrange("p (a b) -> p a b", b=64),
                    XT[:, g * 512:(g + 1) * 512].rearrange("p (a b) -> p a b", b=64),
                    self.wtile[:, g * 8:(g + 1) * 8].unsqueeze(2)
                        .broadcast_to([128, 8, 64]), ALU.mult), "vec")
            vxw = E.now("vec")
            self.acq(4)
            E.wait("tensor", "al", max(alb, alc))
            E.op("tensor", lambda e, c=c: e.matmul(
                self.ps[4][:, 0:128], self.bfm[:, :], CFM[:, c * 128:(c + 1) * 128],
                start=True, stop=True), "mm")
            mmv = E.now("mm")
            E.wait("vector", "mm", mmv)
            E.op("vector", lambda e: e.tensor_tensor(
                self.bcm[:, :], self.ps[4][:, 0:128], self.consts[:, 256:384],
                ALU.mult), "vec")
            self.edone("vector", 4)
            vbc = E.now("vec")
            mmy_prev = E.now("mm")
            vsc_prev = E.now("vec")
            aexp_prev = E.now("act")
            for g in range(8):
                dsl = (0, 1) if g % 2 == 0 else (2, 3)
                for k in range(8):
                    h = 8 * g + k
                    s = dsl[k // 4]
                    if k % 4 == 0:
                        self.acq(s)
                    E.op("tensor", lambda e, h=h, k=k, s=s, c=c: e.matmul(
                        self.ps[s][:, (k % 4) * 128:(k % 4) * 128 + 128],
                        OH[0:64, h * 128:(h + 1) * 128], ACU(c),
                        start=True, stop=True), "mm" if k == 7 else None)
                mmv = E.now("mm")
                E.wait("vector", "mm", mmv)
                E.wait("vector", "act", aexp_prev)
                for k in range(8):
                    h = 8 * g + k
                    s = dsl[k // 4]
                    E.op("vector", lambda e, h=h, k=k, s=s, c=c: e.tensor_scalar(
                        ARG[:, k * 128:(k + 1) * 128],
                        self.ps[s][:, (k % 4) * 128:(k % 4) * 128 + 128],
                        CTI[:, c * 128 + 64 + h:c * 128 + 64 + h + 1], 30.0,
                        ALU.add, ALU.min), "vec")
                    if k % 4 == 3:
                        self.edone("vector", s)
                varg = E.now("vec")
                E.wait("scalar", "vec", varg)
                E.op("scalar", lambda e: e.activation(EXD[:, :], ARG[:, :],
                                                      AF.Exp), "act")
                aexp = E.now("act")
                aexp_prev = aexp
                E.wait("vector", "act", aexp)
                E.wait("vector", "vec", vbc)
                E.wait("vector", "mm", mmy_prev)
                if g == 0:
                    E.wait("vector", "ys", st_prev)
                E.op("vector", lambda e: e.tensor_tensor(
                    SCO[:, :].rearrange("p (a b) -> p a b", b=128),
                    EXD[:, :].rearrange("p (a b) -> p a b", b=128),
                    self.bcm[:, :].unsqueeze(1).broadcast_to([128, 8, 128]),
                    ALU.mult), "vec")
                vsc = E.now("vec")
                self.acq(5)
                for k in range(8):
                    E.op("tensor", lambda e, g=g, k=k, c=c: e.matmul(
                        self.ps[5][:, k * 64:(k + 1) * 64],
                        CFM[:, c * 128:(c + 1) * 128],
                        STATE[:, g * 512 + k * 64:g * 512 + (k + 1) * 64],
                        start=True, stop=True), None)
                self.acq(4)
                E.wait("tensor", "vec", vsc)
                for k in range(8):
                    E.op("tensor", lambda e, g=g, k=k: e.matmul(
                        self.ps[4][:, k * 64:(k + 1) * 64],
                        SCO[:, k * 128:(k + 1) * 128],
                        XT[:, g * 512 + k * 64:g * 512 + (k + 1) * 64],
                        start=True, stop=True), "mm" if k == 7 else None)
                mmy = E.now("mm")
                mmy_prev = mmy
                E.wait("vector", "mm", mmy)
                E.op("vector", lambda e, g=g, c=c: e.tensor_tensor(
                    self.smallf[:, 0:512].rearrange("p (a b) -> p a b", b=64),
                    self.ps[5][:, 0:512].rearrange("p (a b) -> p a b", b=64),
                    WX[:, c * 64:(c + 1) * 64][:, g * 8:(g + 1) * 8].unsqueeze(2)
                        .broadcast_to([128, 8, 64]), ALU.mult), "vec")
                self.edone("vector", 5)
                E.op("vector", lambda e, g=g: e.tensor_tensor(
                    YSB[:, g * 512:(g + 1) * 512], self.smallf[:, 0:512],
                    self.ps[4][:, 0:512], ALU.add), "vec")
                self.edone("vector", 4)
                E.op("vector", lambda e, g=g: e.tensor_tensor(
                    self.smallf[:, 512:1024].rearrange("p (a b) -> p a b", b=64),
                    STATE[:, g * 512:(g + 1) * 512].rearrange("p (a b) -> p a b", b=64),
                    self.ebcT[:, g * 8:(g + 1) * 8].unsqueeze(2)
                        .broadcast_to([128, 8, 64]), ALU.mult), "vec")
                vdec = E.now("vec")
                self.acq(7)
                E.wait("tensor", "al", albt)
                for k in range(8):
                    E.op("tensor", lambda e, g=g, k=k: e.matmul(
                        self.ps[7][:, k * 64:(k + 1) * 64], self.btm[:, :],
                        XW[:, g * 512 + k * 64:g * 512 + (k + 1) * 64],
                        start=True, stop=True), "mm" if k == 7 else None)
                mms = E.now("mm")
                E.wait("vector", "mm", mms)
                E.op("vector", lambda e, g=g: e.tensor_tensor(
                    STATE[:, g * 512:(g + 1) * 512], self.smallf[:, 512:1024],
                    self.ps[7][:, 0:512], ALU.add), "vec")
                self.edone("vector", 7)
                vsc_prev = E.now("vec")
            vch = E.now("vec")
            vch_prev = vch
            E.wait("sync", "vec", vch)
            E.dma("sync", d["d_y"][c * 128:(c + 1) * 128, :], YSB[:, :], sem="ys")
            st_prev = E.now("ys")
        E.wait("sync", "vec", E.now("vec"))
        E.dma("sync", d["cc_s_in"][:, :], STATE[:, :], sem="ys")
        E.dma("sync", d["cc_l_in"][:, :], self.tots[0:1, 1088:1152], sem="ys")
        self.pack_st = E.now("ys")

    def ph_exchange_z(self):
        E, d = self.E, self.d
        E.wait("gpsimd", "ys", self.pack_st)
        E.op("gpsimd", lambda e: e.collective_compute(
            "AllGather", ALU.bypass, replica_groups=[list(range(8))],
            ins=[d["cc_s_in"][:, :]], outs=[d["cc_s_out"][:, :]]), "cc")
        E.op("gpsimd", lambda e: e.collective_compute(
            "AllGather", ALU.bypass, replica_groups=[list(range(8))],
            ins=[d["cc_l_in"][:, :]], outs=[d["cc_l_out"][:, :]]), "cc")
        SZ = self.bv(0, 32768)
        self._phv = E.now("vec")
        pan_mm = {0: E.now("mm"), 1: E.now("mm")}
        for zc in range(D_INNER // 128):
            p = zc % 2
            pan = self.wst[:, p * 2048:p * 2048 + 2048]
            wl = self.wload(pan.rearrange("p (k q) -> p k q", q=128),
                            d["w_z"][zc].transpose([1, 0, 2]), pan_mm[p],
                            sem=f"w{p}")
            for (o, sz) in _splits(1024):
                pairs = [(pan[:, kc * 128:kc * 128 + 128],
                          self.act[:, kc * NT + HALO + o:kc * NT + HALO + o + sz])
                         for kc in range(KD)]
                s, mmv = self.job(pairs, lambda s, sz=sz: self.ps[s][:, 0:sz],
                                  pre=[wl, ("ys", self.pack_st)])
                pan_mm[p] = mmv
                E.wait("scalar", "mm", mmv)
                E.op("scalar", lambda e, s=s, zc=zc, o=o, sz=sz: e.activation(
                    SZ[:, zc * 1024 + o:zc * 1024 + o + sz], self.ps[s][:, 0:sz],
                    AF.Silu), "act")
                self.edone("scalar", s)
        self.z_done = E.now("act")

    def ph_gating(self):
        E, d = self.E, self.d
        SZ = self.bv(0, 32768)
        VP = self.bv(32768, 32768)
        E.wait("sync", "ys", self.corr_st)
        self.wait_stores("sync")
        self.acq(4)
        self.acq(5)
        gn_mm = [E.now("mm"), E.now("mm")]
        v_done = [0, 0]
        for fc in range(32):
            p = fc % 2
            YT = self.wst[:, p * 4096:p * 4096 + 1024]
            XS = self.wst[:, p * 4096 + 1024:p * 4096 + 2048]
            T1 = self.wst[:, p * 4096 + 2048:p * 4096 + 3072]
            E.wait("sync", "vec", v_done[p])
            a1 = E.dma("sync", YT[:, :], d["d_y"][0:1024, fc * 128:(fc + 1) * 128],
                       sem=f"g{p}", transpose=True)
            a2 = E.dma("sync", XS[:, :], d["d_xbc"][fc * 128:(fc + 1) * 128, :],
                       sem=f"g{p}")
            E.wait("vector", f"g{p}", a2)
            E.wait("vector", "act", self.z_done)
            E.wait("vector", "mm", gn_mm[p])
            dc = self.ccols[:, 202 + fc:202 + fc + 1]
            gw = self.ccols[:, 32 + fc:32 + fc + 1]
            vpf = VP[:, fc * 1024:(fc + 1) * 1024]
            E.op("vector", lambda e, XS=XS, YT=YT, T1=T1, dc=dc:
                 e.scalar_tensor_tensor(T1[:, :], XS[:, :], dc, YT[:, :],
                                        ALU.mult, ALU.add), "vec")
            E.op("vector", lambda e, T1=T1, vpf=vpf, fc=fc: e.tensor_tensor(
                vpf, T1[:, :], SZ[:, fc * 1024:(fc + 1) * 1024], ALU.mult), "vec")
            E.op("vector", lambda e, vpf=vpf, T1=T1: e.tensor_tensor(
                T1[:, :], vpf, vpf, ALU.mult), "vec")
            vsq = E.now("vec")
            E.op("vector", lambda e, vpf=vpf, gw=gw: e.tensor_scalar_mul(
                vpf, vpf, gw), "vec")
            E.wait("tensor", "vec", vsq)
            for si, (o, sz) in enumerate(_splits(1024)):
                E.op("tensor", lambda e, si=si, o=o, sz=sz, T1=T1, fc=fc: e.matmul(
                    self.ps[4 + si][0:1, 0:sz], self.consts[:, 448:449],
                    T1[:, o:o + sz], start=(fc == 0), stop=(fc == 31)),
                    "mm" if si == 1 else None)
            gn_mm[p] = E.now("mm")
            v_done[p] = E.now("vec")
        mmgn = E.now("mm")
        E.wait("scalar", "mm", mmgn)
        for si in range(2):
            E.op("scalar", lambda e, si=si: e.activation(
                self.rows[0:1, si * 512:(si + 1) * 512],
                self.ps[4 + si][0:1, 0:512], AF.Ln, scale=1.0 / D_INNER, bias=EPS),
                "act")
            self.edone("scalar", 4 + si)
            E.op("scalar", lambda e, si=si: e.activation(
                self.rows[0:1, si * 512:(si + 1) * 512],
                self.rows[0:1, si * 512:(si + 1) * 512], AF.Exp, scale=-0.5), "act")
        agn = E.now("act")
        E.wait("tensor", "act", agn)
        self.acq(6)
        for i in range(8):
            E.op("tensor", lambda e, i=i: e.matmul(
                self.ps[6][:, i:i + 1], self.rows[0:1, i * 128:(i + 1) * 128],
                self.eye_f[0:1, 0:1], start=True, stop=True),
                "mm" if i == 7 else None)
        mmv = E.now("mm")
        E.wait("vector", "mm", mmv)
        E.op("vector", lambda e: e.tensor_copy(self.gcol[:, :], self.ps[6][:, 0:8]),
             "vec")
        self.edone("vector", 6)
        self.gn_done = E.now("vec")

    def ph_combine(self):
        E, d = self.E, self.d
        SIN = self.bv(40960, 4096)
        SL = [self.bv(45056, 4096), self.bv(49152, 4096)]
        self.wjrow = self.bv(53248, 8192, F32)
        E.wait("gpsimd", "cc", E.now("cc"))
        all_ = E.dma("gpsimd", self.ldece[0:8, :], d["cc_l_out"][:, :], sem="lde")
        sl_al = [None] * 8
        sl_lastread = [0, 0]
        self.acq(6)
        E.wait("tensor", "lde", all_)
        E.op("tensor", lambda e: e.matmul(self.ps[6][0:64, 0:8], self.ldece[:, :],
                                          self.msel9[:, :], start=True, stop=True),
             "mm")
        mmv = E.now("mm")
        E.wait("scalar", "mm", mmv)
        E.op("scalar", lambda e: e.activation(self.wjT[:, :], self.ps[6][0:64, 0:8],
                                              AF.Exp), "act")
        self.edone("scalar", 6)
        awj = E.now("act")
        E.op("vector", lambda e: e.memset(SIN[:, :], 0.0), "vec")
        E.wait("tensor", "act", awj)
        for j in range(2):
            E.wait("gpsimd", "vec", sl_lastread[j % 2])
            sl_al[j] = E.dma("gpsimd", SL[j % 2][:, :],
                             d["cc_s_out"][j * 128:(j + 1) * 128, :], sem=f"sl{j % 2}")
        for j in range(8):
            self.acq(6)
            E.op("tensor", lambda e, j=j: e.transpose(
                self.ps[6][0:1, 0:64], self.wjT[:, j:j + 1],
                self.eye_f[0:64, 0:64]), "mm")
            mmv = E.now("mm")
            E.wait("vector", "mm", mmv)
            E.op("vector", lambda e: e.tensor_copy(self.rows[0:1, 0:64],
                                                   self.ps[6][0:1, 0:64]), "vec")
            self.edone("vector", 6)
            E.op("vector", lambda e: e.tensor_copy(
                self.wjrow[0:1, :].rearrange("p (a b) -> p a b", b=64),
                self.rows[0:1, 0:64].unsqueeze(2).broadcast_to([1, 64, 64])), "vec")
            vrow = E.now("vec")
            for g in range(8):
                s = g % 2
                self.acq(s)
                E.wait("tensor", "vec", vrow)
                E.op("tensor", lambda e, g=g, s=s: e.matmul(
                    self.ps[s][:, 0:512], self.onesf[:, :],
                    self.wjrow[0:1, g * 512:(g + 1) * 512], start=True, stop=True),
                    "mm")
                mmv = E.now("mm")
                E.wait("vector", "mm", mmv)
                E.wait("vector", f"sl{j % 2}", sl_al[j])
                E.op("vector", lambda e, g=g, s=s, j=j: e.tensor_tensor(
                    self.smallf[:, 0:512], SL[j % 2][:, g * 512:(g + 1) * 512],
                    self.ps[s][:, 0:512], ALU.mult), "vec")
                self.edone("vector", s)
                E.op("vector", lambda e, g=g: e.tensor_tensor(
                    SIN[:, g * 512:(g + 1) * 512], SIN[:, g * 512:(g + 1) * 512],
                    self.smallf[:, 0:512], ALU.add), "vec")
            sl_lastread[j % 2] = E.now("vec")
            if j + 2 < 8:
                E.wait("gpsimd", "vec", sl_lastread[(j + 2) % 2])
                sl_al[j + 2] = E.dma("gpsimd", SL[(j + 2) % 2][:, :],
                                     d["cc_s_out"][(j + 2) * 128:(j + 3) * 128, :],
                                     sem=f"sl{j % 2}")
        self.sin_done = E.now("vec")

    def ph_correction(self):
        E, d = self.E, self.d
        SIN = self.bv(40960, 4096)
        CTI = self.bv(32768, 2048, F32)
        CFM = self.bv(35840, 1024)
        YSB = self.bv(36864, 4096)
        v_prev = E.now("vec")
        for c in range(NCH):
            E.wait("sync", "vec", v_prev)
            E.wait("sync", "ys", E.now("ys"))
            aly = E.dma("sync", YSB[:, :], d["d_y"][c * 128:(c + 1) * 128, :])
            self.acq(6)
            E.op("tensor", lambda e, c=c: e.matmul(
                self.ps[6][:, 0:64], self.onesf[:, :],
                self.tots[0:1, 512 + c * 64:512 + (c + 1) * 64],
                start=True, stop=True), "mm")
            mmv = E.now("mm")
            E.wait("vector", "mm", mmv)
            E.op("vector", lambda e, c=c: e.tensor_tensor(
                self.smallf[:, 1024 + 0:1024 + 64], self.ps[6][:, 0:64],
                CTI[:, c * 128:c * 128 + 64], ALU.add), "vec")
            self.edone("vector", 6)
            E.wait("scalar", "vec", E.now("vec"))
            E.wait("scalar", "vec", v_prev)
            E.op("scalar", lambda e: e.activation(
                self.wtile[:, :], self.smallf[:, 1024:1088], AF.Exp), "act")
            acw = E.now("act")
            for g in range(8):
                s = g % 2
                self.acq(s)
                E.wait("tensor", "vec", self.sin_done)
                for k in range(8):
                    E.op("tensor", lambda e, g=g, k=k, s=s, c=c: e.matmul(
                        self.ps[s][:, k * 64:(k + 1) * 64],
                        CFM[:, c * 128:(c + 1) * 128],
                        SIN[:, g * 512 + k * 64:g * 512 + (k + 1) * 64],
                        start=True, stop=True), "mm" if k == 7 else None)
                mmv = E.now("mm")
                E.wait("vector", "mm", mmv)
                E.wait("vector", "act", acw)
                E.wait("vector", "al", aly)
                E.op("vector", lambda e, g=g, s=s: e.tensor_tensor(
                    self.smallf[:, 0:512].rearrange("p (a b) -> p a b", b=64),
                    self.ps[s][:, 0:512].rearrange("p (a b) -> p a b", b=64),
                    self.wtile[:, g * 8:(g + 1) * 8].unsqueeze(2)
                        .broadcast_to([128, 8, 64]), ALU.mult), "vec")
                self.edone("vector", s)
                E.op("vector", lambda e, g=g: e.tensor_tensor(
                    YSB[:, g * 512:(g + 1) * 512], YSB[:, g * 512:(g + 1) * 512],
                    self.smallf[:, 0:512], ALU.add), "vec")
            v_prev = E.now("vec")
            E.wait("sync", "vec", v_prev)
            E.dma("sync", d["d_y"][c * 128:(c + 1) * 128, :], YSB[:, :], sem="ys")
        self.corr_st = E.now("ys")

    def ph_outproj(self):
        E, d = self.E, self.d
        VP = self.bv(32768, 32768)
        vp_v = E.now("vec")
        self._phv = vp_v
        pan_mm = {0: E.now("mm"), 1: E.now("mm")}
        OS = self.wst[:, 0:2048]
        ost_st = {0: 0, 1: 0}
        for dc in range(KD):
            p = dc % 2
            pan = self.act[:, p * 4096:p * 4096 + 4096]
            wl = self.wload(pan.rearrange("p (k q) -> p k q", q=128),
                            d["w_op"][dc].transpose([1, 0, 2]), pan_mm[p],
                            sem=f"w{p}")
            for si, (o, sz) in enumerate(_splits(1024)):
                pairs = [(pan[:, kc * 128:kc * 128 + 128],
                          VP[:, kc * 1024 + o:kc * 1024 + o + sz])
                         for kc in range(D_INNER // 128)]
                s, mmv = self.job(pairs, lambda s, sz=sz: self.ps[s][:, 0:sz],
                                  pre=[wl, ("vec", vp_v)])
                pan_mm[p] = mmv
                ost = OS[:, (si % 2) * 1024:(si % 2) * 1024 + sz]
                E.wait("vector", "mm", mmv)
                E.wait("vector", f"fs{si % 2}", ost_st[si % 2])
                E.op("vector", lambda e, s=s, sz=sz, ost=ost: e.tensor_copy(
                    ost, self.ps[s][:, 0:sz]), "vec")
                self.edone("vector", s)
                E.wait("sync", "vec", E.now("vec"))
                E.dma("sync", d["d_mo"][dc * 128:dc * 128 + 128, o:o + sz], ost,
                      sem=f"fs{si % 2}")
                ost_st[si % 2] = E.now(f"fs{si % 2}")
        self.mo_st = None

    def ph_h(self):
        E, d = self.E, self.d
        SA = self.bv(49152, 4096, F32)
        SB = self.bv(53248, 4096, F32)
        B1 = self.bv(61440, 2048)
        self.wait_stores("sync")
        E.wait("sync", "vec", self.gn_done)
        for i, (o, sz) in enumerate(T8):
            E.wait("sync", "act", E.now("act"))
            E.wait("sync", "ts", E.now("ts"))
            av = E.dma("sync", SA[0:sz, :], d["d_xa"][HALO + o:HALO + o + sz, :])
            avT = E.dma("sync", B1[0:128, :], d["d_mo"][:, o:o + 128], transpose=True)
            ti = i + 1
            m = self.cols[:, 4 * ti:4 * ti + 1]
            r = self.cols[:, 4 * ti + 2:4 * ti + 3]
            E.wait("vector", "al", max(av, avT))
            E.wait("vector", "ts", E.now("ts"))
            E.op("vector", lambda e, sz=sz, m=m, r=r: e.tensor_scalar(
                SB[0:sz, :], SA[0:sz, :], m[0:sz, :], r[0:sz, :],
                ALU.subtract, ALU.mult), "vec")
            E.op("vector", lambda e, sz=sz, i=i: e.scalar_tensor_tensor(
                SB[0:sz, :], B1[0:sz, :], self.gcol[:, i:i + 1], SB[0:sz, :],
                ALU.mult, ALU.add), "vec")
            E.wait("sync", "vec", E.now("vec"))
            E.dma("sync", d["d_h"][o:o + sz, :], SB[0:sz, :], sem="ts")
            q = self.cols[:, 36 + 2 * i:36 + 2 * i + 1]
            rs = self.cols[:, 36 + 2 * i + 1:36 + 2 * i + 2]
            E.wait("scalar", "vec", E.now("vec"))
            E.op("scalar", lambda e, sz=sz, q=q: e.activation(
                SA[0:sz, :], SB[0:sz, :], AF.Square, accum_out=q[0:sz, :]), "act")
            E.op("scalar", lambda e, sz=sz, q=q, rs=rs: e.activation(
                rs[0:sz, :], q[0:sz, :], AF.Ln, scale=1.0 / D_MODEL, bias=EPS), "act")
            E.op("scalar", lambda e, sz=sz, rs=rs: e.activation(
                rs[0:sz, :], rs[0:sz, :], AF.Exp, scale=-0.5), "act")
            E.wait("vector", "act", E.now("act"))
            E.op("vector", lambda e, sz=sz, rs=rs: e.tensor_scalar_mul(
                B1[0:sz, :], SB[0:sz, :], rs[0:sz, :]), "vec")
            E.wait("sync", "vec", E.now("vec"))
            E.dma("sync", d["d_h2n"][o:o + sz, :], B1[0:sz, :], sem="ts")

    def ph_swiglu(self, fills):
        E, d = self.E, self.d
        al_f, vec_f = fills
        pre_f = [("al", al_f), ("vec", vec_f)]
        GU = self.bv(0, 65536)
        sil_rd = {}
        self._phv = E.now("vec")
        pan_mm = {0: E.now("mm"), 1: E.now("mm"), 2: E.now("mm"), 3: E.now("mm")}
        for hc in range(D_FF // 128):
            p = hc % 2
            gp_ = self.wst[:, p * 4096:p * 4096 + 2048]
            up_ = self.wst[:, p * 4096 + 2048:p * 4096 + 4096]
            wlg = self.wload(gp_.rearrange("p (k q) -> p k q", q=128),
                             d["w_gate"][hc].transpose([1, 0, 2]), pan_mm[2 * p],
                             sem=f"w{2 * p}")
            wlu = self.wload(up_.rearrange("p (k q) -> p k q", q=128),
                             d["w_up"][hc].transpose([1, 0, 2]), pan_mm[2 * p + 1],
                             sem=f"w{2 * p + 1}")
            for si, (o, sz) in enumerate(_splits(1024)):
                gpairs = [(gp_[:, kc * 128:kc * 128 + 128],
                           self.act[:, kc * NT + o:kc * NT + o + sz])
                          for kc in range(KD)]
                upairs = [(up_[:, kc * 128:kc * 128 + 128],
                           self.act[:, kc * NT + o:kc * NT + o + sz])
                          for kc in range(KD)]
                sA, mmA = self.job(gpairs, lambda s, sz=sz: self.ps[s][:, 0:sz],
                                   pre=pre_f + [wlg])
                sB, mmB = self.job(upairs, lambda s, sz=sz: self.ps[s][:, 0:sz],
                                   pre=[wlu])
                pan_mm[2 * p] = mmA
                pan_mm[2 * p + 1] = mmB
                sl = self.silub[:, o:o + sz]
                E.wait("scalar", "mm", mmA)
                E.wait("scalar", "vec", sil_rd.get(o, 0))
                E.op("scalar", lambda e, sA=sA, sz=sz, sl=sl: e.activation(
                    sl, self.ps[sA][:, 0:sz], AF.Silu), "act")
                self.edone("scalar", sA)
                E.wait("vector", "mm", mmB)
                E.wait("vector", "act", E.now("act"))
                E.op("vector", lambda e, sB=sB, sz=sz, sl=sl, hc=hc, o=o:
                     e.tensor_tensor(GU[:, hc * 1024 + o:hc * 1024 + o + sz],
                                     self.ps[sB][:, 0:sz], sl, ALU.mult), "vec")
                self.edone("vector", sB)
                sil_rd[o] = E.now("vec")
        vgu = E.now("vec")
        self._phv = vgu
        pan_mm2 = {0: E.now("mm"), 1: E.now("mm")}
        OS = self.wst[:, 0:2048]
        ost_st = {0: 0, 1: 0}
        for dc in range(KD):
            p = dc % 2
            pan = self.act[:, p * 8192:p * 8192 + 8192]
            wld = self.wload(pan.rearrange("p (k q) -> p k q", q=128),
                             d["w_down"][dc].transpose([1, 0, 2]), pan_mm2[p],
                             sem=f"w{p}")
            for si, (o, sz) in enumerate(_splits(1024)):
                pairs = [(pan[:, hc * 128:hc * 128 + 128],
                          GU[:, hc * 1024 + o:hc * 1024 + o + sz])
                         for hc in range(D_FF // 128)]
                s, mmv = self.job(pairs, lambda s, sz=sz: self.ps[s][:, 0:sz],
                                  pre=[wld, ("vec", vgu)])
                pan_mm2[p] = mmv
                ost = OS[:, (si % 2) * 1024:(si % 2) * 1024 + sz]
                E.wait("vector", "mm", mmv)
                E.wait("vector", f"fs{si % 2}", ost_st[si % 2])
                E.op("vector", lambda e, s=s, sz=sz, ost=ost: e.tensor_copy(
                    ost, self.ps[s][:, 0:sz]), "vec")
                self.edone("vector", s)
                E.wait("sync", "vec", E.now("vec"))
                E.dma("sync", d["d_sw"][dc * 128:dc * 128 + 128, o:o + sz], ost,
                      sem=f"fs{si % 2}")
                ost_st[si % 2] = E.now(f"fs{si % 2}")

    def ph_xb(self):
        E, d = self.E, self.d
        SA = self.bv(49152, 4096, F32)
        SB = self.bv(53248, 4096, F32)
        SC = self.bv(57344 + 2048, 4096, F32)
        SC = self.bv(45056, 4096, F32)
        B1 = self.bv(61440, 2048)
        self.wait_stores("sync")
        for i, (o, sz) in enumerate(T8):
            E.wait("sync", "act", E.now("act"))
            E.wait("sync", "vec", E.now("vec"))
            E.wait("sync", "ts", E.now("ts"))
            av = E.dma("sync", SA[0:sz, :], d["d_xa"][HALO + o:HALO + o + sz, :])
            av2 = E.dma("sync", SC[0:sz, :], d["d_h"][o:o + sz, :])
            avT = E.dma("sync", B1[0:128, :], d["d_sw"][:, o:o + 128], transpose=True)
            E.wait("vector", "al", avT)
            E.op("vector", lambda e, sz=sz: e.tensor_tensor(
                SA[0:sz, :], SA[0:sz, :], SC[0:sz, :], ALU.add), "vec")
            E.op("vector", lambda e, sz=sz: e.tensor_tensor(
                SA[0:sz, :], SA[0:sz, :], B1[0:sz, :], ALU.add), "vec")
            E.wait("sync", "vec", E.now("vec"))
            E.dma("sync", d["d_xb"][o:o + sz, :], SA[0:sz, :], sem="ts")
            m, q, r, v = self.ln_stats(SA, sz, 52 + 4 * i)
            E.wait("vector", "act", E.now("act"))
            E.wait("vector", "ts", E.now("ts"))
            E.op("vector", lambda e, sz=sz, m=m, r=r: e.tensor_scalar(
                B1[0:sz, :], SA[0:sz, :], m[0:sz, :], r[0:sz, :],
                ALU.subtract, ALU.mult), "vec")
            E.wait("sync", "vec", E.now("vec"))
            E.dma("sync", d["d_xn3"][o:o + sz, :], B1[0:sz, :], sem="ts")

    def ph_final(self):
        E, d = self.E, self.d
        SA = self.bv(49152, 4096, F32)
        B1 = self.bv(61440, 2048)
        self.wait_stores("sync")
        for i, (o, sz) in enumerate(T8):
            E.wait("sync", "vec", E.now("vec"))
            E.wait("sync", "ts", E.now("ts"))
            av = E.dma("sync", SA[0:sz, :], d["d_xb"][o:o + sz, :])
            avT = E.dma("sync", B1[0:128, :], d["d_f2"][:, o:o + 128], transpose=True)
            E.wait("vector", "al", avT)
            E.op("vector", lambda e, sz=sz: e.tensor_tensor(
                SA[0:sz, :], SA[0:sz, :], B1[0:sz, :], ALU.add), "vec")
            E.wait("sync", "vec", E.now("vec"))
            E.dma("sync", self.out_t[o:o + sz, :], SA[0:sz, :], sem="ts")
        E.wait("sync", "ts", E.now("ts"))

    def ph_debug(self):
        E = self.E
        self.wait_stores("sync")
        E.wait("sync", "vec", E.now("vec"))
        E.wait("sync", "act", E.now("act"))
        E.wait("sync", "mm", E.now("mm"))
        for name, t in self.dbg_t.items():
            s = self.d[name]
            E.dma("sync", t[:, :], s[:, :], sem="ts")
        E.wait("sync", "ts", E.now("ts"))


_DEBUG = ()
_STAGES = 99


def host_shared(inputs):
    def tile_lhsT(w):
        K, M = w.shape
        return np.ascontiguousarray(
            w.reshape(K // 128, 128, M // 128, 128).transpose(2, 0, 1, 3))

    ip = np.asarray(inputs["L0_attn_in_proj_w"])
    cw = np.asarray(inputs["L0_attn_conv_w"])[:, 0, :]
    diag = np.zeros((NFC, K_CONV, 128, 128), np.float32)
    idx = np.arange(128)
    diag[:, :, idx, idx] = cw.reshape(NFC, 128, K_CONV).transpose(0, 2, 1)

    def colmat(v):
        v = np.asarray(v, np.float32)
        return np.ascontiguousarray(v.reshape(-1, 128).T)

    ccols = np.zeros((128, 256), np.float32)
    ccols[:, 0:16] = colmat(inputs["L0_attn_ln1_w"])
    ccols[:, 16:32] = colmat(inputs["L0_attn_ln2_w"])
    ccols[:, 32:64] = colmat(inputs["L0_attn_gnorm_w"])
    ccols[:, 64:108] = colmat(inputs["L0_ffn_s_gate"])
    ccols[:, 108:152] = colmat(inputs["L0_ffn_s_up"])
    ccols[:, 152:168] = colmat(inputs["L0_ffn_s_down"])
    ccols[:, 168:202] = colmat(inputs["L0_attn_conv_b"])
    ccols[:, 202:234] = colmat(np.repeat(np.asarray(inputs["L0_attn_D"], np.float32),
                                         D_HEAD))
    ccols[:, 250] = EPS

    dcol = np.zeros((64, 2), np.float32)
    dcol[:, 0] = np.asarray(inputs["L0_attn_dt_bias"], np.float32)
    dcol[:, 1] = np.asarray(inputs["L0_attn_A_log"], np.float32)

    consts = np.zeros((128, 512), np.float32)
    consts[0, 0:128] = 1.0
    consts[0, 128:256] = -128.0
    consts[:, 256:384] = np.triu(np.ones((128, 128), np.float32))
    consts[:, 384:512] = 0.0
    consts[:, 448] = 1.0

    onehot = np.zeros((64, 64, 128), np.float32)
    onehot[np.arange(64), np.arange(64), :] = 1.0

    sh = {
        "w_xbc": tile_lhsT(np.ascontiguousarray(ip[:, D_INNER:D_INNER + CONV_DIM])),
        "w_dt": np.ascontiguousarray(ip[:, -N_HEADS:].reshape(KD, 128, N_HEADS)),
        "w_z": tile_lhsT(np.ascontiguousarray(ip[:, :D_INNER])),
        "w_gate": tile_lhsT(np.asarray(inputs["L0_attn_gate_w"])),
        "w_up": tile_lhsT(np.asarray(inputs["L0_attn_up_w"])),
        "w_down": tile_lhsT(np.asarray(inputs["L0_attn_down_w"])),
        "w_op": tile_lhsT(np.asarray(inputs["L0_attn_out_proj_w"])),
        "wq_g": tile_lhsT(np.ascontiguousarray(np.asarray(inputs["L0_ffn_wq_gate"]).T)),
        "wq_u": tile_lhsT(np.ascontiguousarray(np.asarray(inputs["L0_ffn_wq_up"]).T)),
        "wq_d": tile_lhsT(np.ascontiguousarray(np.asarray(inputs["L0_ffn_wq_down"]).T)),
        "w_conv": diag,
        "ccols_in": ccols,
        "dcol_in": dcol,
        "onehot": np.ascontiguousarray(onehot.reshape(64, 64 * 128)),
        "ones_f": np.ones((1, 128), np.float32),
        "consts_in": consts.astype(BF),
        "eye_f_in": np.eye(128, dtype=np.float32),
    }
    return sh


def host_core(inputs, core_id):
    x = np.asarray(inputs["L0_attn_hidden_states"])
    b, seg = divmod(core_id, 4)
    t0 = seg * 1024
    halo = (np.zeros((HALO, D_MODEL), np.float32) if seg == 0
            else np.ascontiguousarray(np.asarray(x[b, t0 - HALO:t0], np.float32)))
    xin = np.concatenate([halo, np.asarray(x[b, t0:t0 + 1024], np.float32)], axis=0)

    msel = np.zeros((9, 8), np.float32)
    msel[8, :] = NEG
    for j in range(8):
        jb, jseg = divmod(j, 4)
        if jb == b and jseg < seg:
            msel[8, j] = 0.0
            for i in range(8):
                ib, iseg = divmod(i, 4)
                if ib == b and jseg < iseg < seg:
                    msel[i, j] = 1.0
    return {"x_in": np.ascontiguousarray(xin), "msel_in": msel}


def kernel(**inputs):
    sh = host_shared(inputs)
    in_maps = []
    for c in range(8):
        m = dict(sh)
        m.update(host_core(inputs, c))
        in_maps.append(m)
    b = Build(debug=_DEBUG, stages=_STAGES)
    nc = b.build()
    res = run_bass_kernel_spmd(nc, in_maps, core_ids=list(range(8)))
    out = np.zeros((2, 4096, D_MODEL), np.float32)
    for c in range(8):
        bb, seg = divmod(c, 4)
        out[bb, seg * 1024:(seg + 1) * 1024] = res.results[c]["out"]
    if _DEBUG:
        kernel.dbg = [{k: v for k, v in r.items() if k.startswith("dbg_")}
                      for r in res.results]
    return out


# version 11
# speedup vs baseline: 1.0303x; 1.0303x over previous
import numpy as np
import ml_dtypes
import concourse.bass as bass
import concourse.mybir as mybir
from concourse.bass_utils import run_bass_kernel_spmd
from contextlib import ExitStack

F32 = mybir.dt.float32
BF16 = mybir.dt.bfloat16
I32 = mybir.dt.int32
AF = mybir.ActivationFunctionType
ALU = mybir.AluOpType
AX = mybir.AxisListType.X
BF = ml_dtypes.bfloat16

D_MODEL = 2048
D_INNER = 4096
N_HEADS = 64
D_HEAD = 64
D_STATE = 128
K_CONV = 4
D_FF = 8192
HID = 5632
CONV_DIM = D_INNER + 2 * D_STATE
NFC = CONV_DIM // 128
EPS = 1e-5
HALO = 16
NT = 1024 + HALO
Q = 128
NCH = 8
KD = D_MODEL // 128
NEG = -30000.0

T9 = [(0, HALO)] + [(HALO + 128 * i, 128) for i in range(8)]
T8 = [(128 * i, 128) for i in range(8)]


def _splits(n):
    k = -(-n // 512)
    base, rem = divmod(n, k)
    out, o = [], 0
    for i in range(k):
        s = base + (1 if i < rem else 0)
        out.append((o, s))
        o += s
    return out


class Emit:
    ENGS = ("sync", "scalar", "vector", "tensor", "gpsimd")
    CHAIN = {"vector": "vec", "scalar": "act"}

    def __init__(self):
        self.q = {e: [] for e in self.ENGS}
        self.cnt = {}

    def op(self, eng, fn, sem=None, amt=1, chain=True):
        ch = self.CHAIN.get(eng)
        if ch is not None and chain:
            self.wait(eng, ch, self.cnt.get(ch, 0))
        if sem is not None:
            self.cnt[sem] = self.cnt.get(sem, 0) + amt
            self.q[eng].append(("op", fn, sem, amt))
            return self.cnt[sem]
        self.q[eng].append(("op", fn, None, 0))
        return None

    def dma(self, eng, out, in_, sem="al", **kw):
        return self.op(eng, (lambda e, o=out, i=in_, k=kw: e.dma_start(out=o, in_=i, **k)),
                       sem, 16, chain=False)

    def wait(self, eng, sem, val):
        if val:
            self.cnt.setdefault(sem, 0)
            self.q[eng].append(("wait", sem, val))

    def now(self, sem):
        return self.cnt.get(sem, 0)

    def replay(self, eng_obj, eng_name, sems):
        for item in self.q[eng_name]:
            if item[0] == "op":
                _, fn, sem, amt = item
                ins = fn(eng_obj)
                if sem is not None:
                    ins.then_inc(sems[sem], amt)
            else:
                _, sem, val = item
                eng_obj.wait_ge(sems[sem], val)


class Build:
    def __init__(self, debug=(), stages=99):
        self.nc = bass.Bass("TRN2", target_bir_lowering=False)
        self.E = Emit()
        self.stages = stages
        self.debug = list(debug)
        self.slot_use = [0] * 8
        self.rot_i = 0
        self.mk_dram()

    def mk_dram(self):
        nc = self.nc
        self.d = {}
        def din(name, shape, dt):
            self.d[name] = nc.dram_tensor(name, list(shape), dt, kind="ExternalInput")
        def dsc(name, shape, dt):
            self.d[name] = nc.dram_tensor(name, list(shape), dt)

        din("x_in", [NT, D_MODEL], F32)
        din("w_xbc", [NFC, KD, 128, 128], F32)
        din("w_dt", [KD, 128, N_HEADS], F32)
        din("w_z", [D_INNER // 128, KD, 128, 128], F32)
        din("w_gate", [D_FF // 128, KD, 128, 128], F32)
        din("w_up", [D_FF // 128, KD, 128, 128], F32)
        din("w_down", [KD, D_FF // 128, 128, 128], F32)
        din("w_op", [KD, D_INNER // 128, 128, 128], F32)
        din("wq_g", [HID // 128, KD, 128, 128], I32)
        din("wq_u", [HID // 128, KD, 128, 128], I32)
        din("wq_d", [KD, HID // 128, 128, 128], I32)
        din("w_conv", [NFC, K_CONV, 128, 128], F32)
        din("ccols_in", [128, 256], F32)
        din("dcol_in", [64, 2], F32)
        din("msel_in", [9, 8], F32)
        din("onehot", [64, 64 * 128], F32)
        din("ones_f", [1, 128], F32)
        din("consts_in", [128, 512], BF16)
        din("eye_f_in", [128, 128], F32)

        self.out_t = nc.dram_tensor("out", [1024, D_MODEL], F32, kind="ExternalOutput")

        dsc("d_xa", [NT, D_MODEL], F32)
        dsc("d_xn1", [NT, D_MODEL], BF16)
        dsc("d_t1", [NT, D_MODEL], BF16)
        dsc("d_f1", [D_MODEL, 1152], BF16)
        dsc("d_xbc", [CONV_DIM, 1024], BF16)
        dsc("d_y", [1024, D_INNER], BF16)
        dsc("d_mo", [D_MODEL, 1024], BF16)
        dsc("d_h", [1024, D_MODEL], F32)
        dsc("d_h2n", [1024, D_MODEL], BF16)
        dsc("d_sw", [D_MODEL, 1024], BF16)
        dsc("d_xb", [1024, D_MODEL], F32)
        dsc("d_xn3", [1024, D_MODEL], BF16)
        dsc("d_f2", [D_MODEL, 1024], BF16)
        dsc("cc_s_in", [128, D_INNER], BF16)
        dsc("cc_s_out", [8 * 128, D_INNER], BF16)
        dsc("cc_l_in", [1, 64], F32)
        dsc("cc_l_out", [8, 64], F32)
        dsc("d_probe", [128, 2048], BF16)

        self.dbg_t = {}
        for name in self.debug:
            src = self.d[name]
            self.dbg_t[name] = nc.dram_tensor(
                "dbg_" + name, list(src.shape), src.dtype, kind="ExternalOutput")

    def build(self):
        nc, E = self.nc, self.E
        ctx = ExitStack()
        def sb(name, shape, dt):
            return ctx.enter_context(nc.sbuf_tensor(name, list(shape), dt))
        with ctx:
            self.act = sb("act_fm", [128, KD * NT], BF16)
            self.ba = sb("barena", [128, 65536], BF16)
            self.wst = sb("wst", [128, 8192], BF16)
            self.silub = sb("silub", [128, 1040], BF16)
            self.consts = sb("consts", [128, 512], BF16)
            self.eye_f = sb("eye_f", [128, 128], F32)
            self.onesf = sb("onesf", [1, 128], F32)
            self.ccols = sb("ccols", [128, 256], F32)
            self.cols = sb("cols", [128, 96], F32)
            self.rows = sb("rows", [1, 1024], F32)
            self.rowsb = sb("rowsb", [1, 2 * NT], BF16)
            self.dcol = sb("dcol", [64, 8], F32)
            self.tots = sb("tots", [1, 1152], F32)
            self.smallf = sb("smallf", [128, 1088], F32)
            self.wtile = sb("wtile", [128, 64], F32)
            self.ebcT = sb("ebcT", [128, 64], F32)
            self.gcol = sb("gcol", [128, 8], F32)
            self.ldece = sb("ldece", [9, 64], F32)
            self.msel9 = sb("msel9", [9, 8], F32)
            self.bfm = sb("bfm", [128, 128], BF16)
            self.btm = sb("btm", [128, 128], BF16)
            self.bcm = sb("bcm", [128, 128], BF16)
            self.wjT = sb("wjT", [64, 8], F32)
            self.zbuf = sb("zbuf", [64, 128], F32)
            self.ps = [ctx.enter_context(nc.psum_tensor(f"ps{i}", [128, 512], F32))
                       for i in range(8)]
            nc.const_aps.register(F32, EPS, self.ccols[:, 250:251])
            sem_names = ["al", "wl", "mm", "vec", "act", "gp", "st", "cc",
                         "w0", "w1", "w2", "w3", "sl0", "sl1", "lde",
                         "g0", "g1", "ts", "fs0", "fs1", "xs0", "xs1", "ys", "oh"] + \
                        [f"e{i}" for i in range(8)]
            sems = {n: ctx.enter_context(nc.semaphore(n)) for n in sem_names}
            block = ctx.enter_context(nc.Block())

            self.emit_all()

            for name in Emit.ENGS:
                def mk(nm):
                    def f(eng):
                        E.replay(eng, nm, sems)
                    return f
                getattr(block, name)(mk(name))
        return nc

    def bv(self, c0, ncols, dt=BF16):
        ap = self.ba[:, c0:c0 + ncols]
        return ap if dt == BF16 else ap.bitcast(dt)

    def rot(self):
        s = self.rot_i % 4
        self.rot_i += 1
        return s

    def acq(self, s):
        self.E.wait("tensor", f"e{s}", self.slot_use[s])

    def edone(self, eng, s):
        self.E.op(eng, lambda e: e.nop(), f"e{s}")
        self.slot_use[s] += 1

    def job(self, pairs, out_fn, pre=()):
        E = self.E
        s = self.rot()
        self.acq(s)
        for sem, val in pre:
            E.wait("tensor", sem, val)
        n = len(pairs)
        for j, (lh, rh) in enumerate(pairs):
            last = (j == n - 1)
            E.op("tensor", lambda e, lh=lh, rh=rh, s=s, st=(j == 0),
                 sp=last, of=out_fn: e.matmul(of(s), lh, rh,
                                              start=st, stop=sp),
                 "mm" if last else None)
        return s, E.now("mm")

    STORE_SEMS = ("ts", "fs0", "fs1", "xs0", "xs1", "ys")

    def wait_stores(self, eng):
        for s in self.STORE_SEMS:
            self.E.wait(eng, s, self.E.now(s))

    def wload(self, dst_ap, src_ap, last_mm, sem="w0"):
        E = self.E
        E.wait("gpsimd", "mm", last_mm)
        E.wait("gpsimd", "vec", getattr(self, "_phv", 0))
        return (sem, E.dma("gpsimd", dst_ap, src_ap, sem=sem))

    def ln_stats(self, buf, sz, sb0, keep_v=False):
        E = self.E
        m = self.cols[:, sb0:sb0 + 1]
        q = self.cols[:, sb0 + 1:sb0 + 2]
        r = self.cols[:, sb0 + 2:sb0 + 3]
        v = self.cols[:, sb0 + 3:sb0 + 4]
        scr = self.bv(57344, 4096, F32)
        E.op("vector", lambda e: e.reduce_sum(m[0:sz, :], buf[0:sz, :], axis=AX), "vec")
        E.op("vector", lambda e: e.tensor_scalar_mul(m[0:sz, :], m[0:sz, :],
                                                     1.0 / D_MODEL), "vec")
        E.wait("scalar", "vec", E.now("vec"))
        E.op("scalar", lambda e: e.activation(
            scr[0:sz, :], buf[0:sz, :], AF.Square, accum_out=q[0:sz, :]), "act")
        E.wait("vector", "act", E.now("act"))
        E.op("vector", lambda e: e.tensor_tensor(
            v[0:sz, :], m[0:sz, :], m[0:sz, :], ALU.mult), "vec")
        E.op("vector", lambda e: e.scalar_tensor_tensor(
            v[0:sz, :], q[0:sz, :], 1.0 / D_MODEL, v[0:sz, :],
            ALU.mult, ALU.subtract), "vec")
        E.wait("scalar", "vec", E.now("vec"))
        E.op("scalar", lambda e: e.activation(r[0:sz, :], v[0:sz, :], AF.Ln,
                                              bias=EPS), "act")
        E.op("scalar", lambda e: e.activation(r[0:sz, :], r[0:sz, :], AF.Exp,
                                              scale=-0.5), "act")
        return m, q, r, v

    def emit_all(self):
        E = self.E
        n = self.stages
        steps = [
            self.ph_consts,
            self.ph_ln1,
            lambda: setattr(self, "_f", self.ph_fill_act(self.d["d_xn1"], NT, None, E.now("st"))),
            lambda: self.ph_ffnq(NT, self.d["d_f1"], self._f),
            self.ph_xa,
            lambda: setattr(self, "_f", self.ph_fill_act(self.d["d_t1"], NT, 0, E.now("st"))),
            lambda: self.ph_inproj(self._f),
            self.ph_scan,
            self.ph_exchange_z,
            self.ph_combine,
            self.ph_correction,
            self.ph_gating,
            self.ph_outproj,
            self.ph_h,
            lambda: setattr(self, "_f", self.ph_fill_act(self.d["d_h2n"], 1024, 16, E.now("st"))),
            lambda: self.ph_swiglu(self._f),
            self.ph_xb,
            lambda: setattr(self, "_f", self.ph_fill_act(self.d["d_xn3"], 1024, None, E.now("st"))),
            lambda: self.ph_ffnq(1024, self.d["d_f2"], self._f),
            self.ph_final,
        ]
        for i, s in enumerate(steps):
            if i >= n:
                break
            s()
        self.ph_debug()
        self.wait_stores("sync")

    def ph_consts(self):
        E, d = self.E, self.d
        E.dma("sync", self.consts[:, :], d["consts_in"][:, :])
        E.dma("sync", self.eye_f[:, :], d["eye_f_in"][:, :])
        E.dma("sync", self.onesf[:, :], d["ones_f"][:, :])
        E.dma("sync", self.ccols[:, :], d["ccols_in"][:, :])
        E.dma("sync", self.dcol[:, 0:2], d["dcol_in"][:, :])
        E.dma("sync", self.msel9[:, :], d["msel_in"][:, :])
        al = E.now("al")
        E.wait("scalar", "al", al)
        E.op("scalar", lambda e: e.activation(self.dcol[:, 2:3], self.dcol[:, 1:2],
                                              AF.Exp), "act")
        E.wait("vector", "act", E.now("act"))
        E.op("vector", lambda e: e.tensor_scalar_mul(self.dcol[:, 3:4],
                                                     self.dcol[:, 2:3], -1.0), "vec")
        E.wait("vector", "al", al)
        E.op("vector", lambda e: e.tensor_scalar_mul(self.ccols[:, 234:250],
                                                     self.ccols[:, 152:168], 0.5), "vec")
        E.op("vector", lambda e: e.memset(self.zbuf[:, :], 0.0), "vec")
        E.dma("sync", self.ldece[8:9, 0:64], self.d["ones_f"][0:1, 0:64], sem="ts")
        self.c_init = (E.now("vec"), E.now("act"))

    def ph_ln1(self):
        E, d = self.E, self.d
        SA = self.bv(49152, 4096, F32)
        SB = self.bv(53248, 4096, F32)
        B1 = self.bv(61440, 2048)
        E.wait("vector", "vec", self.c_init[0])
        for i, (o, sz) in enumerate(T9):
            buf = SA if i % 2 == 0 else SB
            E.wait("sync", "act", E.now("act"))
            E.wait("sync", "vec", E.now("vec"))
            av = E.dma("sync", buf[0:sz, :], d["x_in"][o:o + sz, :])
            E.wait("vector", "al", av)
            m, q, r, v = self.ln_stats(buf, sz, 84)
            E.wait("vector", "act", E.now("act"))
            E.wait("vector", "ts", E.now("ts"))
            E.op("vector", lambda e, b=buf, sz=sz, m=m, r=r: e.tensor_scalar(
                B1[0:sz, :], b[0:sz, :], m[0:sz, :], r[0:sz, :],
                ALU.subtract, ALU.mult), "vec")
            E.wait("sync", "vec", E.now("vec"))
            E.dma("sync", d["d_xn1"][o:o + sz, :], B1[0:sz, :], sem="ts")

    def ph_fill_act(self, src_tm, n_tok, colw, wait_st):
        E = self.E
        self.wait_stores("sync")
        E.wait("sync", "mm", E.now("mm"))
        for kc in range(KD):
            av = E.dma("sync", self.act[:, kc * NT:kc * NT + n_tok],
                       src_tm[0:n_tok, kc * 128:kc * 128 + 128], transpose=True)
        E.wait("vector", "al", av)
        if colw is not None:
            for kc in range(KD):
                E.op("vector", lambda e, k=kc, n=n_tok: e.tensor_scalar_mul(
                    self.act[:, k * NT:k * NT + n], self.act[:, k * NT:k * NT + n],
                    self.ccols[:, colw + k:colw + k + 1]), "vec")
        return (E.now("al"), E.now("vec"))

    def ph_ffnq(self, n_tok, dst_slab, fills):
        E, d = self.E, self.d
        NH = HID // 128
        sp = _splits(n_tok)
        al_f, vec_f = fills
        ones_c = self.consts[:, 448:449]
        neg128 = self.consts[0:1, 128:256]
        pre_f = [("al", al_f), ("vec", vec_f)]

        for (o, sz) in sp:
            pairs = [(ones_c, self.act[:, kc * NT + o:kc * NT + o + sz])
                     for kc in range(KD)]
            s, mmv = self.job(pairs, lambda s, sz=sz: self.ps[s][0:1, 0:sz], pre=pre_f)
            E.wait("vector", "mm", mmv)
            E.op("vector", lambda e, s=s, o=o, sz=sz: e.tensor_copy(
                self.rowsb[0:1, o:o + sz], self.ps[s][0:1, 0:sz]), "vec")
            self.edone("vector", s)
        vrow = E.now("vec")

        GU = self.bv(0, NH * n_tok)
        self._sil_rd = {}
        self._sgu_pend = {}
        self._phv = E.now("vec")
        pan_mm = {0: E.now("mm"), 1: E.now("mm"), 2: E.now("mm"), 3: E.now("mm")}
        for hc in range(NH):
            p = hc % 2
            gp_ = self.wst[:, p * 4096:p * 4096 + 2048]
            up_ = self.wst[:, p * 4096 + 2048:p * 4096 + 4096]
            wlg = self.wload(gp_.rearrange("p (k q) -> p k q", q=128),
                             d["wq_g"][hc].transpose([1, 0, 2]), pan_mm[2 * p],
                             sem=f"w{2 * p}")
            wlu = self.wload(up_.rearrange("p (k q) -> p k q", q=128),
                             d["wq_u"][hc].transpose([1, 0, 2]), pan_mm[2 * p + 1],
                             sem=f"w{2 * p + 1}")
            for si, (o, sz) in enumerate(sp):
                gpairs = [(gp_[:, kc * 128:kc * 128 + 128],
                           self.act[:, kc * NT + o:kc * NT + o + sz])
                          for kc in range(KD)] + \
                         [(neg128, self.rowsb[0:1, o:o + sz])]
                upairs = [(up_[:, kc * 128:kc * 128 + 128],
                           self.act[:, kc * NT + o:kc * NT + o + sz])
                          for kc in range(KD)] + \
                         [(neg128, self.rowsb[0:1, o:o + sz])]
                prew = pre_f + [wlg, ("vec", vrow)]
                sA, mmA = self.job(gpairs, lambda s, sz=sz: self.ps[s][:, 0:sz],
                                   pre=prew)
                sB, mmB = self.job(upairs, lambda s, sz=sz: self.ps[s][:, 0:sz],
                                   pre=[wlu])
                pan_mm[2 * p] = mmA
                pan_mm[2 * p + 1] = mmB
                sgc = self.ccols[:, 64 + hc:64 + hc + 1]
                suc = self.ccols[:, 108 + hc:108 + hc + 1]
                sl = self.silub[:, o:o + sz]
                E.wait("scalar", "mm", mmA)
                E.wait("scalar", "vec", self._sil_rd.get(o, 0))
                E.op("scalar", lambda e, sA=sA, sz=sz, sgc=sgc, sl=sl: e.activation(
                    sl, self.ps[sA][:, 0:sz], AF.Silu, scale=sgc), "act")
                self.edone("scalar", sA)
                av = E.now("act")
                E.wait("vector", "mm", mmB)
                E.wait("vector", "act", av)
                gu_sl = GU[:, hc * n_tok + o:hc * n_tok + o + sz]
                E.op("vector", lambda e, sB=sB, sz=sz, suc=suc, sl=sl, g=gu_sl:
                     e.scalar_tensor_tensor(g, self.ps[sB][:, 0:sz], suc, sl,
                                            ALU.mult, ALU.mult), "vec")
                self._sil_rd[o] = E.now("vec")
                self.edone("vector", sB)
                guv = E.now("vec")
                rs = 4 + si
                if hc == 0:
                    self.acq(rs)
                else:
                    pg, pgu, psz = self._sgu_pend[si]
                    E.wait("tensor", "vec", pg)
                    E.op("tensor", lambda e, rs=rs, g=pgu, sz=psz, st=(hc == 1):
                         e.matmul(self.ps[rs][0:1, 0:sz], ones_c, g,
                                  start=st, stop=False), "mm")
                self._sgu_pend[si] = (guv, gu_sl, sz)
        for si, (o, sz) in enumerate(sp):
            pg, pgu, psz = self._sgu_pend[si]
            E.wait("tensor", "vec", pg)
            E.op("tensor", lambda e, rs=4 + si, g=pgu, sz=psz:
                 e.matmul(self.ps[rs][0:1, 0:sz], ones_c, g,
                          start=False, stop=True), "mm")
        mm_sgu = E.now("mm")
        E.wait("vector", "mm", mm_sgu)
        for si, (o, sz) in enumerate(sp):
            E.op("vector", lambda e, si=si, o=o, sz=sz: e.tensor_copy(
                self.rowsb[0:1, NT + o:NT + o + sz], self.ps[4 + si][0:1, 0:sz]),
                "vec")
            self.edone("vector", 4 + si)
        vgu = E.now("vec")

        self._phv = E.now("vec")
        pan_mm2 = {0: mm_sgu, 1: mm_sgu}
        OS = self.wst[:, 0:2048]
        ost_st = {0: 0, 1: 0}
        for dc in range(KD):
            p = dc % 2
            pan = self.act[:, p * 5632:p * 5632 + 5632]
            wld = self.wload(pan.rearrange("p (k q) -> p k q", q=128),
                             d["wq_d"][dc].transpose([1, 0, 2]), pan_mm2[p],
                             sem=f"w{p}")
            for si, (o, sz) in enumerate(sp):
                pairs = [(pan[:, hc * 128:hc * 128 + 128],
                          GU[:, hc * n_tok + o:hc * n_tok + o + sz])
                         for hc in range(NH)] + \
                        [(neg128, self.rowsb[0:1, NT + o:NT + o + sz])]
                s, mmv = self.job(pairs, lambda s, sz=sz: self.ps[s][:, 0:sz],
                                  pre=[wld, ("vec", vgu)])
                pan_mm2[p] = mmv
                sdc = self.ccols[:, 234 + dc:234 + dc + 1]
                ost = OS[:, (si % 2) * 1024:(si % 2) * 1024 + sz]
                E.wait("vector", "mm", mmv)
                E.wait("vector", f"fs{si % 2}", ost_st[si % 2])
                E.op("vector", lambda e, s=s, sz=sz, sdc=sdc, ost=ost:
                     e.tensor_scalar_mul(ost, self.ps[s][:, 0:sz], sdc), "vec")
                self.edone("vector", s)
                E.wait("sync", "vec", E.now("vec"))
                E.dma("sync", dst_slab[dc * 128:dc * 128 + 128, o:o + sz], ost,
                      sem=f"fs{si % 2}")
                ost_st[si % 2] = E.now(f"fs{si % 2}")

    def ph_xa(self):
        E, d = self.E, self.d
        SA = self.bv(49152, 4096, F32)
        SB = self.bv(53248, 4096, F32)
        B1 = self.bv(61440, 2048)
        B2 = self.bv(63488, 2048)
        self.wait_stores("sync")
        for i, (o, sz) in enumerate(T9):
            buf = SA if i % 2 == 0 else SB
            E.wait("sync", "act", E.now("act"))
            E.wait("sync", "vec", E.now("vec"))
            E.wait("sync", "ts", E.now("ts"))
            av = E.dma("sync", buf[0:sz, :], d["x_in"][o:o + sz, :])
            avT = E.dma("sync", B1[0:128, :], d["d_f1"][:, o:o + 128], transpose=True)
            E.wait("vector", "al", avT)
            E.op("vector", lambda e, b=buf, sz=sz: e.tensor_tensor(
                b[0:sz, :], b[0:sz, :], B1[0:sz, :], ALU.add), "vec")
            E.wait("sync", "vec", E.now("vec"))
            E.dma("sync", d["d_xa"][o:o + sz, :], buf[0:sz, :], sem="ts")
            m, q, r, v = self.ln_stats(buf, sz, 4 * i)
            w = self.smallf[:, 1024 + i:1024 + i + 1]
            E.wait("vector", "act", E.now("act"))
            E.op("vector", lambda e, sz=sz, r=r, w=w: e.tensor_tensor(
                w[0:sz, :], r[0:sz, :], r[0:sz, :], ALU.mult), "vec")
            E.op("vector", lambda e, sz=sz, v=v, w=w: e.tensor_tensor(
                w[0:sz, :], w[0:sz, :], v[0:sz, :], ALU.mult), "vec")
            E.wait("scalar", "vec", E.now("vec"))
            E.op("scalar", lambda e, sz=sz, w=w: e.activation(
                w[0:sz, :], w[0:sz, :], AF.Ln, bias=EPS), "act")
            E.op("scalar", lambda e, sz=sz, w=w: e.activation(
                w[0:sz, :], w[0:sz, :], AF.Exp, scale=-0.5), "act")
            E.wait("vector", "act", E.now("act"))
            E.wait("vector", "ts", E.now("ts"))
            E.op("vector", lambda e, sz=sz, r=r, w=w: e.tensor_tensor(
                w[0:sz, :], w[0:sz, :], r[0:sz, :], ALU.mult), "vec")
            E.op("vector", lambda e, b=buf, sz=sz, m=m, w=w: e.tensor_scalar(
                B2[0:sz, :], b[0:sz, :], m[0:sz, :], w[0:sz, :],
                ALU.subtract, ALU.mult), "vec")
            E.wait("sync", "vec", E.now("vec"))
            E.dma("sync", d["d_t1"][o:o + sz, :], B2[0:sz, :], sem="ts")

    def ph_inproj(self, fills):
        E, d = self.E, self.d
        sp = _splits(NT)
        al_f, vec_f = fills
        pre_f = [("al", al_f), ("vec", vec_f)]
        RAW = [self.bv(0, 1040), self.bv(1040, 1040)]
        self._phv = E.now("vec")
        pan_mm = {0: E.now("mm"), 1: E.now("mm")}
        conv_mm = {0: 0, 1: 0}
        cst_st = {0: 0, 1: 0}
        pend_conv = None
        for fc in range(NFC):
            p = fc % 2
            pan = self.wst[:, p * 2560:p * 2560 + 2048]
            dia = self.wst[:, p * 2560 + 2048:p * 2560 + 2560]
            wlp = self.wload(pan.rearrange("p (k q) -> p k q", q=128),
                             d["w_xbc"][fc].transpose([1, 0, 2]), pan_mm[p],
                             sem=f"w{p}")
            wld = self.wload(dia.rearrange("p (k q) -> p k q", q=128),
                             d["w_conv"][fc].transpose([1, 0, 2]), pan_mm[p],
                             sem=f"w{p}")
            raw = RAW[p]
            rawv = []
            for si, (o, sz) in enumerate(sp):
                pairs = [(pan[:, kc * 128:kc * 128 + 128],
                          self.act[:, kc * NT + o:kc * NT + o + sz])
                         for kc in range(KD)]
                s, mmv = self.job(pairs, lambda s, sz=sz: self.ps[s][:, 0:sz],
                                  pre=pre_f + [wld])
                pan_mm[p] = mmv
                E.wait("vector", "mm", mmv)
                E.wait("vector", "mm", conv_mm[p])
                E.op("vector", lambda e, s=s, o=o, sz=sz, raw=raw: e.tensor_copy(
                    raw[:, o:o + sz], self.ps[s][:, 0:sz]), "vec")
                self.edone("vector", s)
            vr = E.now("vec")

            def emit_conv(vr=None, raw=None, dia=None, wld=None, fc=None, p=None):
                for (o, sz) in _splits(1024):
                    pairs = [(dia[:, j * 128:j * 128 + 128],
                              raw[:, o + 13 + j:o + 13 + j + sz])
                             for j in range(K_CONV)]
                    s, mmv = self.job(pairs, lambda s, sz=sz: self.ps[s][:, 0:sz],
                                      pre=[("vec", vr), wld])
                    pan_mm[p] = mmv
                    conv_mm[p] = mmv
                    cp = 0 if o == 0 else 1
                    cbc = self.ccols[:, 168 + fc:168 + fc + 1]
                    cst = (self.silub[:, 0:sz] if o == 0
                           else self.silub[:, 512:512 + sz])
                    E.wait("scalar", "mm", mmv)
                    E.wait("scalar", f"xs{cp}", cst_st[cp])
                    E.op("scalar", lambda e, s=s, sz=sz, cbc=cbc, cst=cst:
                         e.activation(cst, self.ps[s][:, 0:sz], AF.Silu, bias=cbc),
                         "act")
                    self.edone("scalar", s)
                    E.wait("sync", "act", E.now("act"))
                    E.dma("sync", d["d_xbc"][fc * 128:fc * 128 + 128, o:o + sz],
                          cst, sem=f"xs{cp}")
                    cst_st[cp] = E.now(f"xs{cp}")

            if pend_conv is not None:
                emit_conv(**pend_conv)
            pend_conv = dict(vr=vr, raw=raw, dia=dia, wld=wld, fc=fc, p=p)
        if pend_conv is not None:
            emit_conv(**pend_conv)
        DT = self.bv(40960, 10400, F32)
        wdt = self.wst[:, 5120 + 0:5120 + 1024]
        wl = self.wload(wdt.rearrange("p (k q) -> p k q", q=64),
                        d["w_dt"][:, :, :].transpose([1, 0, 2]), max(pan_mm.values()),
                        sem="w2")
        for si, (o, sz) in enumerate(sp):
            pairs = [(wdt[:, kc * 64:kc * 64 + 64],
                      self.act[:, kc * NT + o:kc * NT + o + sz])
                     for kc in range(KD)]
            s, mmv = self.job(pairs, lambda s, sz=sz: self.ps[s][0:64, 0:sz],
                              pre=pre_f + [wl])
            E.wait("scalar", "mm", mmv)
            E.op("scalar", lambda e, s=s, o=o, sz=sz: e.activation(
                DT[0:64, o:o + sz], self.ps[s][0:64, 0:sz], AF.Exp,
                bias=self.dcol[:, 0:1]), "act")
            self.edone("scalar", s)
            E.op("scalar", lambda e, o=o, sz=sz: e.activation(
                DT[0:64, o:o + sz], DT[0:64, o:o + sz], AF.Ln, bias=1.0), "act")
        E.op("scalar", lambda e: e.activation(DT[0:64, 1040:2080], DT[0:64, 0:1040],
                                              AF.Ln), "act")
        adt = E.now("act")
        E.wait("vector", "act", adt)
        E.op("vector", lambda e: e.tensor_scalar_mul(
            DT[0:64, 2080:3120], DT[0:64, 0:1040], self.dcol[:, 3:4]), "vec")
        self.dt_ready = (E.now("vec"), E.now("act"))

    def ph_scan(self):
        E, d = self.E, self.d
        DT = self.bv(40960, 10400, F32)
        ACU = lambda c: DT[0:64, 3120 + c * 128:3120 + (c + 1) * 128]
        LMA = DT[0:64, 4144:5168]
        OH = self.bv(0, 16384, F32)
        XT = self.bv(16384, 4096)
        XW = self.bv(20480, 4096)
        EXD = self.bv(24576, 1024)
        SCO = self.bv(25600, 1024)
        ARG = self.bv(30720, 2048, F32)
        STATE = self.bv(26624, 4096)
        CTI = self.bv(32768, 2048, F32)
        WX = self.bv(34816, 1024, F32)
        CFM = self.bv(35840, 1024)
        YSB = self.bv(36864, 4096)

        E.wait("sync", "vec", self.dt_ready[0])
        E.wait("sync", "mm", E.now("mm"))
        self.wait_stores("sync")
        oh_al = E.dma("sync", OH[0:64, :], d["onehot"][:, :], sem="oh")
        E.wait("tensor", "oh", oh_al)
        E.wait("vector", "oh", oh_al)
        E.op("vector", lambda e: e.memset(STATE[:, :], 0.0), "vec")
        E.op("vector", lambda e: e.memset(self.tots[:, :], 0.0), "vec")
        for c in range(NCH):
            E.op("vector", lambda e, c=c: e.tensor_tensor_scan(
                ACU(c), DT[0:64, 2080 + 16 + c * 128:2080 + 16 + (c + 1) * 128],
                self.zbuf[:, :], 0.0, ALU.add, ALU.add), "vec")
            E.op("vector", lambda e, c=c: e.reduce_sum(
                self.smallf[0:64, 1024 + c:1024 + c + 1],
                DT[0:64, 2080 + 16 + c * 128:2080 + 16 + (c + 1) * 128],
                axis=AX), "vec")
        vq = E.now("vec")
        E.op("vector", lambda e: e.tensor_tensor(
            LMA, DT[0:64, 1040 + 16:1040 + 16 + 1024], DT[0:64, 3120:4144],
            ALU.subtract), "vec")
        E.wait("tensor", "vec", vq)
        self.acq(6)
        for c in range(NCH):
            E.op("tensor", lambda e, c=c: e.transpose(
                self.ps[6][0:1, c * 64:c * 64 + 64],
                self.smallf[0:64, 1024 + c:1024 + c + 1],
                self.eye_f[0:64, 0:64]), "mm")
        mmv = E.now("mm")
        E.wait("vector", "mm", mmv)
        E.op("vector", lambda e: e.tensor_copy(self.tots[0:1, 0:512],
                                               self.ps[6][0:1, 0:512]), "vec")
        self.edone("vector", 6)
        for c in range(1, NCH):
            E.op("vector", lambda e, c=c: e.tensor_tensor(
                self.tots[0:1, 512 + c * 64:512 + (c + 1) * 64],
                self.tots[0:1, 512 + (c - 1) * 64:512 + c * 64],
                self.tots[0:1, (c - 1) * 64:c * 64], ALU.add), "vec")
        E.op("vector", lambda e: e.tensor_tensor(
            self.tots[0:1, 1088:1152], self.tots[0:1, 512 + 448:512 + 512],
            self.tots[0:1, 448:512], ALU.add), "vec")
        vpre = E.now("vec")

        vch_prev = E.now("vec")
        st_prev = E.now("ys")
        for c in range(NCH):
            self.acq(6)
            E.wait("tensor", "vec", vpre)
            E.op("tensor", lambda e, c=c: e.transpose(
                self.ps[6][0:128, 0:64], ACU(c), self.eye_f[0:64, 0:64]), "mm")
            E.op("tensor", lambda e, c=c: e.transpose(
                self.ps[6][0:128, 64:128], LMA[:, c * 128:(c + 1) * 128],
                self.eye_f[0:64, 0:64]), "mm")
            mmv = E.now("mm")
            E.wait("vector", "mm", mmv)
            E.op("vector", lambda e, c=c: e.tensor_copy(
                CTI[:, c * 128:(c + 1) * 128], self.ps[6][:, 0:128]), "vec")
            self.edone("vector", 6)
            vc = E.now("vec")
            E.wait("scalar", "vec", vc)
            E.op("scalar", lambda e, c=c: e.activation(
                WX[:, c * 64:(c + 1) * 64], CTI[:, c * 128:c * 128 + 64], AF.Exp),
                "act")
            self.acq(6)
            E.op("tensor", lambda e, c=c: e.matmul(
                self.ps[6][:, 0:64], self.onesf[:, :],
                self.tots[0:1, c * 64:(c + 1) * 64], start=True, stop=True), "mm")
            mmv = E.now("mm")
            E.wait("vector", "mm", mmv)
            E.wait("vector", "vec", vc)
            E.op("vector", lambda e, c=c: e.tensor_tensor(
                self.smallf[:, 0:64], self.ps[6][:, 0:64],
                CTI[:, c * 128 + 64:c * 128 + 128], ALU.add), "vec")
            vv = E.now("vec")
            E.wait("scalar", "vec", vv)
            E.wait("scalar", "vec", vch_prev)
            E.op("scalar", lambda e: e.activation(self.ebcT[:, :], self.ps[6][:, 0:64],
                                                  AF.Exp), "act")
            self.edone("scalar", 6)
            E.op("scalar", lambda e: e.activation(self.wtile[:, :],
                                                  self.smallf[:, 0:64], AF.Exp), "act")
            a_pre = E.now("act")
            E.wait("sync", "mm", E.now("mm"))
            E.wait("sync", "vec", vch_prev)
            alx = E.dma("sync", XT[:, :], d["d_xbc"][0:4096, c * 128:(c + 1) * 128],
                        transpose=True)
            alb = E.dma("sync", self.bfm[:, :],
                        d["d_xbc"][4096:4224, c * 128:(c + 1) * 128])
            alc = E.dma("sync", CFM[:, c * 128:(c + 1) * 128],
                        d["d_xbc"][4224:4352, c * 128:(c + 1) * 128])
            albt = E.dma("sync", self.btm[:, :],
                         d["d_xbc"][4096:4224, c * 128:(c + 1) * 128], transpose=True)
            alx = alb = alc = albt
            E.wait("vector", "act", a_pre)
            E.wait("vector", "al", albt)
            for g in range(8):
                E.op("vector", lambda e, g=g: e.tensor_tensor(
                    XW[:, g * 512:(g + 1) * 512].rearrange("p (a b) -> p a b", b=64),
                    XT[:, g * 512:(g + 1) * 512].rearrange("p (a b) -> p a b", b=64),
                    self.wtile[:, g * 8:(g + 1) * 8].unsqueeze(2)
                        .broadcast_to([128, 8, 64]), ALU.mult), "vec")
            vxw = E.now("vec")
            self.acq(4)
            E.wait("tensor", "al", max(alb, alc))
            E.op("tensor", lambda e, c=c: e.matmul(
                self.ps[4][:, 0:128], self.bfm[:, :], CFM[:, c * 128:(c + 1) * 128],
                start=True, stop=True), "mm")
            mmv = E.now("mm")
            E.wait("vector", "mm", mmv)
            E.op("vector", lambda e: e.tensor_tensor(
                self.bcm[:, :], self.ps[4][:, 0:128], self.consts[:, 256:384],
                ALU.mult), "vec")
            self.edone("vector", 4)
            vbc = E.now("vec")
            mmy_prev = E.now("mm")
            vsc_prev = E.now("vec")
            aexp_prev = E.now("act")
            for g in range(8):
                dsl = (0, 1) if g % 2 == 0 else (2, 3)
                for k in range(8):
                    h = 8 * g + k
                    s = dsl[k // 4]
                    if k % 4 == 0:
                        self.acq(s)
                    E.op("tensor", lambda e, h=h, k=k, s=s, c=c: e.matmul(
                        self.ps[s][:, (k % 4) * 128:(k % 4) * 128 + 128],
                        OH[0:64, h * 128:(h + 1) * 128], ACU(c),
                        start=True, stop=True), "mm" if k == 7 else None)
                mmv = E.now("mm")
                E.wait("vector", "mm", mmv)
                E.wait("vector", "act", aexp_prev)
                for k in range(8):
                    h = 8 * g + k
                    s = dsl[k // 4]
                    E.op("vector", lambda e, h=h, k=k, s=s, c=c: e.tensor_scalar(
                        ARG[:, k * 128:(k + 1) * 128],
                        self.ps[s][:, (k % 4) * 128:(k % 4) * 128 + 128],
                        CTI[:, c * 128 + 64 + h:c * 128 + 64 + h + 1], 30.0,
                        ALU.add, ALU.min), "vec")
                    if k % 4 == 3:
                        self.edone("vector", s)
                varg = E.now("vec")
                E.wait("scalar", "vec", varg)
                E.op("scalar", lambda e: e.activation(EXD[:, :], ARG[:, :],
                                                      AF.Exp), "act")
                aexp = E.now("act")
                aexp_prev = aexp
                E.wait("vector", "act", aexp)
                E.wait("vector", "vec", vbc)
                E.wait("vector", "mm", mmy_prev)
                if g == 0:
                    E.wait("vector", "ys", st_prev)
                E.op("vector", lambda e: e.tensor_tensor(
                    SCO[:, :].rearrange("p (a b) -> p a b", b=128),
                    EXD[:, :].rearrange("p (a b) -> p a b", b=128),
                    self.bcm[:, :].unsqueeze(1).broadcast_to([128, 8, 128]),
                    ALU.mult), "vec")
                vsc = E.now("vec")
                self.acq(5)
                for k in range(8):
                    E.op("tensor", lambda e, g=g, k=k, c=c: e.matmul(
                        self.ps[5][:, k * 64:(k + 1) * 64],
                        CFM[:, c * 128:(c + 1) * 128],
                        STATE[:, g * 512 + k * 64:g * 512 + (k + 1) * 64],
                        start=True, stop=True), None)
                self.acq(4)
                E.wait("tensor", "vec", vsc)
                for k in range(8):
                    E.op("tensor", lambda e, g=g, k=k: e.matmul(
                        self.ps[4][:, k * 64:(k + 1) * 64],
                        SCO[:, k * 128:(k + 1) * 128],
                        XT[:, g * 512 + k * 64:g * 512 + (k + 1) * 64],
                        start=True, stop=True), "mm" if k == 7 else None)
                mmy = E.now("mm")
                mmy_prev = mmy
                E.wait("vector", "mm", mmy)
                E.op("vector", lambda e, g=g, c=c: e.tensor_tensor(
                    self.smallf[:, 0:512].rearrange("p (a b) -> p a b", b=64),
                    self.ps[5][:, 0:512].rearrange("p (a b) -> p a b", b=64),
                    WX[:, c * 64:(c + 1) * 64][:, g * 8:(g + 1) * 8].unsqueeze(2)
                        .broadcast_to([128, 8, 64]), ALU.mult), "vec")
                self.edone("vector", 5)
                E.op("vector", lambda e, g=g: e.tensor_tensor(
                    YSB[:, g * 512:(g + 1) * 512], self.smallf[:, 0:512],
                    self.ps[4][:, 0:512], ALU.add), "vec")
                self.edone("vector", 4)
                E.op("vector", lambda e, g=g: e.tensor_tensor(
                    self.smallf[:, 512:1024].rearrange("p (a b) -> p a b", b=64),
                    STATE[:, g * 512:(g + 1) * 512].rearrange("p (a b) -> p a b", b=64),
                    self.ebcT[:, g * 8:(g + 1) * 8].unsqueeze(2)
                        .broadcast_to([128, 8, 64]), ALU.mult), "vec")
                vdec = E.now("vec")
                self.acq(7)
                E.wait("tensor", "al", albt)
                for k in range(8):
                    E.op("tensor", lambda e, g=g, k=k: e.matmul(
                        self.ps[7][:, k * 64:(k + 1) * 64], self.btm[:, :],
                        XW[:, g * 512 + k * 64:g * 512 + (k + 1) * 64],
                        start=True, stop=True), "mm" if k == 7 else None)
                mms = E.now("mm")
                E.wait("vector", "mm", mms)
                E.op("vector", lambda e, g=g: e.tensor_tensor(
                    STATE[:, g * 512:(g + 1) * 512], self.smallf[:, 512:1024],
                    self.ps[7][:, 0:512], ALU.add), "vec")
                self.edone("vector", 7)
                vsc_prev = E.now("vec")
            vch = E.now("vec")
            vch_prev = vch
            E.wait("sync", "vec", vch)
            E.dma("sync", d["d_y"][c * 128:(c + 1) * 128, :], YSB[:, :], sem="ys")
            st_prev = E.now("ys")
        E.wait("sync", "vec", E.now("vec"))
        E.dma("sync", d["cc_s_in"][:, :], STATE[:, :], sem="ys")
        E.dma("sync", d["cc_l_in"][:, :], self.tots[0:1, 1088:1152], sem="ys")
        self.pack_st = E.now("ys")

    def ph_exchange_z(self):
        E, d = self.E, self.d
        E.wait("gpsimd", "ys", self.pack_st)
        E.op("gpsimd", lambda e: e.collective_compute(
            "AllGather", ALU.bypass, replica_groups=[list(range(8))],
            ins=[d["cc_s_in"][:, :]], outs=[d["cc_s_out"][:, :]]), "cc")
        E.op("gpsimd", lambda e: e.collective_compute(
            "AllGather", ALU.bypass, replica_groups=[list(range(8))],
            ins=[d["cc_l_in"][:, :]], outs=[d["cc_l_out"][:, :]]), "cc")
        SZ = self.bv(0, 32768)
        self._phv = E.now("vec")
        pan_mm = {0: E.now("mm"), 1: E.now("mm")}
        for zc in range(D_INNER // 128):
            p = zc % 2
            pan = self.wst[:, p * 2048:p * 2048 + 2048]
            wl = self.wload(pan.rearrange("p (k q) -> p k q", q=128),
                            d["w_z"][zc].transpose([1, 0, 2]), pan_mm[p],
                            sem=f"w{p}")
            for (o, sz) in _splits(1024):
                pairs = [(pan[:, kc * 128:kc * 128 + 128],
                          self.act[:, kc * NT + HALO + o:kc * NT + HALO + o + sz])
                         for kc in range(KD)]
                s, mmv = self.job(pairs, lambda s, sz=sz: self.ps[s][:, 0:sz],
                                  pre=[wl, ("ys", self.pack_st)])
                pan_mm[p] = mmv
                E.wait("scalar", "mm", mmv)
                E.op("scalar", lambda e, s=s, zc=zc, o=o, sz=sz: e.activation(
                    SZ[:, zc * 1024 + o:zc * 1024 + o + sz], self.ps[s][:, 0:sz],
                    AF.Silu), "act")
                self.edone("scalar", s)
        self.z_done = E.now("act")

    def ph_gating(self):
        E, d = self.E, self.d
        SZ = self.bv(0, 32768)
        VP = self.bv(32768, 32768)
        E.wait("sync", "ys", self.corr_st)
        self.wait_stores("sync")
        self.acq(4)
        self.acq(5)
        gn_mm = [E.now("mm"), E.now("mm")]
        v_done = [0, 0]
        for fc in range(32):
            p = fc % 2
            YT = self.wst[:, p * 4096:p * 4096 + 1024]
            XS = self.wst[:, p * 4096 + 1024:p * 4096 + 2048]
            T1 = self.wst[:, p * 4096 + 2048:p * 4096 + 3072]
            E.wait("sync", "vec", v_done[p])
            a1 = E.dma("sync", YT[:, :], d["d_y"][0:1024, fc * 128:(fc + 1) * 128],
                       sem=f"g{p}", transpose=True)
            a2 = E.dma("sync", XS[:, :], d["d_xbc"][fc * 128:(fc + 1) * 128, :],
                       sem=f"g{p}")
            E.wait("vector", f"g{p}", a2)
            E.wait("vector", "act", self.z_done)
            E.wait("vector", "mm", gn_mm[p])
            dc = self.ccols[:, 202 + fc:202 + fc + 1]
            gw = self.ccols[:, 32 + fc:32 + fc + 1]
            vpf = VP[:, fc * 1024:(fc + 1) * 1024]
            E.op("vector", lambda e, XS=XS, YT=YT, T1=T1, dc=dc:
                 e.scalar_tensor_tensor(T1[:, :], XS[:, :], dc, YT[:, :],
                                        ALU.mult, ALU.add), "vec")
            E.op("vector", lambda e, T1=T1, vpf=vpf, fc=fc: e.tensor_tensor(
                vpf, T1[:, :], SZ[:, fc * 1024:(fc + 1) * 1024], ALU.mult), "vec")
            E.op("vector", lambda e, vpf=vpf, T1=T1: e.tensor_tensor(
                T1[:, :], vpf, vpf, ALU.mult), "vec")
            vsq = E.now("vec")
            E.op("vector", lambda e, vpf=vpf, gw=gw: e.tensor_scalar_mul(
                vpf, vpf, gw), "vec")
            E.wait("tensor", "vec", vsq)
            for si, (o, sz) in enumerate(_splits(1024)):
                E.op("tensor", lambda e, si=si, o=o, sz=sz, T1=T1, fc=fc: e.matmul(
                    self.ps[4 + si][0:1, 0:sz], self.consts[:, 448:449],
                    T1[:, o:o + sz], start=(fc == 0), stop=(fc == 31)),
                    "mm" if si == 1 else None)
            gn_mm[p] = E.now("mm")
            v_done[p] = E.now("vec")
        mmgn = E.now("mm")
        E.wait("scalar", "mm", mmgn)
        for si in range(2):
            E.op("scalar", lambda e, si=si: e.activation(
                self.rows[0:1, si * 512:(si + 1) * 512],
                self.ps[4 + si][0:1, 0:512], AF.Ln, scale=1.0 / D_INNER, bias=EPS),
                "act")
            self.edone("scalar", 4 + si)
            E.op("scalar", lambda e, si=si: e.activation(
                self.rows[0:1, si * 512:(si + 1) * 512],
                self.rows[0:1, si * 512:(si + 1) * 512], AF.Exp, scale=-0.5), "act")
        agn = E.now("act")
        E.wait("tensor", "act", agn)
        self.acq(6)
        for i in range(8):
            E.op("tensor", lambda e, i=i: e.matmul(
                self.ps[6][:, i:i + 1], self.rows[0:1, i * 128:(i + 1) * 128],
                self.eye_f[0:1, 0:1], start=True, stop=True),
                "mm" if i == 7 else None)
        mmv = E.now("mm")
        E.wait("vector", "mm", mmv)
        E.op("vector", lambda e: e.tensor_copy(self.gcol[:, :], self.ps[6][:, 0:8]),
             "vec")
        self.edone("vector", 6)
        self.gn_done = E.now("vec")

    def ph_combine(self):
        E, d = self.E, self.d
        SIN = self.bv(40960, 4096)
        SL = [self.bv(45056, 4096), self.bv(49152, 4096)]
        self.wjrow = self.bv(53248, 8192, F32)
        E.wait("gpsimd", "cc", E.now("cc"))
        all_ = E.dma("gpsimd", self.ldece[0:8, :], d["cc_l_out"][:, :], sem="lde")
        sl_al = [None] * 8
        sl_lastread = [0, 0]
        self.acq(6)
        E.wait("tensor", "lde", all_)
        E.op("tensor", lambda e: e.matmul(self.ps[6][0:64, 0:8], self.ldece[:, :],
                                          self.msel9[:, :], start=True, stop=True),
             "mm")
        mmv = E.now("mm")
        E.wait("scalar", "mm", mmv)
        E.op("scalar", lambda e: e.activation(self.wjT[:, :], self.ps[6][0:64, 0:8],
                                              AF.Exp), "act")
        self.edone("scalar", 6)
        awj = E.now("act")
        E.op("vector", lambda e: e.memset(SIN[:, :], 0.0), "vec")
        E.wait("tensor", "act", awj)
        for j in range(2):
            E.wait("gpsimd", "vec", sl_lastread[j % 2])
            sl_al[j] = E.dma("gpsimd", SL[j % 2][:, :],
                             d["cc_s_out"][j * 128:(j + 1) * 128, :], sem=f"sl{j % 2}")
        for j in range(8):
            self.acq(6)
            E.op("tensor", lambda e, j=j: e.transpose(
                self.ps[6][0:1, 0:64], self.wjT[:, j:j + 1],
                self.eye_f[0:64, 0:64]), "mm")
            mmv = E.now("mm")
            E.wait("vector", "mm", mmv)
            E.op("vector", lambda e: e.tensor_copy(self.rows[0:1, 0:64],
                                                   self.ps[6][0:1, 0:64]), "vec")
            self.edone("vector", 6)
            E.op("vector", lambda e: e.tensor_copy(
                self.wjrow[0:1, :].rearrange("p (a b) -> p a b", b=64),
                self.rows[0:1, 0:64].unsqueeze(2).broadcast_to([1, 64, 64])), "vec")
            vrow = E.now("vec")
            for g in range(8):
                s = g % 2
                self.acq(s)
                E.wait("tensor", "vec", vrow)
                E.op("tensor", lambda e, g=g, s=s: e.matmul(
                    self.ps[s][:, 0:512], self.onesf[:, :],
                    self.wjrow[0:1, g * 512:(g + 1) * 512], start=True, stop=True),
                    "mm")
                mmv = E.now("mm")
                E.wait("vector", "mm", mmv)
                E.wait("vector", f"sl{j % 2}", sl_al[j])
                E.op("vector", lambda e, g=g, s=s, j=j: e.tensor_tensor(
                    self.smallf[:, 0:512], SL[j % 2][:, g * 512:(g + 1) * 512],
                    self.ps[s][:, 0:512], ALU.mult), "vec")
                self.edone("vector", s)
                E.op("vector", lambda e, g=g: e.tensor_tensor(
                    SIN[:, g * 512:(g + 1) * 512], SIN[:, g * 512:(g + 1) * 512],
                    self.smallf[:, 0:512], ALU.add), "vec")
            sl_lastread[j % 2] = E.now("vec")
            if j + 2 < 8:
                E.wait("gpsimd", "vec", sl_lastread[(j + 2) % 2])
                sl_al[j + 2] = E.dma("gpsimd", SL[(j + 2) % 2][:, :],
                                     d["cc_s_out"][(j + 2) * 128:(j + 3) * 128, :],
                                     sem=f"sl{j % 2}")
        self.sin_done = E.now("vec")

    def ph_correction(self):
        E, d = self.E, self.d
        SIN = self.bv(40960, 4096)
        CTI = self.bv(32768, 2048, F32)
        CFM = self.bv(35840, 1024)
        YSB = self.bv(36864, 4096)
        v_prev = E.now("vec")
        for c in range(NCH):
            E.wait("sync", "vec", v_prev)
            E.wait("sync", "ys", E.now("ys"))
            aly = E.dma("sync", YSB[:, :], d["d_y"][c * 128:(c + 1) * 128, :])
            self.acq(6)
            E.op("tensor", lambda e, c=c: e.matmul(
                self.ps[6][:, 0:64], self.onesf[:, :],
                self.tots[0:1, 512 + c * 64:512 + (c + 1) * 64],
                start=True, stop=True), "mm")
            mmv = E.now("mm")
            E.wait("vector", "mm", mmv)
            E.op("vector", lambda e, c=c: e.tensor_tensor(
                self.smallf[:, 1024 + 0:1024 + 64], self.ps[6][:, 0:64],
                CTI[:, c * 128:c * 128 + 64], ALU.add), "vec")
            self.edone("vector", 6)
            E.wait("scalar", "vec", E.now("vec"))
            E.wait("scalar", "vec", v_prev)
            E.op("scalar", lambda e: e.activation(
                self.wtile[:, :], self.smallf[:, 1024:1088], AF.Exp), "act")
            acw = E.now("act")
            for g in range(8):
                s = g % 2
                self.acq(s)
                E.wait("tensor", "vec", self.sin_done)
                for k in range(8):
                    E.op("tensor", lambda e, g=g, k=k, s=s, c=c: e.matmul(
                        self.ps[s][:, k * 64:(k + 1) * 64],
                        CFM[:, c * 128:(c + 1) * 128],
                        SIN[:, g * 512 + k * 64:g * 512 + (k + 1) * 64],
                        start=True, stop=True), "mm" if k == 7 else None)
                mmv = E.now("mm")
                E.wait("vector", "mm", mmv)
                E.wait("vector", "act", acw)
                E.wait("vector", "al", aly)
                E.op("vector", lambda e, g=g, s=s: e.tensor_tensor(
                    self.smallf[:, 0:512].rearrange("p (a b) -> p a b", b=64),
                    self.ps[s][:, 0:512].rearrange("p (a b) -> p a b", b=64),
                    self.wtile[:, g * 8:(g + 1) * 8].unsqueeze(2)
                        .broadcast_to([128, 8, 64]), ALU.mult), "vec")
                self.edone("vector", s)
                E.op("vector", lambda e, g=g: e.tensor_tensor(
                    YSB[:, g * 512:(g + 1) * 512], YSB[:, g * 512:(g + 1) * 512],
                    self.smallf[:, 0:512], ALU.add), "vec")
            v_prev = E.now("vec")
            E.wait("sync", "vec", v_prev)
            E.dma("sync", d["d_y"][c * 128:(c + 1) * 128, :], YSB[:, :], sem="ys")
        self.corr_st = E.now("ys")

    def ph_outproj(self):
        E, d = self.E, self.d
        VP = self.bv(32768, 32768)
        vp_v = E.now("vec")
        self._phv = vp_v
        pan_mm = {0: E.now("mm"), 1: E.now("mm")}
        OS = self.wst[:, 0:2048]
        ost_st = {0: 0, 1: 0}
        for dc in range(KD):
            p = dc % 2
            pan = self.act[:, p * 4096:p * 4096 + 4096]
            wl = self.wload(pan.rearrange("p (k q) -> p k q", q=128),
                            d["w_op"][dc].transpose([1, 0, 2]), pan_mm[p],
                            sem=f"w{p}")
            for si, (o, sz) in enumerate(_splits(1024)):
                pairs = [(pan[:, kc * 128:kc * 128 + 128],
                          VP[:, kc * 1024 + o:kc * 1024 + o + sz])
                         for kc in range(D_INNER // 128)]
                s, mmv = self.job(pairs, lambda s, sz=sz: self.ps[s][:, 0:sz],
                                  pre=[wl, ("vec", vp_v)])
                pan_mm[p] = mmv
                ost = OS[:, (si % 2) * 1024:(si % 2) * 1024 + sz]
                E.wait("vector", "mm", mmv)
                E.wait("vector", f"fs{si % 2}", ost_st[si % 2])
                E.op("vector", lambda e, s=s, sz=sz, ost=ost: e.tensor_copy(
                    ost, self.ps[s][:, 0:sz]), "vec")
                self.edone("vector", s)
                E.wait("sync", "vec", E.now("vec"))
                E.dma("sync", d["d_mo"][dc * 128:dc * 128 + 128, o:o + sz], ost,
                      sem=f"fs{si % 2}")
                ost_st[si % 2] = E.now(f"fs{si % 2}")
        self.mo_st = None

    def ph_h(self):
        E, d = self.E, self.d
        SA = self.bv(49152, 4096, F32)
        SB = self.bv(53248, 4096, F32)
        B1 = self.bv(61440, 2048)
        self.wait_stores("sync")
        E.wait("sync", "vec", self.gn_done)
        for i, (o, sz) in enumerate(T8):
            E.wait("sync", "act", E.now("act"))
            E.wait("sync", "ts", E.now("ts"))
            av = E.dma("sync", SA[0:sz, :], d["d_xa"][HALO + o:HALO + o + sz, :])
            avT = E.dma("sync", B1[0:128, :], d["d_mo"][:, o:o + 128], transpose=True)
            ti = i + 1
            m = self.cols[:, 4 * ti:4 * ti + 1]
            r = self.cols[:, 4 * ti + 2:4 * ti + 3]
            E.wait("vector", "al", max(av, avT))
            E.wait("vector", "ts", E.now("ts"))
            E.op("vector", lambda e, sz=sz, m=m, r=r: e.tensor_scalar(
                SB[0:sz, :], SA[0:sz, :], m[0:sz, :], r[0:sz, :],
                ALU.subtract, ALU.mult), "vec")
            E.op("vector", lambda e, sz=sz, i=i: e.scalar_tensor_tensor(
                SB[0:sz, :], B1[0:sz, :], self.gcol[:, i:i + 1], SB[0:sz, :],
                ALU.mult, ALU.add), "vec")
            E.wait("sync", "vec", E.now("vec"))
            E.dma("sync", d["d_h"][o:o + sz, :], SB[0:sz, :], sem="ts")
            q = self.cols[:, 36 + 2 * i:36 + 2 * i + 1]
            rs = self.cols[:, 36 + 2 * i + 1:36 + 2 * i + 2]
            E.wait("scalar", "vec", E.now("vec"))
            E.op("scalar", lambda e, sz=sz, q=q: e.activation(
                SA[0:sz, :], SB[0:sz, :], AF.Square, accum_out=q[0:sz, :]), "act")
            E.op("scalar", lambda e, sz=sz, q=q, rs=rs: e.activation(
                rs[0:sz, :], q[0:sz, :], AF.Ln, scale=1.0 / D_MODEL, bias=EPS), "act")
            E.op("scalar", lambda e, sz=sz, rs=rs: e.activation(
                rs[0:sz, :], rs[0:sz, :], AF.Exp, scale=-0.5), "act")
            E.wait("vector", "act", E.now("act"))
            E.op("vector", lambda e, sz=sz, rs=rs: e.tensor_scalar_mul(
                B1[0:sz, :], SB[0:sz, :], rs[0:sz, :]), "vec")
            E.wait("sync", "vec", E.now("vec"))
            E.dma("sync", d["d_h2n"][o:o + sz, :], B1[0:sz, :], sem="ts")

    def ph_swiglu(self, fills):
        E, d = self.E, self.d
        al_f, vec_f = fills
        pre_f = [("al", al_f), ("vec", vec_f)]
        GU = self.bv(0, 65536)
        sil_rd = {}
        self._phv = E.now("vec")
        pan_mm = {0: E.now("mm"), 1: E.now("mm"), 2: E.now("mm"), 3: E.now("mm")}
        for hc in range(D_FF // 128):
            p = hc % 2
            gp_ = self.wst[:, p * 4096:p * 4096 + 2048]
            up_ = self.wst[:, p * 4096 + 2048:p * 4096 + 4096]
            wlg = self.wload(gp_.rearrange("p (k q) -> p k q", q=128),
                             d["w_gate"][hc].transpose([1, 0, 2]), pan_mm[2 * p],
                             sem=f"w{2 * p}")
            wlu = self.wload(up_.rearrange("p (k q) -> p k q", q=128),
                             d["w_up"][hc].transpose([1, 0, 2]), pan_mm[2 * p + 1],
                             sem=f"w{2 * p + 1}")
            for si, (o, sz) in enumerate(_splits(1024)):
                gpairs = [(gp_[:, kc * 128:kc * 128 + 128],
                           self.act[:, kc * NT + o:kc * NT + o + sz])
                          for kc in range(KD)]
                upairs = [(up_[:, kc * 128:kc * 128 + 128],
                           self.act[:, kc * NT + o:kc * NT + o + sz])
                          for kc in range(KD)]
                sA, mmA = self.job(gpairs, lambda s, sz=sz: self.ps[s][:, 0:sz],
                                   pre=pre_f + [wlg])
                sB, mmB = self.job(upairs, lambda s, sz=sz: self.ps[s][:, 0:sz],
                                   pre=[wlu])
                pan_mm[2 * p] = mmA
                pan_mm[2 * p + 1] = mmB
                sl = self.silub[:, o:o + sz]
                E.wait("scalar", "mm", mmA)
                E.wait("scalar", "vec", sil_rd.get(o, 0))
                E.op("scalar", lambda e, sA=sA, sz=sz, sl=sl: e.activation(
                    sl, self.ps[sA][:, 0:sz], AF.Silu), "act")
                self.edone("scalar", sA)
                E.wait("vector", "mm", mmB)
                E.wait("vector", "act", E.now("act"))
                E.op("vector", lambda e, sB=sB, sz=sz, sl=sl, hc=hc, o=o:
                     e.tensor_tensor(GU[:, hc * 1024 + o:hc * 1024 + o + sz],
                                     self.ps[sB][:, 0:sz], sl, ALU.mult), "vec")
                self.edone("vector", sB)
                sil_rd[o] = E.now("vec")
        vgu = E.now("vec")
        self._phv = vgu
        pan_mm2 = {0: E.now("mm"), 1: E.now("mm")}
        OS = self.wst[:, 0:2048]
        ost_st = {0: 0, 1: 0}
        for dc in range(KD):
            p = dc % 2
            pan = self.act[:, p * 8192:p * 8192 + 8192]
            wld = self.wload(pan.rearrange("p (k q) -> p k q", q=128),
                             d["w_down"][dc].transpose([1, 0, 2]), pan_mm2[p],
                             sem=f"w{p}")
            for si, (o, sz) in enumerate(_splits(1024)):
                pairs = [(pan[:, hc * 128:hc * 128 + 128],
                          GU[:, hc * 1024 + o:hc * 1024 + o + sz])
                         for hc in range(D_FF // 128)]
                s, mmv = self.job(pairs, lambda s, sz=sz: self.ps[s][:, 0:sz],
                                  pre=[wld, ("vec", vgu)])
                pan_mm2[p] = mmv
                ost = OS[:, (si % 2) * 1024:(si % 2) * 1024 + sz]
                E.wait("vector", "mm", mmv)
                E.wait("vector", f"fs{si % 2}", ost_st[si % 2])
                E.op("vector", lambda e, s=s, sz=sz, ost=ost: e.tensor_copy(
                    ost, self.ps[s][:, 0:sz]), "vec")
                self.edone("vector", s)
                E.wait("sync", "vec", E.now("vec"))
                E.dma("sync", d["d_sw"][dc * 128:dc * 128 + 128, o:o + sz], ost,
                      sem=f"fs{si % 2}")
                ost_st[si % 2] = E.now(f"fs{si % 2}")

    def ph_xb(self):
        E, d = self.E, self.d
        SA = self.bv(49152, 4096, F32)
        SB = self.bv(53248, 4096, F32)
        SC = self.bv(57344 + 2048, 4096, F32)
        SC = self.bv(45056, 4096, F32)
        B1 = self.bv(61440, 2048)
        self.wait_stores("sync")
        for i, (o, sz) in enumerate(T8):
            E.wait("sync", "act", E.now("act"))
            E.wait("sync", "vec", E.now("vec"))
            E.wait("sync", "ts", E.now("ts"))
            av = E.dma("sync", SA[0:sz, :], d["d_xa"][HALO + o:HALO + o + sz, :])
            av2 = E.dma("sync", SC[0:sz, :], d["d_h"][o:o + sz, :])
            avT = E.dma("sync", B1[0:128, :], d["d_sw"][:, o:o + 128], transpose=True)
            E.wait("vector", "al", avT)
            E.op("vector", lambda e, sz=sz: e.tensor_tensor(
                SA[0:sz, :], SA[0:sz, :], SC[0:sz, :], ALU.add), "vec")
            E.op("vector", lambda e, sz=sz: e.tensor_tensor(
                SA[0:sz, :], SA[0:sz, :], B1[0:sz, :], ALU.add), "vec")
            E.wait("sync", "vec", E.now("vec"))
            E.dma("sync", d["d_xb"][o:o + sz, :], SA[0:sz, :], sem="ts")
            m, q, r, v = self.ln_stats(SA, sz, 52 + 4 * i)
            E.wait("vector", "act", E.now("act"))
            E.wait("vector", "ts", E.now("ts"))
            E.op("vector", lambda e, sz=sz, m=m, r=r: e.tensor_scalar(
                B1[0:sz, :], SA[0:sz, :], m[0:sz, :], r[0:sz, :],
                ALU.subtract, ALU.mult), "vec")
            E.wait("sync", "vec", E.now("vec"))
            E.dma("sync", d["d_xn3"][o:o + sz, :], B1[0:sz, :], sem="ts")

    def ph_final(self):
        E, d = self.E, self.d
        SA = self.bv(49152, 4096, F32)
        B1 = self.bv(61440, 2048)
        self.wait_stores("sync")
        for i, (o, sz) in enumerate(T8):
            E.wait("sync", "vec", E.now("vec"))
            E.wait("sync", "ts", E.now("ts"))
            av = E.dma("sync", SA[0:sz, :], d["d_xb"][o:o + sz, :])
            avT = E.dma("sync", B1[0:128, :], d["d_f2"][:, o:o + 128], transpose=True)
            E.wait("vector", "al", avT)
            E.op("vector", lambda e, sz=sz: e.tensor_tensor(
                SA[0:sz, :], SA[0:sz, :], B1[0:sz, :], ALU.add), "vec")
            E.wait("sync", "vec", E.now("vec"))
            E.dma("sync", self.out_t[o:o + sz, :], SA[0:sz, :], sem="ts")
        E.wait("sync", "ts", E.now("ts"))

    def ph_debug(self):
        E = self.E
        self.wait_stores("sync")
        E.wait("sync", "vec", E.now("vec"))
        E.wait("sync", "act", E.now("act"))
        E.wait("sync", "mm", E.now("mm"))
        for name, t in self.dbg_t.items():
            s = self.d[name]
            E.dma("sync", t[:, :], s[:, :], sem="ts")
        E.wait("sync", "ts", E.now("ts"))


_DEBUG = ()
_STAGES = 99


def host_shared(inputs):
    def tile_lhsT(w):
        K, M = w.shape
        return np.ascontiguousarray(
            w.reshape(K // 128, 128, M // 128, 128).transpose(2, 0, 1, 3))

    ip = np.asarray(inputs["L0_attn_in_proj_w"])
    cw = np.asarray(inputs["L0_attn_conv_w"])[:, 0, :]
    diag = np.zeros((NFC, K_CONV, 128, 128), np.float32)
    idx = np.arange(128)
    diag[:, :, idx, idx] = cw.reshape(NFC, 128, K_CONV).transpose(0, 2, 1)

    def colmat(v):
        v = np.asarray(v, np.float32)
        return np.ascontiguousarray(v.reshape(-1, 128).T)

    ccols = np.zeros((128, 256), np.float32)
    ccols[:, 0:16] = colmat(inputs["L0_attn_ln1_w"])
    ccols[:, 16:32] = colmat(inputs["L0_attn_ln2_w"])
    ccols[:, 32:64] = colmat(inputs["L0_attn_gnorm_w"])
    ccols[:, 64:108] = colmat(inputs["L0_ffn_s_gate"])
    ccols[:, 108:152] = colmat(inputs["L0_ffn_s_up"])
    ccols[:, 152:168] = colmat(inputs["L0_ffn_s_down"])
    ccols[:, 168:202] = colmat(inputs["L0_attn_conv_b"])
    ccols[:, 202:234] = colmat(np.repeat(np.asarray(inputs["L0_attn_D"], np.float32),
                                         D_HEAD))
    ccols[:, 250] = EPS

    dcol = np.zeros((64, 2), np.float32)
    dcol[:, 0] = np.asarray(inputs["L0_attn_dt_bias"], np.float32)
    dcol[:, 1] = np.asarray(inputs["L0_attn_A_log"], np.float32)

    consts = np.zeros((128, 512), np.float32)
    consts[0, 0:128] = 1.0
    consts[0, 128:256] = -128.0
    consts[:, 256:384] = np.triu(np.ones((128, 128), np.float32))
    consts[:, 384:512] = 0.0
    consts[:, 448] = 1.0

    onehot = np.zeros((64, 64, 128), np.float32)
    onehot[np.arange(64), np.arange(64), :] = 1.0

    sh = {
        "w_xbc": tile_lhsT(np.ascontiguousarray(ip[:, D_INNER:D_INNER + CONV_DIM])),
        "w_dt": np.ascontiguousarray(ip[:, -N_HEADS:].reshape(KD, 128, N_HEADS)),
        "w_z": tile_lhsT(np.ascontiguousarray(ip[:, :D_INNER])),
        "w_gate": tile_lhsT(np.asarray(inputs["L0_attn_gate_w"])),
        "w_up": tile_lhsT(np.asarray(inputs["L0_attn_up_w"])),
        "w_down": tile_lhsT(np.asarray(inputs["L0_attn_down_w"])),
        "w_op": tile_lhsT(np.asarray(inputs["L0_attn_out_proj_w"])),
        "wq_g": tile_lhsT(np.ascontiguousarray(np.asarray(inputs["L0_ffn_wq_gate"]).T)),
        "wq_u": tile_lhsT(np.ascontiguousarray(np.asarray(inputs["L0_ffn_wq_up"]).T)),
        "wq_d": tile_lhsT(np.ascontiguousarray(np.asarray(inputs["L0_ffn_wq_down"]).T)),
        "w_conv": diag,
        "ccols_in": ccols,
        "dcol_in": dcol,
        "onehot": np.ascontiguousarray(onehot.reshape(64, 64 * 128)),
        "ones_f": np.ones((1, 128), np.float32),
        "consts_in": consts.astype(BF),
        "eye_f_in": np.eye(128, dtype=np.float32),
    }
    return sh


def host_core(inputs, core_id):
    x = np.asarray(inputs["L0_attn_hidden_states"])
    b, seg = divmod(core_id, 4)
    t0 = seg * 1024
    halo = (np.zeros((HALO, D_MODEL), np.float32) if seg == 0
            else np.ascontiguousarray(np.asarray(x[b, t0 - HALO:t0], np.float32)))
    xin = np.concatenate([halo, np.asarray(x[b, t0:t0 + 1024], np.float32)], axis=0)

    msel = np.zeros((9, 8), np.float32)
    msel[8, :] = NEG
    for j in range(8):
        jb, jseg = divmod(j, 4)
        if jb == b and jseg < seg:
            msel[8, j] = 0.0
            for i in range(8):
                ib, iseg = divmod(i, 4)
                if ib == b and jseg < iseg < seg:
                    msel[i, j] = 1.0
    return {"x_in": np.ascontiguousarray(xin), "msel_in": msel}


def kernel(**inputs):
    sh = host_shared(inputs)
    in_maps = []
    for c in range(8):
        m = dict(sh)
        m.update(host_core(inputs, c))
        in_maps.append(m)
    b = Build(debug=_DEBUG, stages=_STAGES)
    nc = b.build()
    res = run_bass_kernel_spmd(nc, in_maps, core_ids=list(range(8)))
    out = np.zeros((2, 4096, D_MODEL), np.float32)
    for c in range(8):
        bb, seg = divmod(c, 4)
        out[bb, seg * 1024:(seg + 1) * 1024] = res.results[c]["out"]
    if _DEBUG:
        kernel.dbg = [{k: v for k, v in r.items() if k.startswith("dbg_")}
                      for r in res.results]
    return out


# version 12
# speedup vs baseline: 1.0687x; 1.0373x over previous
import numpy as np
import ml_dtypes
import concourse.bass as bass
import concourse.mybir as mybir
from concourse.bass_utils import run_bass_kernel_spmd
from contextlib import ExitStack

F32 = mybir.dt.float32
BF16 = mybir.dt.bfloat16
I32 = mybir.dt.int32
AF = mybir.ActivationFunctionType
ALU = mybir.AluOpType
AX = mybir.AxisListType.X
BF = ml_dtypes.bfloat16

D_MODEL = 2048
D_INNER = 4096
N_HEADS = 64
D_HEAD = 64
D_STATE = 128
K_CONV = 4
D_FF = 8192
HID = 5632
CONV_DIM = D_INNER + 2 * D_STATE
NFC = CONV_DIM // 128
EPS = 1e-5
HALO = 16
NT = 1024 + HALO
Q = 128
NCH = 8
KD = D_MODEL // 128
NEG = -30000.0

T9 = [(0, HALO)] + [(HALO + 128 * i, 128) for i in range(8)]
T8 = [(128 * i, 128) for i in range(8)]


def _splits(n):
    k = -(-n // 512)
    base, rem = divmod(n, k)
    out, o = [], 0
    for i in range(k):
        s = base + (1 if i < rem else 0)
        out.append((o, s))
        o += s
    return out


class Emit:
    ENGS = ("sync", "scalar", "vector", "tensor", "gpsimd")
    CHAIN = {"vector": "vec", "scalar": "act"}

    def __init__(self):
        self.q = {e: [] for e in self.ENGS}
        self.cnt = {}

    def op(self, eng, fn, sem=None, amt=1, chain=True):
        ch = self.CHAIN.get(eng)
        if ch is not None and chain:
            self.wait(eng, ch, self.cnt.get(ch, 0))
        if sem is not None:
            self.cnt[sem] = self.cnt.get(sem, 0) + amt
            self.q[eng].append(("op", fn, sem, amt))
            return self.cnt[sem]
        self.q[eng].append(("op", fn, None, 0))
        return None

    def dma(self, eng, out, in_, sem="al", **kw):
        return self.op(eng, (lambda e, o=out, i=in_, k=kw: e.dma_start(out=o, in_=i, **k)),
                       sem, 16, chain=False)

    def wait(self, eng, sem, val):
        if val:
            self.cnt.setdefault(sem, 0)
            self.q[eng].append(("wait", sem, val))

    def now(self, sem):
        return self.cnt.get(sem, 0)

    def replay(self, eng_obj, eng_name, sems):
        for item in self.q[eng_name]:
            if item[0] == "op":
                _, fn, sem, amt = item
                ins = fn(eng_obj)
                if sem is not None:
                    ins.then_inc(sems[sem], amt)
            else:
                _, sem, val = item
                eng_obj.wait_ge(sems[sem], val)


class Build:
    def __init__(self, debug=(), stages=99):
        self.nc = bass.Bass("TRN2", target_bir_lowering=False)
        self.E = Emit()
        self.stages = stages
        self.debug = list(debug)
        self.slot_use = [0] * 8
        self.rot_i = 0
        self.mk_dram()

    def mk_dram(self):
        nc = self.nc
        self.d = {}
        def din(name, shape, dt):
            self.d[name] = nc.dram_tensor(name, list(shape), dt, kind="ExternalInput")
        def dsc(name, shape, dt):
            self.d[name] = nc.dram_tensor(name, list(shape), dt)

        din("x_in", [NT, D_MODEL], F32)
        din("w_xbc", [NFC, KD, 128, 128], F32)
        din("w_dt", [KD, 128, N_HEADS], F32)
        din("w_z", [D_INNER // 128, KD, 128, 128], F32)
        din("w_gate", [D_FF // 128, KD, 128, 128], F32)
        din("w_up", [D_FF // 128, KD, 128, 128], F32)
        din("w_down", [KD, D_FF // 128, 128, 128], F32)
        din("w_op", [KD, D_INNER // 128, 128, 128], F32)
        din("wq_g", [HID // 128, KD, 128, 128], I32)
        din("wq_u", [HID // 128, KD, 128, 128], I32)
        din("wq_d", [KD, HID // 128, 128, 128], I32)
        din("w_conv", [NFC, K_CONV, 128, 128], F32)
        din("ccols_in", [128, 256], F32)
        din("dcol_in", [64, 2], F32)
        din("msel_in", [9, 8], F32)
        din("onehot", [64, 64 * 128], F32)
        din("ones_f", [1, 128], F32)
        din("consts_in", [128, 512], BF16)
        din("eye_f_in", [128, 128], F32)

        self.out_t = nc.dram_tensor("out", [1024, D_MODEL], F32, kind="ExternalOutput")

        dsc("d_xa", [NT, D_MODEL], F32)
        dsc("d_xn1", [NT, D_MODEL], BF16)
        dsc("d_t1", [NT, D_MODEL], BF16)
        dsc("d_f1", [D_MODEL, 1152], BF16)
        dsc("d_xbc", [CONV_DIM, 1024], BF16)
        dsc("d_y", [1024, D_INNER], BF16)
        dsc("d_mo", [D_MODEL, 1024], BF16)
        dsc("d_h", [1024, D_MODEL], F32)
        dsc("d_h2n", [1024, D_MODEL], BF16)
        dsc("d_sw", [D_MODEL, 1024], BF16)
        dsc("d_xb", [1024, D_MODEL], F32)
        dsc("d_xn3", [1024, D_MODEL], BF16)
        dsc("d_f2", [D_MODEL, 1024], BF16)
        dsc("cc_s_in", [128, D_INNER], BF16)
        dsc("cc_s_out", [8 * 128, D_INNER], BF16)
        dsc("cc_l_in", [1, 64], F32)
        dsc("cc_l_out", [8, 64], F32)
        dsc("d_probe", [128, 2048], BF16)

        self.dbg_t = {}
        for name in self.debug:
            src = self.d[name]
            self.dbg_t[name] = nc.dram_tensor(
                "dbg_" + name, list(src.shape), src.dtype, kind="ExternalOutput")

    def build(self):
        nc, E = self.nc, self.E
        ctx = ExitStack()
        def sb(name, shape, dt):
            return ctx.enter_context(nc.sbuf_tensor(name, list(shape), dt))
        with ctx:
            self.act = sb("act_fm", [128, KD * NT], BF16)
            self.ba = sb("barena", [128, 65536], BF16)
            self.wst = sb("wst", [128, 8192], BF16)
            self.silub = sb("silub", [128, 1040], BF16)
            self.consts = sb("consts", [128, 512], BF16)
            self.eye_f = sb("eye_f", [128, 128], F32)
            self.onesf = sb("onesf", [1, 128], F32)
            self.ccols = sb("ccols", [128, 256], F32)
            self.cols = sb("cols", [128, 96], F32)
            self.rows = sb("rows", [1, 1024], F32)
            self.rowsb = sb("rowsb", [1, 2 * NT], BF16)
            self.dcol = sb("dcol", [64, 8], F32)
            self.tots = sb("tots", [1, 1152], F32)
            self.smallf = sb("smallf", [128, 1088], F32)
            self.wtile = sb("wtile", [128, 64], F32)
            self.ebcT = sb("ebcT", [128, 64], F32)
            self.gcol = sb("gcol", [128, 8], F32)
            self.ldece = sb("ldece", [9, 64], F32)
            self.msel9 = sb("msel9", [9, 8], F32)
            self.bfm = sb("bfm", [128, 128], BF16)
            self.btm = sb("btm", [128, 128], BF16)
            self.bcm = sb("bcm", [128, 128], BF16)
            self.wjT = sb("wjT", [64, 8], F32)
            self.zbuf = sb("zbuf", [64, 128], F32)
            self.ps = [ctx.enter_context(nc.psum_tensor(f"ps{i}", [128, 512], F32))
                       for i in range(8)]
            nc.const_aps.register(F32, EPS, self.ccols[:, 250:251])
            sem_names = ["al", "wl", "mm", "vec", "act", "gp", "st", "cc",
                         "w0", "w1", "w2", "w3", "sl0", "sl1", "lde",
                         "g0", "g1", "ts", "fs0", "fs1", "xs0", "xs1", "ys", "oh"] + \
                        [f"e{i}" for i in range(8)]
            sems = {n: ctx.enter_context(nc.semaphore(n)) for n in sem_names}
            block = ctx.enter_context(nc.Block())

            self.emit_all()

            for name in Emit.ENGS:
                def mk(nm):
                    def f(eng):
                        E.replay(eng, nm, sems)
                    return f
                getattr(block, name)(mk(name))
        return nc

    def bv(self, c0, ncols, dt=BF16):
        ap = self.ba[:, c0:c0 + ncols]
        return ap if dt == BF16 else ap.bitcast(dt)

    def rot(self):
        s = self.rot_i % 4
        self.rot_i += 1
        return s

    def acq(self, s):
        self.E.wait("tensor", f"e{s}", self.slot_use[s])

    def edone(self, eng, s):
        self.E.op(eng, lambda e: e.nop(), f"e{s}")
        self.slot_use[s] += 1

    def job(self, pairs, out_fn, pre=()):
        E = self.E
        s = self.rot()
        self.acq(s)
        for sem, val in pre:
            E.wait("tensor", sem, val)
        n = len(pairs)
        for j, (lh, rh) in enumerate(pairs):
            last = (j == n - 1)
            E.op("tensor", lambda e, lh=lh, rh=rh, s=s, st=(j == 0),
                 sp=last, of=out_fn: e.matmul(of(s), lh, rh,
                                              start=st, stop=sp),
                 "mm" if last else None)
        return s, E.now("mm")

    STORE_SEMS = ("ts", "fs0", "fs1", "xs0", "xs1", "ys")

    def wait_stores(self, eng):
        for s in self.STORE_SEMS:
            self.E.wait(eng, s, self.E.now(s))

    def wload(self, dst_ap, src_ap, last_mm, sem="w0"):
        E = self.E
        E.wait("gpsimd", "mm", last_mm)
        E.wait("gpsimd", "vec", getattr(self, "_phv", 0))
        return (sem, E.dma("gpsimd", dst_ap, src_ap, sem=sem))

    def ln_stats(self, buf, sz, sb0, keep_v=False):
        E = self.E
        m = self.cols[:, sb0:sb0 + 1]
        q = self.cols[:, sb0 + 1:sb0 + 2]
        r = self.cols[:, sb0 + 2:sb0 + 3]
        v = self.cols[:, sb0 + 3:sb0 + 4]
        scr = self.bv(57344, 4096, F32)
        E.op("vector", lambda e: e.reduce_sum(m[0:sz, :], buf[0:sz, :], axis=AX), "vec")
        E.op("vector", lambda e: e.tensor_scalar_mul(m[0:sz, :], m[0:sz, :],
                                                     1.0 / D_MODEL), "vec")
        E.wait("scalar", "vec", E.now("vec"))
        E.op("scalar", lambda e: e.activation(
            scr[0:sz, :], buf[0:sz, :], AF.Square, accum_out=q[0:sz, :]), "act")
        E.wait("vector", "act", E.now("act"))
        E.op("vector", lambda e: e.tensor_tensor(
            v[0:sz, :], m[0:sz, :], m[0:sz, :], ALU.mult), "vec")
        E.op("vector", lambda e: e.scalar_tensor_tensor(
            v[0:sz, :], q[0:sz, :], 1.0 / D_MODEL, v[0:sz, :],
            ALU.mult, ALU.subtract), "vec")
        E.wait("scalar", "vec", E.now("vec"))
        E.op("scalar", lambda e: e.activation(r[0:sz, :], v[0:sz, :], AF.Ln,
                                              bias=EPS), "act")
        E.op("scalar", lambda e: e.activation(r[0:sz, :], r[0:sz, :], AF.Exp,
                                              scale=-0.5), "act")
        return m, q, r, v

    def emit_all(self):
        E = self.E
        n = self.stages
        steps = [
            self.ph_consts,
            self.ph_ln1,
            lambda: setattr(self, "_f", self.ph_fill_act(self.d["d_xn1"], NT, None, E.now("st"))),
            lambda: self.ph_ffnq(NT, self.d["d_f1"], self._f),
            self.ph_xa,
            lambda: setattr(self, "_f", self.ph_fill_act(self.d["d_t1"], NT, 0, E.now("st"))),
            lambda: self.ph_inproj(self._f),
            self.ph_scan,
            self.ph_exchange_z,
            self.ph_combine,
            self.ph_correction,
            self.ph_gating,
            self.ph_outproj,
            self.ph_h,
            lambda: setattr(self, "_f", self.ph_fill_act(self.d["d_h2n"], 1024, 16, E.now("st"))),
            lambda: self.ph_swiglu(self._f),
            self.ph_xb,
            lambda: setattr(self, "_f", self.ph_fill_act(self.d["d_xn3"], 1024, None, E.now("st"))),
            lambda: self.ph_ffnq(1024, self.d["d_f2"], self._f),
            self.ph_final,
        ]
        for i, s in enumerate(steps):
            if i >= n:
                break
            s()
        self.ph_debug()
        self.wait_stores("sync")

    def ph_consts(self):
        E, d = self.E, self.d
        E.dma("sync", self.consts[:, :], d["consts_in"][:, :])
        E.dma("sync", self.eye_f[:, :], d["eye_f_in"][:, :])
        E.dma("sync", self.onesf[:, :], d["ones_f"][:, :])
        E.dma("sync", self.ccols[:, :], d["ccols_in"][:, :])
        E.dma("sync", self.dcol[:, 0:2], d["dcol_in"][:, :])
        E.dma("sync", self.msel9[:, :], d["msel_in"][:, :])
        al = E.now("al")
        E.wait("scalar", "al", al)
        E.op("scalar", lambda e: e.activation(self.dcol[:, 2:3], self.dcol[:, 1:2],
                                              AF.Exp), "act")
        E.wait("vector", "act", E.now("act"))
        E.op("vector", lambda e: e.tensor_scalar_mul(self.dcol[:, 3:4],
                                                     self.dcol[:, 2:3], -1.0), "vec")
        E.wait("vector", "al", al)
        E.op("vector", lambda e: e.tensor_scalar_mul(self.ccols[:, 234:250],
                                                     self.ccols[:, 152:168], 0.5), "vec")
        E.op("vector", lambda e: e.memset(self.zbuf[:, :], 0.0), "vec")
        E.dma("sync", self.ldece[8:9, 0:64], self.d["ones_f"][0:1, 0:64], sem="ts")
        self.c_init = (E.now("vec"), E.now("act"))

    def ph_ln1(self):
        E, d = self.E, self.d
        SA = self.bv(49152, 4096, F32)
        SB = self.bv(53248, 4096, F32)
        B1 = self.bv(61440, 2048)
        E.wait("vector", "vec", self.c_init[0])
        for i, (o, sz) in enumerate(T9):
            buf = SA if i % 2 == 0 else SB
            E.wait("sync", "act", E.now("act"))
            E.wait("sync", "vec", E.now("vec"))
            av = E.dma("sync", buf[0:sz, :], d["x_in"][o:o + sz, :])
            E.wait("vector", "al", av)
            m, q, r, v = self.ln_stats(buf, sz, 84)
            E.wait("vector", "act", E.now("act"))
            E.wait("vector", "ts", E.now("ts"))
            E.op("vector", lambda e, b=buf, sz=sz, m=m, r=r: e.tensor_scalar(
                B1[0:sz, :], b[0:sz, :], m[0:sz, :], r[0:sz, :],
                ALU.subtract, ALU.mult), "vec")
            E.wait("sync", "vec", E.now("vec"))
            E.dma("sync", d["d_xn1"][o:o + sz, :], B1[0:sz, :], sem="ts")

    def ph_fill_act(self, src_tm, n_tok, colw, wait_st):
        E = self.E
        self.wait_stores("sync")
        E.wait("sync", "mm", E.now("mm"))
        for kc in range(KD):
            av = E.dma("sync", self.act[:, kc * NT:kc * NT + n_tok],
                       src_tm[0:n_tok, kc * 128:kc * 128 + 128], transpose=True)
        E.wait("vector", "al", av)
        if colw is not None:
            for kc in range(KD):
                E.op("vector", lambda e, k=kc, n=n_tok: e.tensor_scalar_mul(
                    self.act[:, k * NT:k * NT + n], self.act[:, k * NT:k * NT + n],
                    self.ccols[:, colw + k:colw + k + 1]), "vec")
        return (E.now("al"), E.now("vec"))

    def ph_ffnq(self, n_tok, dst_slab, fills):
        E, d = self.E, self.d
        NH = HID // 128
        sp = _splits(n_tok)
        al_f, vec_f = fills
        ones_c = self.consts[:, 448:449]
        neg128 = self.consts[0:1, 128:256]
        pre_f = [("al", al_f), ("vec", vec_f)]

        for (o, sz) in sp:
            pairs = [(ones_c, self.act[:, kc * NT + o:kc * NT + o + sz])
                     for kc in range(KD)]
            s, mmv = self.job(pairs, lambda s, sz=sz: self.ps[s][0:1, 0:sz], pre=pre_f)
            E.wait("vector", "mm", mmv)
            E.op("vector", lambda e, s=s, o=o, sz=sz: e.tensor_copy(
                self.rowsb[0:1, o:o + sz], self.ps[s][0:1, 0:sz]), "vec")
            self.edone("vector", s)
        vrow = E.now("vec")

        GU = self.bv(0, NH * n_tok)
        self._sil_rd = {}
        self._sgu_pend = {}
        self._phv = E.now("vec")
        pan_mm = {0: E.now("mm"), 1: E.now("mm"), 2: E.now("mm"), 3: E.now("mm")}
        for hc in range(NH):
            p = hc % 2
            gp_ = self.wst[:, p * 4096:p * 4096 + 2048]
            up_ = self.wst[:, p * 4096 + 2048:p * 4096 + 4096]
            wlg = self.wload(gp_.rearrange("p (k q) -> p k q", q=128),
                             d["wq_g"][hc].transpose([1, 0, 2]), pan_mm[2 * p],
                             sem=f"w{2 * p}")
            wlu = self.wload(up_.rearrange("p (k q) -> p k q", q=128),
                             d["wq_u"][hc].transpose([1, 0, 2]), pan_mm[2 * p + 1],
                             sem=f"w{2 * p + 1}")
            for si, (o, sz) in enumerate(sp):
                gpairs = [(gp_[:, kc * 128:kc * 128 + 128],
                           self.act[:, kc * NT + o:kc * NT + o + sz])
                          for kc in range(KD)] + \
                         [(neg128, self.rowsb[0:1, o:o + sz])]
                upairs = [(up_[:, kc * 128:kc * 128 + 128],
                           self.act[:, kc * NT + o:kc * NT + o + sz])
                          for kc in range(KD)] + \
                         [(neg128, self.rowsb[0:1, o:o + sz])]
                prew = pre_f + [wlg, ("vec", vrow)]
                sA, mmA = self.job(gpairs, lambda s, sz=sz: self.ps[s][:, 0:sz],
                                   pre=prew)
                sB, mmB = self.job(upairs, lambda s, sz=sz: self.ps[s][:, 0:sz],
                                   pre=[wlu])
                pan_mm[2 * p] = mmA
                pan_mm[2 * p + 1] = mmB
                sgc = self.ccols[:, 64 + hc:64 + hc + 1]
                suc = self.ccols[:, 108 + hc:108 + hc + 1]
                sl = self.silub[:, o:o + sz]
                E.wait("scalar", "mm", mmA)
                E.wait("scalar", "vec", self._sil_rd.get(o, 0))
                E.op("scalar", lambda e, sA=sA, sz=sz, sgc=sgc, sl=sl: e.activation(
                    sl, self.ps[sA][:, 0:sz], AF.Silu, scale=sgc), "act")
                self.edone("scalar", sA)
                av = E.now("act")
                E.wait("vector", "mm", mmB)
                E.wait("vector", "act", av)
                gu_sl = GU[:, hc * n_tok + o:hc * n_tok + o + sz]
                E.op("vector", lambda e, sB=sB, sz=sz, suc=suc, sl=sl, g=gu_sl:
                     e.scalar_tensor_tensor(g, self.ps[sB][:, 0:sz], suc, sl,
                                            ALU.mult, ALU.mult), "vec")
                self._sil_rd[o] = E.now("vec")
                self.edone("vector", sB)
                guv = E.now("vec")
                rs = 4 + si
                if hc == 0:
                    self.acq(rs)
                else:
                    pg, pgu, psz = self._sgu_pend[si]
                    E.wait("tensor", "vec", pg)
                    E.op("tensor", lambda e, rs=rs, g=pgu, sz=psz, st=(hc == 1):
                         e.matmul(self.ps[rs][0:1, 0:sz], ones_c, g,
                                  start=st, stop=False), "mm")
                self._sgu_pend[si] = (guv, gu_sl, sz)
        for si, (o, sz) in enumerate(sp):
            pg, pgu, psz = self._sgu_pend[si]
            E.wait("tensor", "vec", pg)
            E.op("tensor", lambda e, rs=4 + si, g=pgu, sz=psz:
                 e.matmul(self.ps[rs][0:1, 0:sz], ones_c, g,
                          start=False, stop=True), "mm")
        mm_sgu = E.now("mm")
        E.wait("vector", "mm", mm_sgu)
        for si, (o, sz) in enumerate(sp):
            E.op("vector", lambda e, si=si, o=o, sz=sz: e.tensor_copy(
                self.rowsb[0:1, NT + o:NT + o + sz], self.ps[4 + si][0:1, 0:sz]),
                "vec")
            self.edone("vector", 4 + si)
        vgu = E.now("vec")

        self._phv = E.now("vec")
        pan_mm2 = {0: mm_sgu, 1: mm_sgu}
        OS = self.wst[:, 0:2048]
        ost_st = {0: 0, 1: 0}
        for dc in range(KD):
            p = dc % 2
            pan = self.act[:, p * 5632:p * 5632 + 5632]
            wld = self.wload(pan.rearrange("p (k q) -> p k q", q=128),
                             d["wq_d"][dc].transpose([1, 0, 2]), pan_mm2[p],
                             sem=f"w{p}")
            for si, (o, sz) in enumerate(sp):
                pairs = [(pan[:, hc * 128:hc * 128 + 128],
                          GU[:, hc * n_tok + o:hc * n_tok + o + sz])
                         for hc in range(NH)] + \
                        [(neg128, self.rowsb[0:1, NT + o:NT + o + sz])]
                s, mmv = self.job(pairs, lambda s, sz=sz: self.ps[s][:, 0:sz],
                                  pre=[wld, ("vec", vgu)])
                pan_mm2[p] = mmv
                sdc = self.ccols[:, 234 + dc:234 + dc + 1]
                ost = OS[:, (si % 2) * 1024:(si % 2) * 1024 + sz]
                E.wait("vector", "mm", mmv)
                E.wait("vector", f"fs{si % 2}", ost_st[si % 2])
                E.op("vector", lambda e, s=s, sz=sz, sdc=sdc, ost=ost:
                     e.tensor_scalar_mul(ost, self.ps[s][:, 0:sz], sdc), "vec")
                self.edone("vector", s)
                E.wait("sync", "vec", E.now("vec"))
                E.dma("sync", dst_slab[dc * 128:dc * 128 + 128, o:o + sz], ost,
                      sem=f"fs{si % 2}")
                ost_st[si % 2] = E.now(f"fs{si % 2}")

    def ph_xa(self):
        E, d = self.E, self.d
        SA = self.bv(49152, 4096, F32)
        SB = self.bv(53248, 4096, F32)
        B1 = self.bv(61440, 2048)
        B2 = self.bv(63488, 2048)
        self.wait_stores("sync")
        for i, (o, sz) in enumerate(T9):
            buf = SA if i % 2 == 0 else SB
            E.wait("sync", "act", E.now("act"))
            E.wait("sync", "vec", E.now("vec"))
            E.wait("sync", "ts", E.now("ts"))
            av = E.dma("sync", buf[0:sz, :], d["x_in"][o:o + sz, :])
            avT = E.dma("sync", B1[0:128, :], d["d_f1"][:, o:o + 128], transpose=True)
            E.wait("vector", "al", avT)
            E.op("vector", lambda e, b=buf, sz=sz: e.tensor_tensor(
                b[0:sz, :], b[0:sz, :], B1[0:sz, :], ALU.add), "vec")
            E.wait("sync", "vec", E.now("vec"))
            E.dma("sync", d["d_xa"][o:o + sz, :], buf[0:sz, :], sem="ts")
            m, q, r, v = self.ln_stats(buf, sz, 4 * i)
            w = self.smallf[:, 1024 + i:1024 + i + 1]
            E.wait("vector", "act", E.now("act"))
            E.op("vector", lambda e, sz=sz, r=r, w=w: e.tensor_tensor(
                w[0:sz, :], r[0:sz, :], r[0:sz, :], ALU.mult), "vec")
            E.op("vector", lambda e, sz=sz, v=v, w=w: e.tensor_tensor(
                w[0:sz, :], w[0:sz, :], v[0:sz, :], ALU.mult), "vec")
            E.wait("scalar", "vec", E.now("vec"))
            E.op("scalar", lambda e, sz=sz, w=w: e.activation(
                w[0:sz, :], w[0:sz, :], AF.Ln, bias=EPS), "act")
            E.op("scalar", lambda e, sz=sz, w=w: e.activation(
                w[0:sz, :], w[0:sz, :], AF.Exp, scale=-0.5), "act")
            E.wait("vector", "act", E.now("act"))
            E.wait("vector", "ts", E.now("ts"))
            E.op("vector", lambda e, sz=sz, r=r, w=w: e.tensor_tensor(
                w[0:sz, :], w[0:sz, :], r[0:sz, :], ALU.mult), "vec")
            E.op("vector", lambda e, b=buf, sz=sz, m=m, w=w: e.tensor_scalar(
                B2[0:sz, :], b[0:sz, :], m[0:sz, :], w[0:sz, :],
                ALU.subtract, ALU.mult), "vec")
            E.wait("sync", "vec", E.now("vec"))
            E.dma("sync", d["d_t1"][o:o + sz, :], B2[0:sz, :], sem="ts")

    def ph_inproj(self, fills):
        E, d = self.E, self.d
        sp = _splits(NT)
        al_f, vec_f = fills
        pre_f = [("al", al_f), ("vec", vec_f)]
        RAW = [self.bv(0, 1040), self.bv(1040, 1040)]
        self._phv = E.now("vec")
        raw_mm = {0: E.now("mm"), 1: E.now("mm")}
        pan_mm = {0: E.now("mm"), 1: E.now("mm")}
        conv_mm = {0: 0, 1: 0}
        cst_st = {0: 0, 1: 0}
        pend_conv = None
        for fc in range(NFC):
            p = fc % 2
            pan = self.wst[:, p * 2560:p * 2560 + 2048]
            dia = self.wst[:, p * 2560 + 2048:p * 2560 + 2560]
            wlp = self.wload(pan.rearrange("p (k q) -> p k q", q=128),
                             d["w_xbc"][fc].transpose([1, 0, 2]), raw_mm[p],
                             sem=f"w{p}")
            wld = self.wload(dia.rearrange("p (k q) -> p k q", q=128),
                             d["w_conv"][fc].transpose([1, 0, 2]), conv_mm[p],
                             sem=f"w{2 + p}")
            raw = RAW[p]
            rawv = []
            for si, (o, sz) in enumerate(sp):
                pairs = [(pan[:, kc * 128:kc * 128 + 128],
                          self.act[:, kc * NT + o:kc * NT + o + sz])
                         for kc in range(KD)]
                s, mmv = self.job(pairs, lambda s, sz=sz: self.ps[s][:, 0:sz],
                                  pre=pre_f + [wlp])
                raw_mm[p] = mmv
                E.wait("vector", "mm", mmv)
                E.wait("vector", "mm", conv_mm[p])
                E.op("vector", lambda e, s=s, o=o, sz=sz, raw=raw: e.tensor_copy(
                    raw[:, o:o + sz], self.ps[s][:, 0:sz]), "vec")
                self.edone("vector", s)
            vr = E.now("vec")

            def emit_conv(vr=None, raw=None, dia=None, wld=None, fc=None, p=None):
                for (o, sz) in _splits(1024):
                    pairs = [(dia[:, j * 128:j * 128 + 128],
                              raw[:, o + 13 + j:o + 13 + j + sz])
                             for j in range(K_CONV)]
                    s, mmv = self.job(pairs, lambda s, sz=sz: self.ps[s][:, 0:sz],
                                      pre=[("vec", vr), wld])
                    conv_mm[p] = mmv
                    cp = 0 if o == 0 else 1
                    cbc = self.ccols[:, 168 + fc:168 + fc + 1]
                    cst = (self.silub[:, 0:sz] if o == 0
                           else self.silub[:, 512:512 + sz])
                    E.wait("scalar", "mm", mmv)
                    E.wait("scalar", f"xs{cp}", cst_st[cp])
                    E.op("scalar", lambda e, s=s, sz=sz, cbc=cbc, cst=cst:
                         e.activation(cst, self.ps[s][:, 0:sz], AF.Silu, bias=cbc),
                         "act")
                    self.edone("scalar", s)
                    E.wait("sync", "act", E.now("act"))
                    E.dma("sync", d["d_xbc"][fc * 128:fc * 128 + 128, o:o + sz],
                          cst, sem=f"xs{cp}")
                    cst_st[cp] = E.now(f"xs{cp}")

            if pend_conv is not None:
                emit_conv(**pend_conv)
            pend_conv = dict(vr=vr, raw=raw, dia=dia, wld=wld, fc=fc, p=p)
        if pend_conv is not None:
            emit_conv(**pend_conv)
        DT = self.bv(40960, 10400, F32)
        wdt = self.wst[:, 5120 + 0:5120 + 1024]
        wl = self.wload(wdt.rearrange("p (k q) -> p k q", q=64),
                        d["w_dt"][:, :, :].transpose([1, 0, 2]), E.now("mm"),
                        sem="w2")
        for si, (o, sz) in enumerate(sp):
            pairs = [(wdt[:, kc * 64:kc * 64 + 64],
                      self.act[:, kc * NT + o:kc * NT + o + sz])
                     for kc in range(KD)]
            s, mmv = self.job(pairs, lambda s, sz=sz: self.ps[s][0:64, 0:sz],
                              pre=pre_f + [wl])
            E.wait("scalar", "mm", mmv)
            E.op("scalar", lambda e, s=s, o=o, sz=sz: e.activation(
                DT[0:64, o:o + sz], self.ps[s][0:64, 0:sz], AF.Exp,
                bias=self.dcol[:, 0:1]), "act")
            self.edone("scalar", s)
            E.op("scalar", lambda e, o=o, sz=sz: e.activation(
                DT[0:64, o:o + sz], DT[0:64, o:o + sz], AF.Ln, bias=1.0), "act")
        E.op("scalar", lambda e: e.activation(DT[0:64, 1040:2080], DT[0:64, 0:1040],
                                              AF.Ln), "act")
        adt = E.now("act")
        E.wait("vector", "act", adt)
        E.op("vector", lambda e: e.tensor_scalar_mul(
            DT[0:64, 2080:3120], DT[0:64, 0:1040], self.dcol[:, 3:4]), "vec")
        self.dt_ready = (E.now("vec"), E.now("act"))

    def ph_scan(self):
        E, d = self.E, self.d
        DT = self.bv(40960, 10400, F32)
        ACU = lambda c: DT[0:64, 3120 + c * 128:3120 + (c + 1) * 128]
        LMA = DT[0:64, 4144:5168]
        OH = self.bv(0, 16384, F32)
        XT = self.bv(16384, 4096)
        XW = self.bv(20480, 4096)
        EXD = self.bv(24576, 1024)
        SCO = self.bv(25600, 1024)
        ARG = self.bv(30720, 2048, F32)
        STATE = self.bv(26624, 4096)
        CTI = self.bv(32768, 2048, F32)
        WX = self.bv(34816, 1024, F32)
        CFM = self.bv(35840, 1024)
        YSB = self.bv(36864, 4096)

        E.wait("sync", "vec", self.dt_ready[0])
        E.wait("sync", "mm", E.now("mm"))
        self.wait_stores("sync")
        oh_al = E.dma("sync", OH[0:64, :], d["onehot"][:, :], sem="oh")
        E.wait("tensor", "oh", oh_al)
        E.wait("vector", "oh", oh_al)
        E.op("vector", lambda e: e.memset(STATE[:, :], 0.0), "vec")
        E.op("vector", lambda e: e.memset(self.tots[:, :], 0.0), "vec")
        for c in range(NCH):
            E.op("vector", lambda e, c=c: e.tensor_tensor_scan(
                ACU(c), DT[0:64, 2080 + 16 + c * 128:2080 + 16 + (c + 1) * 128],
                self.zbuf[:, :], 0.0, ALU.add, ALU.add), "vec")
            E.op("vector", lambda e, c=c: e.reduce_sum(
                self.smallf[0:64, 1024 + c:1024 + c + 1],
                DT[0:64, 2080 + 16 + c * 128:2080 + 16 + (c + 1) * 128],
                axis=AX), "vec")
        vq = E.now("vec")
        E.op("vector", lambda e: e.tensor_tensor(
            LMA, DT[0:64, 1040 + 16:1040 + 16 + 1024], DT[0:64, 3120:4144],
            ALU.subtract), "vec")
        E.wait("tensor", "vec", vq)
        self.acq(6)
        for c in range(NCH):
            E.op("tensor", lambda e, c=c: e.transpose(
                self.ps[6][0:1, c * 64:c * 64 + 64],
                self.smallf[0:64, 1024 + c:1024 + c + 1],
                self.eye_f[0:64, 0:64]), "mm")
        mmv = E.now("mm")
        E.wait("vector", "mm", mmv)
        E.op("vector", lambda e: e.tensor_copy(self.tots[0:1, 0:512],
                                               self.ps[6][0:1, 0:512]), "vec")
        self.edone("vector", 6)
        for c in range(1, NCH):
            E.op("vector", lambda e, c=c: e.tensor_tensor(
                self.tots[0:1, 512 + c * 64:512 + (c + 1) * 64],
                self.tots[0:1, 512 + (c - 1) * 64:512 + c * 64],
                self.tots[0:1, (c - 1) * 64:c * 64], ALU.add), "vec")
        E.op("vector", lambda e: e.tensor_tensor(
            self.tots[0:1, 1088:1152], self.tots[0:1, 512 + 448:512 + 512],
            self.tots[0:1, 448:512], ALU.add), "vec")
        vpre = E.now("vec")

        vch_prev = E.now("vec")
        st_prev = E.now("ys")
        for c in range(NCH):
            self.acq(6)
            E.wait("tensor", "vec", vpre)
            E.op("tensor", lambda e, c=c: e.transpose(
                self.ps[6][0:128, 0:64], ACU(c), self.eye_f[0:64, 0:64]), "mm")
            E.op("tensor", lambda e, c=c: e.transpose(
                self.ps[6][0:128, 64:128], LMA[:, c * 128:(c + 1) * 128],
                self.eye_f[0:64, 0:64]), "mm")
            mmv = E.now("mm")
            E.wait("vector", "mm", mmv)
            E.op("vector", lambda e, c=c: e.tensor_copy(
                CTI[:, c * 128:(c + 1) * 128], self.ps[6][:, 0:128]), "vec")
            self.edone("vector", 6)
            vc = E.now("vec")
            E.wait("scalar", "vec", vc)
            E.op("scalar", lambda e, c=c: e.activation(
                WX[:, c * 64:(c + 1) * 64], CTI[:, c * 128:c * 128 + 64], AF.Exp),
                "act")
            self.acq(6)
            E.op("tensor", lambda e, c=c: e.matmul(
                self.ps[6][:, 0:64], self.onesf[:, :],
                self.tots[0:1, c * 64:(c + 1) * 64], start=True, stop=True), "mm")
            mmv = E.now("mm")
            E.wait("vector", "mm", mmv)
            E.wait("vector", "vec", vc)
            E.op("vector", lambda e, c=c: e.tensor_tensor(
                self.smallf[:, 0:64], self.ps[6][:, 0:64],
                CTI[:, c * 128 + 64:c * 128 + 128], ALU.add), "vec")
            vv = E.now("vec")
            E.wait("scalar", "vec", vv)
            E.wait("scalar", "vec", vch_prev)
            E.op("scalar", lambda e: e.activation(self.ebcT[:, :], self.ps[6][:, 0:64],
                                                  AF.Exp), "act")
            self.edone("scalar", 6)
            E.op("scalar", lambda e: e.activation(self.wtile[:, :],
                                                  self.smallf[:, 0:64], AF.Exp), "act")
            a_pre = E.now("act")
            E.wait("sync", "mm", E.now("mm"))
            E.wait("sync", "vec", vch_prev)
            alx = E.dma("sync", XT[:, :], d["d_xbc"][0:4096, c * 128:(c + 1) * 128],
                        transpose=True)
            alb = E.dma("sync", self.bfm[:, :],
                        d["d_xbc"][4096:4224, c * 128:(c + 1) * 128])
            alc = E.dma("sync", CFM[:, c * 128:(c + 1) * 128],
                        d["d_xbc"][4224:4352, c * 128:(c + 1) * 128])
            albt = E.dma("sync", self.btm[:, :],
                         d["d_xbc"][4096:4224, c * 128:(c + 1) * 128], transpose=True)
            alx = alb = alc = albt
            E.wait("vector", "act", a_pre)
            E.wait("vector", "al", albt)
            for g in range(8):
                E.op("vector", lambda e, g=g: e.tensor_tensor(
                    XW[:, g * 512:(g + 1) * 512].rearrange("p (a b) -> p a b", b=64),
                    XT[:, g * 512:(g + 1) * 512].rearrange("p (a b) -> p a b", b=64),
                    self.wtile[:, g * 8:(g + 1) * 8].unsqueeze(2)
                        .broadcast_to([128, 8, 64]), ALU.mult), "vec")
            vxw = E.now("vec")
            self.acq(4)
            E.wait("tensor", "al", max(alb, alc))
            E.op("tensor", lambda e, c=c: e.matmul(
                self.ps[4][:, 0:128], self.bfm[:, :], CFM[:, c * 128:(c + 1) * 128],
                start=True, stop=True), "mm")
            mmv = E.now("mm")
            E.wait("vector", "mm", mmv)
            E.op("vector", lambda e: e.tensor_tensor(
                self.bcm[:, :], self.ps[4][:, 0:128], self.consts[:, 256:384],
                ALU.mult), "vec")
            self.edone("vector", 4)
            vbc = E.now("vec")
            mmy_prev = E.now("mm")
            vsc_prev = E.now("vec")
            aexp_prev = E.now("act")
            for g in range(8):
                dsl = (0, 1) if g % 2 == 0 else (2, 3)
                for k in range(8):
                    h = 8 * g + k
                    s = dsl[k // 4]
                    if k % 4 == 0:
                        self.acq(s)
                    E.op("tensor", lambda e, h=h, k=k, s=s, c=c: e.matmul(
                        self.ps[s][:, (k % 4) * 128:(k % 4) * 128 + 128],
                        OH[0:64, h * 128:(h + 1) * 128], ACU(c),
                        start=True, stop=True), "mm" if k == 7 else None)
                mmv = E.now("mm")
                E.wait("vector", "mm", mmv)
                E.wait("vector", "act", aexp_prev)
                for k in range(8):
                    h = 8 * g + k
                    s = dsl[k // 4]
                    E.op("vector", lambda e, h=h, k=k, s=s, c=c: e.tensor_scalar(
                        ARG[:, k * 128:(k + 1) * 128],
                        self.ps[s][:, (k % 4) * 128:(k % 4) * 128 + 128],
                        CTI[:, c * 128 + 64 + h:c * 128 + 64 + h + 1], 30.0,
                        ALU.add, ALU.min), "vec")
                    if k % 4 == 3:
                        self.edone("vector", s)
                varg = E.now("vec")
                E.wait("scalar", "vec", varg)
                E.op("scalar", lambda e: e.activation(EXD[:, :], ARG[:, :],
                                                      AF.Exp), "act")
                aexp = E.now("act")
                aexp_prev = aexp
                E.wait("vector", "act", aexp)
                E.wait("vector", "vec", vbc)
                E.wait("vector", "mm", mmy_prev)
                if g == 0:
                    E.wait("vector", "ys", st_prev)
                E.op("vector", lambda e: e.tensor_tensor(
                    SCO[:, :].rearrange("p (a b) -> p a b", b=128),
                    EXD[:, :].rearrange("p (a b) -> p a b", b=128),
                    self.bcm[:, :].unsqueeze(1).broadcast_to([128, 8, 128]),
                    ALU.mult), "vec")
                vsc = E.now("vec")
                self.acq(5)
                for k in range(8):
                    E.op("tensor", lambda e, g=g, k=k, c=c: e.matmul(
                        self.ps[5][:, k * 64:(k + 1) * 64],
                        CFM[:, c * 128:(c + 1) * 128],
                        STATE[:, g * 512 + k * 64:g * 512 + (k + 1) * 64],
                        start=True, stop=True), None)
                self.acq(4)
                E.wait("tensor", "vec", vsc)
                for k in range(8):
                    E.op("tensor", lambda e, g=g, k=k: e.matmul(
                        self.ps[4][:, k * 64:(k + 1) * 64],
                        SCO[:, k * 128:(k + 1) * 128],
                        XT[:, g * 512 + k * 64:g * 512 + (k + 1) * 64],
                        start=True, stop=True), "mm" if k == 7 else None)
                mmy = E.now("mm")
                mmy_prev = mmy
                E.wait("vector", "mm", mmy)
                E.op("vector", lambda e, g=g, c=c: e.tensor_tensor(
                    self.smallf[:, 0:512].rearrange("p (a b) -> p a b", b=64),
                    self.ps[5][:, 0:512].rearrange("p (a b) -> p a b", b=64),
                    WX[:, c * 64:(c + 1) * 64][:, g * 8:(g + 1) * 8].unsqueeze(2)
                        .broadcast_to([128, 8, 64]), ALU.mult), "vec")
                self.edone("vector", 5)
                E.op("vector", lambda e, g=g: e.tensor_tensor(
                    YSB[:, g * 512:(g + 1) * 512], self.smallf[:, 0:512],
                    self.ps[4][:, 0:512], ALU.add), "vec")
                self.edone("vector", 4)
                E.op("vector", lambda e, g=g: e.tensor_tensor(
                    self.smallf[:, 512:1024].rearrange("p (a b) -> p a b", b=64),
                    STATE[:, g * 512:(g + 1) * 512].rearrange("p (a b) -> p a b", b=64),
                    self.ebcT[:, g * 8:(g + 1) * 8].unsqueeze(2)
                        .broadcast_to([128, 8, 64]), ALU.mult), "vec")
                vdec = E.now("vec")
                self.acq(7)
                E.wait("tensor", "al", albt)
                for k in range(8):
                    E.op("tensor", lambda e, g=g, k=k: e.matmul(
                        self.ps[7][:, k * 64:(k + 1) * 64], self.btm[:, :],
                        XW[:, g * 512 + k * 64:g * 512 + (k + 1) * 64],
                        start=True, stop=True), "mm" if k == 7 else None)
                mms = E.now("mm")
                E.wait("vector", "mm", mms)
                E.op("vector", lambda e, g=g: e.tensor_tensor(
                    STATE[:, g * 512:(g + 1) * 512], self.smallf[:, 512:1024],
                    self.ps[7][:, 0:512], ALU.add), "vec")
                self.edone("vector", 7)
                vsc_prev = E.now("vec")
            vch = E.now("vec")
            vch_prev = vch
            E.wait("sync", "vec", vch)
            E.dma("sync", d["d_y"][c * 128:(c + 1) * 128, :], YSB[:, :], sem="ys")
            st_prev = E.now("ys")
        E.wait("sync", "vec", E.now("vec"))
        E.dma("sync", d["cc_s_in"][:, :], STATE[:, :], sem="ys")
        E.dma("sync", d["cc_l_in"][:, :], self.tots[0:1, 1088:1152], sem="ys")
        self.pack_st = E.now("ys")

    def ph_exchange_z(self):
        E, d = self.E, self.d
        E.wait("gpsimd", "ys", self.pack_st)
        E.op("gpsimd", lambda e: e.collective_compute(
            "AllGather", ALU.bypass, replica_groups=[list(range(8))],
            ins=[d["cc_s_in"][:, :]], outs=[d["cc_s_out"][:, :]]), "cc")
        E.op("gpsimd", lambda e: e.collective_compute(
            "AllGather", ALU.bypass, replica_groups=[list(range(8))],
            ins=[d["cc_l_in"][:, :]], outs=[d["cc_l_out"][:, :]]), "cc")
        SZ = self.bv(0, 32768)
        self._phv = E.now("vec")
        pan_mm = {0: E.now("mm"), 1: E.now("mm")}
        for zc in range(D_INNER // 128):
            p = zc % 2
            pan = self.wst[:, p * 2048:p * 2048 + 2048]
            wl = self.wload(pan.rearrange("p (k q) -> p k q", q=128),
                            d["w_z"][zc].transpose([1, 0, 2]), pan_mm[p],
                            sem=f"w{p}")
            for (o, sz) in _splits(1024):
                pairs = [(pan[:, kc * 128:kc * 128 + 128],
                          self.act[:, kc * NT + HALO + o:kc * NT + HALO + o + sz])
                         for kc in range(KD)]
                s, mmv = self.job(pairs, lambda s, sz=sz: self.ps[s][:, 0:sz],
                                  pre=[wl, ("ys", self.pack_st)])
                pan_mm[p] = mmv
                E.wait("scalar", "mm", mmv)
                E.op("scalar", lambda e, s=s, zc=zc, o=o, sz=sz: e.activation(
                    SZ[:, zc * 1024 + o:zc * 1024 + o + sz], self.ps[s][:, 0:sz],
                    AF.Silu), "act")
                self.edone("scalar", s)
        self.z_done = E.now("act")

    def ph_gating(self):
        E, d = self.E, self.d
        SZ = self.bv(0, 32768)
        VP = self.bv(32768, 32768)
        E.wait("sync", "ys", self.corr_st)
        self.wait_stores("sync")
        self.acq(4)
        self.acq(5)
        gn_mm = [E.now("mm"), E.now("mm")]
        v_done = [0, 0]
        for fc in range(32):
            p = fc % 2
            YT = self.wst[:, p * 4096:p * 4096 + 1024]
            XS = self.wst[:, p * 4096 + 1024:p * 4096 + 2048]
            T1 = self.wst[:, p * 4096 + 2048:p * 4096 + 3072]
            E.wait("sync", "vec", v_done[p])
            a1 = E.dma("sync", YT[:, :], d["d_y"][0:1024, fc * 128:(fc + 1) * 128],
                       sem=f"g{p}", transpose=True)
            a2 = E.dma("sync", XS[:, :], d["d_xbc"][fc * 128:(fc + 1) * 128, :],
                       sem=f"g{p}")
            E.wait("vector", f"g{p}", a2)
            E.wait("vector", "act", self.z_done)
            E.wait("vector", "mm", gn_mm[p])
            dc = self.ccols[:, 202 + fc:202 + fc + 1]
            gw = self.ccols[:, 32 + fc:32 + fc + 1]
            vpf = VP[:, fc * 1024:(fc + 1) * 1024]
            E.op("vector", lambda e, XS=XS, YT=YT, T1=T1, dc=dc:
                 e.scalar_tensor_tensor(T1[:, :], XS[:, :], dc, YT[:, :],
                                        ALU.mult, ALU.add), "vec")
            E.op("vector", lambda e, T1=T1, vpf=vpf, fc=fc: e.tensor_tensor(
                vpf, T1[:, :], SZ[:, fc * 1024:(fc + 1) * 1024], ALU.mult), "vec")
            E.op("vector", lambda e, vpf=vpf, T1=T1: e.tensor_tensor(
                T1[:, :], vpf, vpf, ALU.mult), "vec")
            vsq = E.now("vec")
            E.op("vector", lambda e, vpf=vpf, gw=gw: e.tensor_scalar_mul(
                vpf, vpf, gw), "vec")
            E.wait("tensor", "vec", vsq)
            for si, (o, sz) in enumerate(_splits(1024)):
                E.op("tensor", lambda e, si=si, o=o, sz=sz, T1=T1, fc=fc: e.matmul(
                    self.ps[4 + si][0:1, 0:sz], self.consts[:, 448:449],
                    T1[:, o:o + sz], start=(fc == 0), stop=(fc == 31)),
                    "mm" if si == 1 else None)
            gn_mm[p] = E.now("mm")
            v_done[p] = E.now("vec")
        mmgn = E.now("mm")
        E.wait("scalar", "mm", mmgn)
        for si in range(2):
            E.op("scalar", lambda e, si=si: e.activation(
                self.rows[0:1, si * 512:(si + 1) * 512],
                self.ps[4 + si][0:1, 0:512], AF.Ln, scale=1.0 / D_INNER, bias=EPS),
                "act")
            self.edone("scalar", 4 + si)
            E.op("scalar", lambda e, si=si: e.activation(
                self.rows[0:1, si * 512:(si + 1) * 512],
                self.rows[0:1, si * 512:(si + 1) * 512], AF.Exp, scale=-0.5), "act")
        agn = E.now("act")
        E.wait("tensor", "act", agn)
        self.acq(6)
        for i in range(8):
            E.op("tensor", lambda e, i=i: e.matmul(
                self.ps[6][:, i:i + 1], self.rows[0:1, i * 128:(i + 1) * 128],
                self.eye_f[0:1, 0:1], start=True, stop=True),
                "mm" if i == 7 else None)
        mmv = E.now("mm")
        E.wait("vector", "mm", mmv)
        E.op("vector", lambda e: e.tensor_copy(self.gcol[:, :], self.ps[6][:, 0:8]),
             "vec")
        self.edone("vector", 6)
        self.gn_done = E.now("vec")

    def ph_combine(self):
        E, d = self.E, self.d
        SIN = self.bv(40960, 4096)
        SL = [self.bv(45056, 4096), self.bv(49152, 4096)]
        self.wjrow = self.bv(53248, 8192, F32)
        E.wait("gpsimd", "cc", E.now("cc"))
        all_ = E.dma("gpsimd", self.ldece[0:8, :], d["cc_l_out"][:, :], sem="lde")
        sl_al = [None] * 8
        sl_lastread = [0, 0]
        self.acq(6)
        E.wait("tensor", "lde", all_)
        E.op("tensor", lambda e: e.matmul(self.ps[6][0:64, 0:8], self.ldece[:, :],
                                          self.msel9[:, :], start=True, stop=True),
             "mm")
        mmv = E.now("mm")
        E.wait("scalar", "mm", mmv)
        E.op("scalar", lambda e: e.activation(self.wjT[:, :], self.ps[6][0:64, 0:8],
                                              AF.Exp), "act")
        self.edone("scalar", 6)
        awj = E.now("act")
        E.op("vector", lambda e: e.memset(SIN[:, :], 0.0), "vec")
        E.wait("tensor", "act", awj)
        for j in range(2):
            E.wait("gpsimd", "vec", sl_lastread[j % 2])
            sl_al[j] = E.dma("gpsimd", SL[j % 2][:, :],
                             d["cc_s_out"][j * 128:(j + 1) * 128, :], sem=f"sl{j % 2}")
        for j in range(8):
            self.acq(6)
            E.op("tensor", lambda e, j=j: e.transpose(
                self.ps[6][0:1, 0:64], self.wjT[:, j:j + 1],
                self.eye_f[0:64, 0:64]), "mm")
            mmv = E.now("mm")
            E.wait("vector", "mm", mmv)
            E.op("vector", lambda e: e.tensor_copy(self.rows[0:1, 0:64],
                                                   self.ps[6][0:1, 0:64]), "vec")
            self.edone("vector", 6)
            E.op("vector", lambda e: e.tensor_copy(
                self.wjrow[0:1, :].rearrange("p (a b) -> p a b", b=64),
                self.rows[0:1, 0:64].unsqueeze(2).broadcast_to([1, 64, 64])), "vec")
            vrow = E.now("vec")
            for g in range(8):
                s = g % 2
                self.acq(s)
                E.wait("tensor", "vec", vrow)
                E.op("tensor", lambda e, g=g, s=s: e.matmul(
                    self.ps[s][:, 0:512], self.onesf[:, :],
                    self.wjrow[0:1, g * 512:(g + 1) * 512], start=True, stop=True),
                    "mm")
                mmv = E.now("mm")
                E.wait("vector", "mm", mmv)
                E.wait("vector", f"sl{j % 2}", sl_al[j])
                E.op("vector", lambda e, g=g, s=s, j=j: e.tensor_tensor(
                    self.smallf[:, 0:512], SL[j % 2][:, g * 512:(g + 1) * 512],
                    self.ps[s][:, 0:512], ALU.mult), "vec")
                self.edone("vector", s)
                E.op("vector", lambda e, g=g: e.tensor_tensor(
                    SIN[:, g * 512:(g + 1) * 512], SIN[:, g * 512:(g + 1) * 512],
                    self.smallf[:, 0:512], ALU.add), "vec")
            sl_lastread[j % 2] = E.now("vec")
            if j + 2 < 8:
                E.wait("gpsimd", "vec", sl_lastread[(j + 2) % 2])
                sl_al[j + 2] = E.dma("gpsimd", SL[(j + 2) % 2][:, :],
                                     d["cc_s_out"][(j + 2) * 128:(j + 3) * 128, :],
                                     sem=f"sl{j % 2}")
        self.sin_done = E.now("vec")

    def ph_correction(self):
        E, d = self.E, self.d
        SIN = self.bv(40960, 4096)
        CTI = self.bv(32768, 2048, F32)
        CFM = self.bv(35840, 1024)
        YSB = self.bv(36864, 4096)
        v_prev = E.now("vec")
        for c in range(NCH):
            E.wait("sync", "vec", v_prev)
            E.wait("sync", "ys", E.now("ys"))
            aly = E.dma("sync", YSB[:, :], d["d_y"][c * 128:(c + 1) * 128, :])
            self.acq(6)
            E.op("tensor", lambda e, c=c: e.matmul(
                self.ps[6][:, 0:64], self.onesf[:, :],
                self.tots[0:1, 512 + c * 64:512 + (c + 1) * 64],
                start=True, stop=True), "mm")
            mmv = E.now("mm")
            E.wait("vector", "mm", mmv)
            E.op("vector", lambda e, c=c: e.tensor_tensor(
                self.smallf[:, 1024 + 0:1024 + 64], self.ps[6][:, 0:64],
                CTI[:, c * 128:c * 128 + 64], ALU.add), "vec")
            self.edone("vector", 6)
            E.wait("scalar", "vec", E.now("vec"))
            E.wait("scalar", "vec", v_prev)
            E.op("scalar", lambda e: e.activation(
                self.wtile[:, :], self.smallf[:, 1024:1088], AF.Exp), "act")
            acw = E.now("act")
            for g in range(8):
                s = g % 2
                self.acq(s)
                E.wait("tensor", "vec", self.sin_done)
                for k in range(8):
                    E.op("tensor", lambda e, g=g, k=k, s=s, c=c: e.matmul(
                        self.ps[s][:, k * 64:(k + 1) * 64],
                        CFM[:, c * 128:(c + 1) * 128],
                        SIN[:, g * 512 + k * 64:g * 512 + (k + 1) * 64],
                        start=True, stop=True), "mm" if k == 7 else None)
                mmv = E.now("mm")
                E.wait("vector", "mm", mmv)
                E.wait("vector", "act", acw)
                E.wait("vector", "al", aly)
                E.op("vector", lambda e, g=g, s=s: e.tensor_tensor(
                    self.smallf[:, 0:512].rearrange("p (a b) -> p a b", b=64),
                    self.ps[s][:, 0:512].rearrange("p (a b) -> p a b", b=64),
                    self.wtile[:, g * 8:(g + 1) * 8].unsqueeze(2)
                        .broadcast_to([128, 8, 64]), ALU.mult), "vec")
                self.edone("vector", s)
                E.op("vector", lambda e, g=g: e.tensor_tensor(
                    YSB[:, g * 512:(g + 1) * 512], YSB[:, g * 512:(g + 1) * 512],
                    self.smallf[:, 0:512], ALU.add), "vec")
            v_prev = E.now("vec")
            E.wait("sync", "vec", v_prev)
            E.dma("sync", d["d_y"][c * 128:(c + 1) * 128, :], YSB[:, :], sem="ys")
        self.corr_st = E.now("ys")

    def ph_outproj(self):
        E, d = self.E, self.d
        VP = self.bv(32768, 32768)
        vp_v = E.now("vec")
        self._phv = vp_v
        pan_mm = {0: E.now("mm"), 1: E.now("mm")}
        OS = self.wst[:, 0:2048]
        ost_st = {0: 0, 1: 0}
        for dc in range(KD):
            p = dc % 2
            pan = self.act[:, p * 4096:p * 4096 + 4096]
            wl = self.wload(pan.rearrange("p (k q) -> p k q", q=128),
                            d["w_op"][dc].transpose([1, 0, 2]), pan_mm[p],
                            sem=f"w{p}")
            for si, (o, sz) in enumerate(_splits(1024)):
                pairs = [(pan[:, kc * 128:kc * 128 + 128],
                          VP[:, kc * 1024 + o:kc * 1024 + o + sz])
                         for kc in range(D_INNER // 128)]
                s, mmv = self.job(pairs, lambda s, sz=sz: self.ps[s][:, 0:sz],
                                  pre=[wl, ("vec", vp_v)])
                pan_mm[p] = mmv
                ost = OS[:, (si % 2) * 1024:(si % 2) * 1024 + sz]
                E.wait("vector", "mm", mmv)
                E.wait("vector", f"fs{si % 2}", ost_st[si % 2])
                E.op("vector", lambda e, s=s, sz=sz, ost=ost: e.tensor_copy(
                    ost, self.ps[s][:, 0:sz]), "vec")
                self.edone("vector", s)
                E.wait("sync", "vec", E.now("vec"))
                E.dma("sync", d["d_mo"][dc * 128:dc * 128 + 128, o:o + sz], ost,
                      sem=f"fs{si % 2}")
                ost_st[si % 2] = E.now(f"fs{si % 2}")
        self.mo_st = None

    def ph_h(self):
        E, d = self.E, self.d
        SA = self.bv(49152, 4096, F32)
        SB = self.bv(53248, 4096, F32)
        B1 = self.bv(61440, 2048)
        self.wait_stores("sync")
        E.wait("sync", "vec", self.gn_done)
        for i, (o, sz) in enumerate(T8):
            E.wait("sync", "act", E.now("act"))
            E.wait("sync", "ts", E.now("ts"))
            av = E.dma("sync", SA[0:sz, :], d["d_xa"][HALO + o:HALO + o + sz, :])
            avT = E.dma("sync", B1[0:128, :], d["d_mo"][:, o:o + 128], transpose=True)
            ti = i + 1
            m = self.cols[:, 4 * ti:4 * ti + 1]
            r = self.cols[:, 4 * ti + 2:4 * ti + 3]
            E.wait("vector", "al", max(av, avT))
            E.wait("vector", "ts", E.now("ts"))
            E.op("vector", lambda e, sz=sz, m=m, r=r: e.tensor_scalar(
                SB[0:sz, :], SA[0:sz, :], m[0:sz, :], r[0:sz, :],
                ALU.subtract, ALU.mult), "vec")
            E.op("vector", lambda e, sz=sz, i=i: e.scalar_tensor_tensor(
                SB[0:sz, :], B1[0:sz, :], self.gcol[:, i:i + 1], SB[0:sz, :],
                ALU.mult, ALU.add), "vec")
            E.wait("sync", "vec", E.now("vec"))
            E.dma("sync", d["d_h"][o:o + sz, :], SB[0:sz, :], sem="ts")
            q = self.cols[:, 36 + 2 * i:36 + 2 * i + 1]
            rs = self.cols[:, 36 + 2 * i + 1:36 + 2 * i + 2]
            E.wait("scalar", "vec", E.now("vec"))
            E.op("scalar", lambda e, sz=sz, q=q: e.activation(
                SA[0:sz, :], SB[0:sz, :], AF.Square, accum_out=q[0:sz, :]), "act")
            E.op("scalar", lambda e, sz=sz, q=q, rs=rs: e.activation(
                rs[0:sz, :], q[0:sz, :], AF.Ln, scale=1.0 / D_MODEL, bias=EPS), "act")
            E.op("scalar", lambda e, sz=sz, rs=rs: e.activation(
                rs[0:sz, :], rs[0:sz, :], AF.Exp, scale=-0.5), "act")
            E.wait("vector", "act", E.now("act"))
            E.op("vector", lambda e, sz=sz, rs=rs: e.tensor_scalar_mul(
                B1[0:sz, :], SB[0:sz, :], rs[0:sz, :]), "vec")
            E.wait("sync", "vec", E.now("vec"))
            E.dma("sync", d["d_h2n"][o:o + sz, :], B1[0:sz, :], sem="ts")

    def ph_swiglu(self, fills):
        E, d = self.E, self.d
        al_f, vec_f = fills
        pre_f = [("al", al_f), ("vec", vec_f)]
        GU = self.bv(0, 65536)
        sil_rd = {}
        self._phv = E.now("vec")
        pan_mm = {0: E.now("mm"), 1: E.now("mm"), 2: E.now("mm"), 3: E.now("mm")}
        for hc in range(D_FF // 128):
            p = hc % 2
            gp_ = self.wst[:, p * 4096:p * 4096 + 2048]
            up_ = self.wst[:, p * 4096 + 2048:p * 4096 + 4096]
            wlg = self.wload(gp_.rearrange("p (k q) -> p k q", q=128),
                             d["w_gate"][hc].transpose([1, 0, 2]), pan_mm[2 * p],
                             sem=f"w{2 * p}")
            wlu = self.wload(up_.rearrange("p (k q) -> p k q", q=128),
                             d["w_up"][hc].transpose([1, 0, 2]), pan_mm[2 * p + 1],
                             sem=f"w{2 * p + 1}")
            for si, (o, sz) in enumerate(_splits(1024)):
                gpairs = [(gp_[:, kc * 128:kc * 128 + 128],
                           self.act[:, kc * NT + o:kc * NT + o + sz])
                          for kc in range(KD)]
                upairs = [(up_[:, kc * 128:kc * 128 + 128],
                           self.act[:, kc * NT + o:kc * NT + o + sz])
                          for kc in range(KD)]
                sA, mmA = self.job(gpairs, lambda s, sz=sz: self.ps[s][:, 0:sz],
                                   pre=pre_f + [wlg])
                sB, mmB = self.job(upairs, lambda s, sz=sz: self.ps[s][:, 0:sz],
                                   pre=[wlu])
                pan_mm[2 * p] = mmA
                pan_mm[2 * p + 1] = mmB
                sl = self.silub[:, o:o + sz]
                E.wait("scalar", "mm", mmA)
                E.wait("scalar", "vec", sil_rd.get(o, 0))
                E.op("scalar", lambda e, sA=sA, sz=sz, sl=sl: e.activation(
                    sl, self.ps[sA][:, 0:sz], AF.Silu), "act")
                self.edone("scalar", sA)
                E.wait("vector", "mm", mmB)
                E.wait("vector", "act", E.now("act"))
                E.op("vector", lambda e, sB=sB, sz=sz, sl=sl, hc=hc, o=o:
                     e.tensor_tensor(GU[:, hc * 1024 + o:hc * 1024 + o + sz],
                                     self.ps[sB][:, 0:sz], sl, ALU.mult), "vec")
                self.edone("vector", sB)
                sil_rd[o] = E.now("vec")
        vgu = E.now("vec")
        self._phv = vgu
        pan_mm2 = {0: E.now("mm"), 1: E.now("mm")}
        OS = self.wst[:, 0:2048]
        ost_st = {0: 0, 1: 0}
        for dc in range(KD):
            p = dc % 2
            pan = self.act[:, p * 8192:p * 8192 + 8192]
            wld = self.wload(pan.rearrange("p (k q) -> p k q", q=128),
                             d["w_down"][dc].transpose([1, 0, 2]), pan_mm2[p],
                             sem=f"w{p}")
            for si, (o, sz) in enumerate(_splits(1024)):
                pairs = [(pan[:, hc * 128:hc * 128 + 128],
                          GU[:, hc * 1024 + o:hc * 1024 + o + sz])
                         for hc in range(D_FF // 128)]
                s, mmv = self.job(pairs, lambda s, sz=sz: self.ps[s][:, 0:sz],
                                  pre=[wld, ("vec", vgu)])
                pan_mm2[p] = mmv
                ost = OS[:, (si % 2) * 1024:(si % 2) * 1024 + sz]
                E.wait("vector", "mm", mmv)
                E.wait("vector", f"fs{si % 2}", ost_st[si % 2])
                E.op("vector", lambda e, s=s, sz=sz, ost=ost: e.tensor_copy(
                    ost, self.ps[s][:, 0:sz]), "vec")
                self.edone("vector", s)
                E.wait("sync", "vec", E.now("vec"))
                E.dma("sync", d["d_sw"][dc * 128:dc * 128 + 128, o:o + sz], ost,
                      sem=f"fs{si % 2}")
                ost_st[si % 2] = E.now(f"fs{si % 2}")

    def ph_xb(self):
        E, d = self.E, self.d
        SA = self.bv(49152, 4096, F32)
        SB = self.bv(53248, 4096, F32)
        SC = self.bv(57344 + 2048, 4096, F32)
        SC = self.bv(45056, 4096, F32)
        B1 = self.bv(61440, 2048)
        self.wait_stores("sync")
        for i, (o, sz) in enumerate(T8):
            E.wait("sync", "act", E.now("act"))
            E.wait("sync", "vec", E.now("vec"))
            E.wait("sync", "ts", E.now("ts"))
            av = E.dma("sync", SA[0:sz, :], d["d_xa"][HALO + o:HALO + o + sz, :])
            av2 = E.dma("sync", SC[0:sz, :], d["d_h"][o:o + sz, :])
            avT = E.dma("sync", B1[0:128, :], d["d_sw"][:, o:o + 128], transpose=True)
            E.wait("vector", "al", avT)
            E.op("vector", lambda e, sz=sz: e.tensor_tensor(
                SA[0:sz, :], SA[0:sz, :], SC[0:sz, :], ALU.add), "vec")
            E.op("vector", lambda e, sz=sz: e.tensor_tensor(
                SA[0:sz, :], SA[0:sz, :], B1[0:sz, :], ALU.add), "vec")
            E.wait("sync", "vec", E.now("vec"))
            E.dma("sync", d["d_xb"][o:o + sz, :], SA[0:sz, :], sem="ts")
            m, q, r, v = self.ln_stats(SA, sz, 52 + 4 * i)
            E.wait("vector", "act", E.now("act"))
            E.wait("vector", "ts", E.now("ts"))
            E.op("vector", lambda e, sz=sz, m=m, r=r: e.tensor_scalar(
                B1[0:sz, :], SA[0:sz, :], m[0:sz, :], r[0:sz, :],
                ALU.subtract, ALU.mult), "vec")
            E.wait("sync", "vec", E.now("vec"))
            E.dma("sync", d["d_xn3"][o:o + sz, :], B1[0:sz, :], sem="ts")

    def ph_final(self):
        E, d = self.E, self.d
        SA = self.bv(49152, 4096, F32)
        B1 = self.bv(61440, 2048)
        self.wait_stores("sync")
        for i, (o, sz) in enumerate(T8):
            E.wait("sync", "vec", E.now("vec"))
            E.wait("sync", "ts", E.now("ts"))
            av = E.dma("sync", SA[0:sz, :], d["d_xb"][o:o + sz, :])
            avT = E.dma("sync", B1[0:128, :], d["d_f2"][:, o:o + 128], transpose=True)
            E.wait("vector", "al", avT)
            E.op("vector", lambda e, sz=sz: e.tensor_tensor(
                SA[0:sz, :], SA[0:sz, :], B1[0:sz, :], ALU.add), "vec")
            E.wait("sync", "vec", E.now("vec"))
            E.dma("sync", self.out_t[o:o + sz, :], SA[0:sz, :], sem="ts")
        E.wait("sync", "ts", E.now("ts"))

    def ph_debug(self):
        E = self.E
        self.wait_stores("sync")
        E.wait("sync", "vec", E.now("vec"))
        E.wait("sync", "act", E.now("act"))
        E.wait("sync", "mm", E.now("mm"))
        for name, t in self.dbg_t.items():
            s = self.d[name]
            E.dma("sync", t[:, :], s[:, :], sem="ts")
        E.wait("sync", "ts", E.now("ts"))


_DEBUG = ()
_STAGES = 99


def host_shared(inputs):
    def tile_lhsT(w):
        K, M = w.shape
        return np.ascontiguousarray(
            w.reshape(K // 128, 128, M // 128, 128).transpose(2, 0, 1, 3))

    ip = np.asarray(inputs["L0_attn_in_proj_w"])
    cw = np.asarray(inputs["L0_attn_conv_w"])[:, 0, :]
    diag = np.zeros((NFC, K_CONV, 128, 128), np.float32)
    idx = np.arange(128)
    diag[:, :, idx, idx] = cw.reshape(NFC, 128, K_CONV).transpose(0, 2, 1)

    def colmat(v):
        v = np.asarray(v, np.float32)
        return np.ascontiguousarray(v.reshape(-1, 128).T)

    ccols = np.zeros((128, 256), np.float32)
    ccols[:, 0:16] = colmat(inputs["L0_attn_ln1_w"])
    ccols[:, 16:32] = colmat(inputs["L0_attn_ln2_w"])
    ccols[:, 32:64] = colmat(inputs["L0_attn_gnorm_w"])
    ccols[:, 64:108] = colmat(inputs["L0_ffn_s_gate"])
    ccols[:, 108:152] = colmat(inputs["L0_ffn_s_up"])
    ccols[:, 152:168] = colmat(inputs["L0_ffn_s_down"])
    ccols[:, 168:202] = colmat(inputs["L0_attn_conv_b"])
    ccols[:, 202:234] = colmat(np.repeat(np.asarray(inputs["L0_attn_D"], np.float32),
                                         D_HEAD))
    ccols[:, 250] = EPS

    dcol = np.zeros((64, 2), np.float32)
    dcol[:, 0] = np.asarray(inputs["L0_attn_dt_bias"], np.float32)
    dcol[:, 1] = np.asarray(inputs["L0_attn_A_log"], np.float32)

    consts = np.zeros((128, 512), np.float32)
    consts[0, 0:128] = 1.0
    consts[0, 128:256] = -128.0
    consts[:, 256:384] = np.triu(np.ones((128, 128), np.float32))
    consts[:, 384:512] = 0.0
    consts[:, 448] = 1.0

    onehot = np.zeros((64, 64, 128), np.float32)
    onehot[np.arange(64), np.arange(64), :] = 1.0

    sh = {
        "w_xbc": tile_lhsT(np.ascontiguousarray(ip[:, D_INNER:D_INNER + CONV_DIM])),
        "w_dt": np.ascontiguousarray(ip[:, -N_HEADS:].reshape(KD, 128, N_HEADS)),
        "w_z": tile_lhsT(np.ascontiguousarray(ip[:, :D_INNER])),
        "w_gate": tile_lhsT(np.asarray(inputs["L0_attn_gate_w"])),
        "w_up": tile_lhsT(np.asarray(inputs["L0_attn_up_w"])),
        "w_down": tile_lhsT(np.asarray(inputs["L0_attn_down_w"])),
        "w_op": tile_lhsT(np.asarray(inputs["L0_attn_out_proj_w"])),
        "wq_g": tile_lhsT(np.ascontiguousarray(np.asarray(inputs["L0_ffn_wq_gate"]).T)),
        "wq_u": tile_lhsT(np.ascontiguousarray(np.asarray(inputs["L0_ffn_wq_up"]).T)),
        "wq_d": tile_lhsT(np.ascontiguousarray(np.asarray(inputs["L0_ffn_wq_down"]).T)),
        "w_conv": diag,
        "ccols_in": ccols,
        "dcol_in": dcol,
        "onehot": np.ascontiguousarray(onehot.reshape(64, 64 * 128)),
        "ones_f": np.ones((1, 128), np.float32),
        "consts_in": consts.astype(BF),
        "eye_f_in": np.eye(128, dtype=np.float32),
    }
    return sh


def host_core(inputs, core_id):
    x = np.asarray(inputs["L0_attn_hidden_states"])
    b, seg = divmod(core_id, 4)
    t0 = seg * 1024
    halo = (np.zeros((HALO, D_MODEL), np.float32) if seg == 0
            else np.ascontiguousarray(np.asarray(x[b, t0 - HALO:t0], np.float32)))
    xin = np.concatenate([halo, np.asarray(x[b, t0:t0 + 1024], np.float32)], axis=0)

    msel = np.zeros((9, 8), np.float32)
    msel[8, :] = NEG
    for j in range(8):
        jb, jseg = divmod(j, 4)
        if jb == b and jseg < seg:
            msel[8, j] = 0.0
            for i in range(8):
                ib, iseg = divmod(i, 4)
                if ib == b and jseg < iseg < seg:
                    msel[i, j] = 1.0
    return {"x_in": np.ascontiguousarray(xin), "msel_in": msel}


def kernel(**inputs):
    sh = host_shared(inputs)
    in_maps = []
    for c in range(8):
        m = dict(sh)
        m.update(host_core(inputs, c))
        in_maps.append(m)
    b = Build(debug=_DEBUG, stages=_STAGES)
    nc = b.build()
    res = run_bass_kernel_spmd(nc, in_maps, core_ids=list(range(8)))
    out = np.zeros((2, 4096, D_MODEL), np.float32)
    for c in range(8):
        bb, seg = divmod(c, 4)
        out[bb, seg * 1024:(seg + 1) * 1024] = res.results[c]["out"]
    if _DEBUG:
        kernel.dbg = [{k: v for k, v in r.items() if k.startswith("dbg_")}
                      for r in res.results]
    return out


# version 13 (final)
# speedup vs baseline: 1.0739x; 1.0049x over previous
import numpy as np
import ml_dtypes
import concourse.bass as bass
import concourse.mybir as mybir
from concourse.bass_utils import run_bass_kernel_spmd
from contextlib import ExitStack

F32 = mybir.dt.float32
BF16 = mybir.dt.bfloat16
I32 = mybir.dt.int32
AF = mybir.ActivationFunctionType
ALU = mybir.AluOpType
AX = mybir.AxisListType.X
BF = ml_dtypes.bfloat16

D_MODEL = 2048
D_INNER = 4096
N_HEADS = 64
D_HEAD = 64
D_STATE = 128
K_CONV = 4
D_FF = 8192
HID = 5632
CONV_DIM = D_INNER + 2 * D_STATE
NFC = CONV_DIM // 128
EPS = 1e-5
HALO = 16
NT = 1024 + HALO
Q = 128
NCH = 8
KD = D_MODEL // 128
NEG = -30000.0

T9 = [(0, HALO)] + [(HALO + 128 * i, 128) for i in range(8)]
T8 = [(128 * i, 128) for i in range(8)]


def _splits(n):
    k = -(-n // 512)
    base, rem = divmod(n, k)
    out, o = [], 0
    for i in range(k):
        s = base + (1 if i < rem else 0)
        out.append((o, s))
        o += s
    return out


class Emit:
    ENGS = ("sync", "scalar", "vector", "tensor", "gpsimd")
    CHAIN = {"vector": "vec", "scalar": "act"}

    def __init__(self):
        self.q = {e: [] for e in self.ENGS}
        self.cnt = {}

    def op(self, eng, fn, sem=None, amt=1, chain=True):
        ch = self.CHAIN.get(eng)
        if ch is not None and chain:
            self.wait(eng, ch, self.cnt.get(ch, 0))
        if sem is not None:
            self.cnt[sem] = self.cnt.get(sem, 0) + amt
            self.q[eng].append(("op", fn, sem, amt))
            return self.cnt[sem]
        self.q[eng].append(("op", fn, None, 0))
        return None

    def dma(self, eng, out, in_, sem="al", **kw):
        return self.op(eng, (lambda e, o=out, i=in_, k=kw: e.dma_start(out=o, in_=i, **k)),
                       sem, 16, chain=False)

    def wait(self, eng, sem, val):
        if val:
            self.cnt.setdefault(sem, 0)
            self.q[eng].append(("wait", sem, val))

    def now(self, sem):
        return self.cnt.get(sem, 0)

    def replay(self, eng_obj, eng_name, sems):
        for item in self.q[eng_name]:
            if item[0] == "op":
                _, fn, sem, amt = item
                ins = fn(eng_obj)
                if sem is not None:
                    ins.then_inc(sems[sem], amt)
            else:
                _, sem, val = item
                eng_obj.wait_ge(sems[sem], val)


class Build:
    def __init__(self, debug=(), stages=99):
        self.nc = bass.Bass("TRN2", target_bir_lowering=False)
        self.E = Emit()
        self.stages = stages
        self.debug = list(debug)
        self.slot_use = [0] * 8
        self.rot_i = 0
        self.mk_dram()

    def mk_dram(self):
        nc = self.nc
        self.d = {}
        def din(name, shape, dt):
            self.d[name] = nc.dram_tensor(name, list(shape), dt, kind="ExternalInput")
        def dsc(name, shape, dt):
            self.d[name] = nc.dram_tensor(name, list(shape), dt)

        din("x_in", [NT, D_MODEL], F32)
        din("w_xbc", [NFC, KD, 128, 128], F32)
        din("w_dt", [KD, 128, N_HEADS], F32)
        din("w_z", [D_INNER // 128, KD, 128, 128], F32)
        din("w_gate", [D_FF // 128, KD, 128, 128], F32)
        din("w_up", [D_FF // 128, KD, 128, 128], F32)
        din("w_down", [KD, D_FF // 128, 128, 128], F32)
        din("w_op", [KD, D_INNER // 128, 128, 128], F32)
        din("wq_g", [HID // 128, KD, 128, 128], I32)
        din("wq_u", [HID // 128, KD, 128, 128], I32)
        din("wq_d", [KD, HID // 128, 128, 128], I32)
        din("w_conv", [NFC, K_CONV, 128, 128], F32)
        din("ccols_in", [128, 256], F32)
        din("dcol_in", [64, 2], F32)
        din("msel_in", [9, 8], F32)
        din("onehot", [64, 64 * 128], F32)
        din("ones_f", [1, 128], F32)
        din("ones_b", [1, 128], BF16)
        din("consts_in", [128, 512], BF16)
        din("eye_f_in", [128, 128], F32)

        self.out_t = nc.dram_tensor("out", [1024, D_MODEL], F32, kind="ExternalOutput")

        dsc("d_xa", [NT, D_MODEL], F32)
        dsc("d_xn1", [NT, D_MODEL], BF16)
        dsc("d_t1", [NT, D_MODEL], BF16)
        dsc("d_f1", [D_MODEL, 1152], BF16)
        dsc("d_xbc", [CONV_DIM, 1024], BF16)
        dsc("d_y", [1024, D_INNER], BF16)
        dsc("d_mo", [D_MODEL, 1024], BF16)
        dsc("d_h", [1024, D_MODEL], F32)
        dsc("d_h2n", [1024, D_MODEL], BF16)
        dsc("d_sw", [D_MODEL, 1024], BF16)
        dsc("d_xb", [1024, D_MODEL], F32)
        dsc("d_xn3", [1024, D_MODEL], BF16)
        dsc("d_f2", [D_MODEL, 1024], BF16)
        dsc("cc_s_in", [128, D_INNER], BF16)
        dsc("cc_s_out", [8 * 128, D_INNER], BF16)
        dsc("cc_l_in", [1, 64], F32)
        dsc("cc_l_out", [8, 64], F32)
        dsc("d_probe", [128, 2048], BF16)

        self.dbg_t = {}
        for name in self.debug:
            src = self.d[name]
            self.dbg_t[name] = nc.dram_tensor(
                "dbg_" + name, list(src.shape), src.dtype, kind="ExternalOutput")

    def build(self):
        nc, E = self.nc, self.E
        ctx = ExitStack()
        def sb(name, shape, dt):
            return ctx.enter_context(nc.sbuf_tensor(name, list(shape), dt))
        with ctx:
            self.act = sb("act_fm", [128, KD * NT], BF16)
            self.ba = sb("barena", [128, 65536], BF16)
            self.wst = sb("wst", [128, 8192], BF16)
            self.silub = sb("silub", [128, 1040], BF16)
            self.consts = sb("consts", [128, 512], BF16)
            self.eye_f = sb("eye_f", [128, 128], F32)
            self.onesf = sb("onesf", [1, 128], F32)
            self.onesb = sb("onesb", [1, 128], BF16)
            self.ccols = sb("ccols", [128, 256], F32)
            self.cols = sb("cols", [128, 96], F32)
            self.rows = sb("rows", [1, 1024], F32)
            self.rowsb = sb("rowsb", [1, 2 * NT], BF16)
            self.dcol = sb("dcol", [64, 8], F32)
            self.tots = sb("tots", [1, 1152], F32)
            self.smallf = sb("smallf", [128, 1088], F32)
            self.wtile = sb("wtile", [128, 64], F32)
            self.ebcT = sb("ebcT", [128, 64], F32)
            self.gcol = sb("gcol", [128, 8], F32)
            self.ldece = sb("ldece", [9, 64], F32)
            self.msel9 = sb("msel9", [9, 8], F32)
            self.bfm = sb("bfm", [128, 128], BF16)
            self.btm = sb("btm", [128, 128], BF16)
            self.bcm = sb("bcm", [128, 128], BF16)
            self.wjT = sb("wjT", [64, 8], F32)
            self.zbuf = sb("zbuf", [64, 128], F32)
            self.ps = [ctx.enter_context(nc.psum_tensor(f"ps{i}", [128, 512], F32))
                       for i in range(8)]
            nc.const_aps.register(F32, EPS, self.ccols[:, 250:251])
            sem_names = ["al", "wl", "mm", "vec", "act", "gp", "st", "cc",
                         "w0", "w1", "w2", "w3", "sl0", "sl1", "lde",
                         "g0", "g1", "ts", "fs0", "fs1", "xs0", "xs1", "ys", "oh"] + \
                        [f"e{i}" for i in range(8)]
            sems = {n: ctx.enter_context(nc.semaphore(n)) for n in sem_names}
            block = ctx.enter_context(nc.Block())

            self.emit_all()

            for name in Emit.ENGS:
                def mk(nm):
                    def f(eng):
                        E.replay(eng, nm, sems)
                    return f
                getattr(block, name)(mk(name))
        return nc

    def bv(self, c0, ncols, dt=BF16):
        ap = self.ba[:, c0:c0 + ncols]
        return ap if dt == BF16 else ap.bitcast(dt)

    def rot(self):
        s = self.rot_i % 4
        self.rot_i += 1
        return s

    def acq(self, s):
        self.E.wait("tensor", f"e{s}", self.slot_use[s])

    def edone(self, eng, s):
        self.E.op(eng, lambda e: e.nop(), f"e{s}")
        self.slot_use[s] += 1

    def job(self, pairs, out_fn, pre=()):
        E = self.E
        s = self.rot()
        self.acq(s)
        for sem, val in pre:
            E.wait("tensor", sem, val)
        n = len(pairs)
        for j, (lh, rh) in enumerate(pairs):
            last = (j == n - 1)
            E.op("tensor", lambda e, lh=lh, rh=rh, s=s, st=(j == 0),
                 sp=last, of=out_fn: e.matmul(of(s), lh, rh,
                                              start=st, stop=sp),
                 "mm" if last else None)
        return s, E.now("mm")

    STORE_SEMS = ("ts", "fs0", "fs1", "xs0", "xs1", "ys")

    def wait_stores(self, eng):
        for s in self.STORE_SEMS:
            self.E.wait(eng, s, self.E.now(s))

    def wload(self, dst_ap, src_ap, last_mm, sem="w0"):
        E = self.E
        E.wait("gpsimd", "mm", last_mm)
        E.wait("gpsimd", "vec", getattr(self, "_phv", 0))
        return (sem, E.dma("gpsimd", dst_ap, src_ap, sem=sem))

    def ln_stats(self, buf, sz, sb0, keep_v=False):
        E = self.E
        m = self.cols[:, sb0:sb0 + 1]
        q = self.cols[:, sb0 + 1:sb0 + 2]
        r = self.cols[:, sb0 + 2:sb0 + 3]
        v = self.cols[:, sb0 + 3:sb0 + 4]
        scr = self.bv(57344, 4096, F32)
        E.op("vector", lambda e: e.reduce_sum(m[0:sz, :], buf[0:sz, :], axis=AX), "vec")
        E.op("vector", lambda e: e.tensor_scalar_mul(m[0:sz, :], m[0:sz, :],
                                                     1.0 / D_MODEL), "vec")
        E.wait("scalar", "vec", E.now("vec"))
        E.op("scalar", lambda e: e.activation(
            scr[0:sz, :], buf[0:sz, :], AF.Square, accum_out=q[0:sz, :]), "act")
        E.wait("vector", "act", E.now("act"))
        E.op("vector", lambda e: e.tensor_tensor(
            v[0:sz, :], m[0:sz, :], m[0:sz, :], ALU.mult), "vec")
        E.op("vector", lambda e: e.scalar_tensor_tensor(
            v[0:sz, :], q[0:sz, :], 1.0 / D_MODEL, v[0:sz, :],
            ALU.mult, ALU.subtract), "vec")
        E.wait("scalar", "vec", E.now("vec"))
        E.op("scalar", lambda e: e.activation(r[0:sz, :], v[0:sz, :], AF.Ln,
                                              bias=EPS), "act")
        E.op("scalar", lambda e: e.activation(r[0:sz, :], r[0:sz, :], AF.Exp,
                                              scale=-0.5), "act")
        return m, q, r, v

    def emit_all(self):
        E = self.E
        n = self.stages
        steps = [
            self.ph_consts,
            self.ph_ln1,
            lambda: setattr(self, "_f", self.ph_fill_act(self.d["d_xn1"], NT, None, E.now("st"))),
            lambda: self.ph_ffnq(NT, self.d["d_f1"], self._f),
            self.ph_xa,
            lambda: setattr(self, "_f", self.ph_fill_act(self.d["d_t1"], NT, 0, E.now("st"))),
            lambda: self.ph_inproj(self._f),
            self.ph_scan,
            self.ph_exchange_z,
            self.ph_combine,
            self.ph_correction,
            self.ph_gating,
            self.ph_outproj,
            self.ph_h,
            lambda: setattr(self, "_f", self.ph_fill_act(self.d["d_h2n"], 1024, 16, E.now("st"))),
            lambda: self.ph_swiglu(self._f),
            self.ph_xb,
            lambda: setattr(self, "_f", self.ph_fill_act(self.d["d_xn3"], 1024, None, E.now("st"))),
            lambda: self.ph_ffnq(1024, self.d["d_f2"], self._f),
            self.ph_final,
        ]
        for i, s in enumerate(steps):
            if i >= n:
                break
            s()
        self.ph_debug()
        self.wait_stores("sync")

    def ph_consts(self):
        E, d = self.E, self.d
        E.dma("sync", self.consts[:, :], d["consts_in"][:, :])
        E.dma("sync", self.eye_f[:, :], d["eye_f_in"][:, :])
        E.dma("sync", self.onesf[:, :], d["ones_f"][:, :])
        E.dma("sync", self.onesb[:, :], d["ones_b"][:, :])
        E.dma("sync", self.ccols[:, :], d["ccols_in"][:, :])
        E.dma("sync", self.dcol[:, 0:2], d["dcol_in"][:, :])
        E.dma("sync", self.msel9[:, :], d["msel_in"][:, :])
        al = E.now("al")
        E.wait("scalar", "al", al)
        E.op("scalar", lambda e: e.activation(self.dcol[:, 2:3], self.dcol[:, 1:2],
                                              AF.Exp), "act")
        E.wait("vector", "act", E.now("act"))
        E.op("vector", lambda e: e.tensor_scalar_mul(self.dcol[:, 3:4],
                                                     self.dcol[:, 2:3], -1.0), "vec")
        E.wait("vector", "al", al)
        E.op("vector", lambda e: e.tensor_scalar_mul(self.ccols[:, 234:250],
                                                     self.ccols[:, 152:168], 0.5), "vec")
        E.op("vector", lambda e: e.memset(self.zbuf[:, :], 0.0), "vec")
        E.dma("sync", self.ldece[8:9, 0:64], self.d["ones_f"][0:1, 0:64], sem="ts")
        self.c_init = (E.now("vec"), E.now("act"))

    def ph_ln1(self):
        E, d = self.E, self.d
        SA = self.bv(49152, 4096, F32)
        SB = self.bv(53248, 4096, F32)
        B1 = self.bv(61440, 2048)
        E.wait("vector", "vec", self.c_init[0])
        for i, (o, sz) in enumerate(T9):
            buf = SA if i % 2 == 0 else SB
            E.wait("sync", "act", E.now("act"))
            E.wait("sync", "vec", E.now("vec"))
            av = E.dma("sync", buf[0:sz, :], d["x_in"][o:o + sz, :])
            E.wait("vector", "al", av)
            m, q, r, v = self.ln_stats(buf, sz, 84)
            E.wait("vector", "act", E.now("act"))
            E.wait("vector", "ts", E.now("ts"))
            E.op("vector", lambda e, b=buf, sz=sz, m=m, r=r: e.tensor_scalar(
                B1[0:sz, :], b[0:sz, :], m[0:sz, :], r[0:sz, :],
                ALU.subtract, ALU.mult), "vec")
            E.wait("sync", "vec", E.now("vec"))
            E.dma("sync", d["d_xn1"][o:o + sz, :], B1[0:sz, :], sem="ts")

    def ph_fill_act(self, src_tm, n_tok, colw, wait_st):
        E = self.E
        self.wait_stores("sync")
        E.wait("sync", "mm", E.now("mm"))
        for kc in range(KD):
            av = E.dma("sync", self.act[:, kc * NT:kc * NT + n_tok],
                       src_tm[0:n_tok, kc * 128:kc * 128 + 128], transpose=True)
        E.wait("vector", "al", av)
        if colw is not None:
            for kc in range(KD):
                E.op("vector", lambda e, k=kc, n=n_tok: e.tensor_scalar_mul(
                    self.act[:, k * NT:k * NT + n], self.act[:, k * NT:k * NT + n],
                    self.ccols[:, colw + k:colw + k + 1]), "vec")
        return (E.now("al"), E.now("vec"))

    def ph_ffnq(self, n_tok, dst_slab, fills):
        E, d = self.E, self.d
        NH = HID // 128
        sp = _splits(n_tok)
        al_f, vec_f = fills
        ones_c = self.consts[:, 448:449]
        neg128 = self.consts[0:1, 128:256]
        pre_f = [("al", al_f), ("vec", vec_f)]

        for (o, sz) in sp:
            pairs = [(ones_c, self.act[:, kc * NT + o:kc * NT + o + sz])
                     for kc in range(KD)]
            s, mmv = self.job(pairs, lambda s, sz=sz: self.ps[s][0:1, 0:sz], pre=pre_f)
            E.wait("vector", "mm", mmv)
            E.op("vector", lambda e, s=s, o=o, sz=sz: e.tensor_copy(
                self.rowsb[0:1, o:o + sz], self.ps[s][0:1, 0:sz]), "vec")
            self.edone("vector", s)
        vrow = E.now("vec")

        GU = self.bv(0, NH * n_tok)
        self._sil_rd = {}
        self._sgu_pend = {}
        self._phv = E.now("vec")
        pan_mm = {0: E.now("mm"), 1: E.now("mm"), 2: E.now("mm"), 3: E.now("mm")}
        for hc in range(NH):
            p = hc % 2
            gp_ = self.wst[:, p * 4096:p * 4096 + 2048]
            up_ = self.wst[:, p * 4096 + 2048:p * 4096 + 4096]
            wlg = self.wload(gp_.rearrange("p (k q) -> p k q", q=128),
                             d["wq_g"][hc].transpose([1, 0, 2]), pan_mm[2 * p],
                             sem=f"w{2 * p}")
            wlu = self.wload(up_.rearrange("p (k q) -> p k q", q=128),
                             d["wq_u"][hc].transpose([1, 0, 2]), pan_mm[2 * p + 1],
                             sem=f"w{2 * p + 1}")
            for si, (o, sz) in enumerate(sp):
                gpairs = [(gp_[:, kc * 128:kc * 128 + 128],
                           self.act[:, kc * NT + o:kc * NT + o + sz])
                          for kc in range(KD)] + \
                         [(neg128, self.rowsb[0:1, o:o + sz])]
                upairs = [(up_[:, kc * 128:kc * 128 + 128],
                           self.act[:, kc * NT + o:kc * NT + o + sz])
                          for kc in range(KD)] + \
                         [(neg128, self.rowsb[0:1, o:o + sz])]
                prew = pre_f + [wlg, ("vec", vrow)]
                sA, mmA = self.job(gpairs, lambda s, sz=sz: self.ps[s][:, 0:sz],
                                   pre=prew)
                sB, mmB = self.job(upairs, lambda s, sz=sz: self.ps[s][:, 0:sz],
                                   pre=[wlu])
                pan_mm[2 * p] = mmA
                pan_mm[2 * p + 1] = mmB
                sgc = self.ccols[:, 64 + hc:64 + hc + 1]
                suc = self.ccols[:, 108 + hc:108 + hc + 1]
                sl = self.silub[:, o:o + sz]
                E.wait("scalar", "mm", mmA)
                E.wait("scalar", "vec", self._sil_rd.get(o, 0))
                E.op("scalar", lambda e, sA=sA, sz=sz, sgc=sgc, sl=sl: e.activation(
                    sl, self.ps[sA][:, 0:sz], AF.Silu, scale=sgc), "act")
                self.edone("scalar", sA)
                av = E.now("act")
                E.wait("vector", "mm", mmB)
                E.wait("vector", "act", av)
                gu_sl = GU[:, hc * n_tok + o:hc * n_tok + o + sz]
                E.op("vector", lambda e, sB=sB, sz=sz, suc=suc, sl=sl, g=gu_sl:
                     e.scalar_tensor_tensor(g, self.ps[sB][:, 0:sz], suc, sl,
                                            ALU.mult, ALU.mult), "vec")
                self._sil_rd[o] = E.now("vec")
                self.edone("vector", sB)
                guv = E.now("vec")
                rs = 4 + si
                if hc == 0:
                    self.acq(rs)
                else:
                    pg, pgu, psz = self._sgu_pend[si]
                    E.wait("tensor", "vec", pg)
                    E.op("tensor", lambda e, rs=rs, g=pgu, sz=psz, st=(hc == 1):
                         e.matmul(self.ps[rs][0:1, 0:sz], ones_c, g,
                                  start=st, stop=False), "mm")
                self._sgu_pend[si] = (guv, gu_sl, sz)
        for si, (o, sz) in enumerate(sp):
            pg, pgu, psz = self._sgu_pend[si]
            E.wait("tensor", "vec", pg)
            E.op("tensor", lambda e, rs=4 + si, g=pgu, sz=psz:
                 e.matmul(self.ps[rs][0:1, 0:sz], ones_c, g,
                          start=False, stop=True), "mm")
        mm_sgu = E.now("mm")
        E.wait("vector", "mm", mm_sgu)
        for si, (o, sz) in enumerate(sp):
            E.op("vector", lambda e, si=si, o=o, sz=sz: e.tensor_copy(
                self.rowsb[0:1, NT + o:NT + o + sz], self.ps[4 + si][0:1, 0:sz]),
                "vec")
            self.edone("vector", 4 + si)
        vgu = E.now("vec")

        self._phv = E.now("vec")
        pan_mm2 = {0: mm_sgu, 1: mm_sgu}
        OS = self.wst[:, 0:2048]
        ost_st = {0: 0, 1: 0}
        for dc in range(KD):
            p = dc % 2
            pan = self.act[:, p * 5632:p * 5632 + 5632]
            wld = self.wload(pan.rearrange("p (k q) -> p k q", q=128),
                             d["wq_d"][dc].transpose([1, 0, 2]), pan_mm2[p],
                             sem=f"w{p}")
            for si, (o, sz) in enumerate(sp):
                pairs = [(pan[:, hc * 128:hc * 128 + 128],
                          GU[:, hc * n_tok + o:hc * n_tok + o + sz])
                         for hc in range(NH)] + \
                        [(neg128, self.rowsb[0:1, NT + o:NT + o + sz])]
                s, mmv = self.job(pairs, lambda s, sz=sz: self.ps[s][:, 0:sz],
                                  pre=[wld, ("vec", vgu)])
                pan_mm2[p] = mmv
                sdc = self.ccols[:, 234 + dc:234 + dc + 1]
                ost = OS[:, (si % 2) * 1024:(si % 2) * 1024 + sz]
                E.wait("vector", "mm", mmv)
                E.wait("vector", f"fs{si % 2}", ost_st[si % 2])
                E.op("vector", lambda e, s=s, sz=sz, sdc=sdc, ost=ost:
                     e.tensor_scalar_mul(ost, self.ps[s][:, 0:sz], sdc), "vec")
                self.edone("vector", s)
                E.wait("sync", "vec", E.now("vec"))
                E.dma("sync", dst_slab[dc * 128:dc * 128 + 128, o:o + sz], ost,
                      sem=f"fs{si % 2}")
                ost_st[si % 2] = E.now(f"fs{si % 2}")

    def ph_xa(self):
        E, d = self.E, self.d
        SA = self.bv(49152, 4096, F32)
        SB = self.bv(53248, 4096, F32)
        B1 = self.bv(61440, 2048)
        B2 = self.bv(63488, 2048)
        self.wait_stores("sync")
        for i, (o, sz) in enumerate(T9):
            buf = SA if i % 2 == 0 else SB
            E.wait("sync", "act", E.now("act"))
            E.wait("sync", "vec", E.now("vec"))
            E.wait("sync", "ts", E.now("ts"))
            av = E.dma("sync", buf[0:sz, :], d["x_in"][o:o + sz, :])
            avT = E.dma("sync", B1[0:128, :], d["d_f1"][:, o:o + 128], transpose=True)
            E.wait("vector", "al", avT)
            E.op("vector", lambda e, b=buf, sz=sz: e.tensor_tensor(
                b[0:sz, :], b[0:sz, :], B1[0:sz, :], ALU.add), "vec")
            E.wait("sync", "vec", E.now("vec"))
            E.dma("sync", d["d_xa"][o:o + sz, :], buf[0:sz, :], sem="ts")
            m, q, r, v = self.ln_stats(buf, sz, 4 * i)
            w = self.smallf[:, 1024 + i:1024 + i + 1]
            E.wait("vector", "act", E.now("act"))
            E.op("vector", lambda e, sz=sz, r=r, w=w: e.tensor_tensor(
                w[0:sz, :], r[0:sz, :], r[0:sz, :], ALU.mult), "vec")
            E.op("vector", lambda e, sz=sz, v=v, w=w: e.tensor_tensor(
                w[0:sz, :], w[0:sz, :], v[0:sz, :], ALU.mult), "vec")
            E.wait("scalar", "vec", E.now("vec"))
            E.op("scalar", lambda e, sz=sz, w=w: e.activation(
                w[0:sz, :], w[0:sz, :], AF.Ln, bias=EPS), "act")
            E.op("scalar", lambda e, sz=sz, w=w: e.activation(
                w[0:sz, :], w[0:sz, :], AF.Exp, scale=-0.5), "act")
            E.wait("vector", "act", E.now("act"))
            E.wait("vector", "ts", E.now("ts"))
            E.op("vector", lambda e, sz=sz, r=r, w=w: e.tensor_tensor(
                w[0:sz, :], w[0:sz, :], r[0:sz, :], ALU.mult), "vec")
            E.op("vector", lambda e, b=buf, sz=sz, m=m, w=w: e.tensor_scalar(
                B2[0:sz, :], b[0:sz, :], m[0:sz, :], w[0:sz, :],
                ALU.subtract, ALU.mult), "vec")
            E.wait("sync", "vec", E.now("vec"))
            E.dma("sync", d["d_t1"][o:o + sz, :], B2[0:sz, :], sem="ts")

    def ph_inproj(self, fills):
        E, d = self.E, self.d
        sp = _splits(NT)
        al_f, vec_f = fills
        pre_f = [("al", al_f), ("vec", vec_f)]
        RAW = [self.bv(0, 1040), self.bv(1040, 1040)]
        self._phv = E.now("vec")
        raw_mm = {0: E.now("mm"), 1: E.now("mm")}
        pan_mm = {0: E.now("mm"), 1: E.now("mm")}
        conv_mm = {0: 0, 1: 0}
        cst_st = {0: 0, 1: 0}
        pend_conv = None
        for fc in range(NFC):
            p = fc % 2
            pan = self.wst[:, p * 2560:p * 2560 + 2048]
            dia = self.wst[:, p * 2560 + 2048:p * 2560 + 2560]
            wlp = self.wload(pan.rearrange("p (k q) -> p k q", q=128),
                             d["w_xbc"][fc].transpose([1, 0, 2]), raw_mm[p],
                             sem=f"w{p}")
            wld = self.wload(dia.rearrange("p (k q) -> p k q", q=128),
                             d["w_conv"][fc].transpose([1, 0, 2]), conv_mm[p],
                             sem=f"w{2 + p}")
            raw = RAW[p]
            rawv = []
            for si, (o, sz) in enumerate(sp):
                pairs = [(pan[:, kc * 128:kc * 128 + 128],
                          self.act[:, kc * NT + o:kc * NT + o + sz])
                         for kc in range(KD)]
                s, mmv = self.job(pairs, lambda s, sz=sz: self.ps[s][:, 0:sz],
                                  pre=pre_f + [wlp])
                raw_mm[p] = mmv
                E.wait("vector", "mm", mmv)
                E.wait("vector", "mm", conv_mm[p])
                E.op("vector", lambda e, s=s, o=o, sz=sz, raw=raw: e.tensor_copy(
                    raw[:, o:o + sz], self.ps[s][:, 0:sz]), "vec")
                self.edone("vector", s)
            vr = E.now("vec")

            def emit_conv(vr=None, raw=None, dia=None, wld=None, fc=None, p=None):
                for (o, sz) in _splits(1024):
                    pairs = [(dia[:, j * 128:j * 128 + 128],
                              raw[:, o + 13 + j:o + 13 + j + sz])
                             for j in range(K_CONV)]
                    s, mmv = self.job(pairs, lambda s, sz=sz: self.ps[s][:, 0:sz],
                                      pre=[("vec", vr), wld])
                    conv_mm[p] = mmv
                    cp = 0 if o == 0 else 1
                    cbc = self.ccols[:, 168 + fc:168 + fc + 1]
                    cst = (self.silub[:, 0:sz] if o == 0
                           else self.silub[:, 512:512 + sz])
                    E.wait("scalar", "mm", mmv)
                    E.wait("scalar", f"xs{cp}", cst_st[cp])
                    E.op("scalar", lambda e, s=s, sz=sz, cbc=cbc, cst=cst:
                         e.activation(cst, self.ps[s][:, 0:sz], AF.Silu, bias=cbc),
                         "act")
                    self.edone("scalar", s)
                    E.wait("sync", "act", E.now("act"))
                    E.dma("sync", d["d_xbc"][fc * 128:fc * 128 + 128, o:o + sz],
                          cst, sem=f"xs{cp}")
                    cst_st[cp] = E.now(f"xs{cp}")

            if pend_conv is not None:
                emit_conv(**pend_conv)
            pend_conv = dict(vr=vr, raw=raw, dia=dia, wld=wld, fc=fc, p=p)
        if pend_conv is not None:
            emit_conv(**pend_conv)
        DT = self.bv(40960, 10400, F32)
        wdt = self.wst[:, 5120 + 0:5120 + 1024]
        wl = self.wload(wdt.rearrange("p (k q) -> p k q", q=64),
                        d["w_dt"][:, :, :].transpose([1, 0, 2]), E.now("mm"),
                        sem="w2")
        for si, (o, sz) in enumerate(sp):
            pairs = [(wdt[:, kc * 64:kc * 64 + 64],
                      self.act[:, kc * NT + o:kc * NT + o + sz])
                     for kc in range(KD)]
            s, mmv = self.job(pairs, lambda s, sz=sz: self.ps[s][0:64, 0:sz],
                              pre=pre_f + [wl])
            E.wait("scalar", "mm", mmv)
            E.op("scalar", lambda e, s=s, o=o, sz=sz: e.activation(
                DT[0:64, o:o + sz], self.ps[s][0:64, 0:sz], AF.Exp,
                bias=self.dcol[:, 0:1]), "act")
            self.edone("scalar", s)
            E.op("scalar", lambda e, o=o, sz=sz: e.activation(
                DT[0:64, o:o + sz], DT[0:64, o:o + sz], AF.Ln, bias=1.0), "act")
        E.op("scalar", lambda e: e.activation(DT[0:64, 1040:2080], DT[0:64, 0:1040],
                                              AF.Ln), "act")
        adt = E.now("act")
        E.wait("vector", "act", adt)
        E.op("vector", lambda e: e.tensor_scalar_mul(
            DT[0:64, 2080:3120], DT[0:64, 0:1040], self.dcol[:, 3:4]), "vec")
        self.dt_ready = (E.now("vec"), E.now("act"))

    def ph_scan(self):
        E, d = self.E, self.d
        DT = self.bv(40960, 10400, F32)
        ACU = lambda c: DT[0:64, 3120 + c * 128:3120 + (c + 1) * 128]
        LMA = DT[0:64, 4144:5168]
        OH = self.bv(0, 16384, F32)
        XT = self.bv(16384, 4096)
        XW = self.bv(20480, 4096)
        EXD = self.bv(24576, 1024)
        SCO = self.bv(25600, 1024)
        ARG = self.bv(30720, 2048, F32)
        STATE = self.bv(26624, 4096)
        CTI = self.bv(32768, 2048, F32)
        WX = self.bv(34816, 1024, F32)
        CFM = self.bv(35840, 1024)
        YSB = self.bv(36864, 4096)

        E.wait("sync", "vec", self.dt_ready[0])
        E.wait("sync", "mm", E.now("mm"))
        self.wait_stores("sync")
        oh_al = E.dma("sync", OH[0:64, :], d["onehot"][:, :], sem="oh")
        E.wait("tensor", "oh", oh_al)
        E.wait("vector", "oh", oh_al)
        E.op("vector", lambda e: e.memset(STATE[:, :], 0.0), "vec")
        E.op("vector", lambda e: e.memset(self.tots[:, :], 0.0), "vec")
        for c in range(NCH):
            E.op("vector", lambda e, c=c: e.tensor_tensor_scan(
                ACU(c), DT[0:64, 2080 + 16 + c * 128:2080 + 16 + (c + 1) * 128],
                self.zbuf[:, :], 0.0, ALU.add, ALU.add), "vec")
            E.op("vector", lambda e, c=c: e.reduce_sum(
                self.smallf[0:64, 1024 + c:1024 + c + 1],
                DT[0:64, 2080 + 16 + c * 128:2080 + 16 + (c + 1) * 128],
                axis=AX), "vec")
        vq = E.now("vec")
        E.op("vector", lambda e: e.tensor_tensor(
            LMA, DT[0:64, 1040 + 16:1040 + 16 + 1024], DT[0:64, 3120:4144],
            ALU.subtract), "vec")
        E.wait("tensor", "vec", vq)
        self.acq(6)
        for c in range(NCH):
            E.op("tensor", lambda e, c=c: e.transpose(
                self.ps[6][0:1, c * 64:c * 64 + 64],
                self.smallf[0:64, 1024 + c:1024 + c + 1],
                self.eye_f[0:64, 0:64]), "mm")
        mmv = E.now("mm")
        E.wait("vector", "mm", mmv)
        E.op("vector", lambda e: e.tensor_copy(self.tots[0:1, 0:512],
                                               self.ps[6][0:1, 0:512]), "vec")
        self.edone("vector", 6)
        for c in range(1, NCH):
            E.op("vector", lambda e, c=c: e.tensor_tensor(
                self.tots[0:1, 512 + c * 64:512 + (c + 1) * 64],
                self.tots[0:1, 512 + (c - 1) * 64:512 + c * 64],
                self.tots[0:1, (c - 1) * 64:c * 64], ALU.add), "vec")
        E.op("vector", lambda e: e.tensor_tensor(
            self.tots[0:1, 1088:1152], self.tots[0:1, 512 + 448:512 + 512],
            self.tots[0:1, 448:512], ALU.add), "vec")
        vpre = E.now("vec")

        vch_prev = E.now("vec")
        st_prev = E.now("ys")
        for c in range(NCH):
            self.acq(6)
            E.wait("tensor", "vec", vpre)
            E.op("tensor", lambda e, c=c: e.transpose(
                self.ps[6][0:128, 0:64], ACU(c), self.eye_f[0:64, 0:64]), "mm")
            E.op("tensor", lambda e, c=c: e.transpose(
                self.ps[6][0:128, 64:128], LMA[:, c * 128:(c + 1) * 128],
                self.eye_f[0:64, 0:64]), "mm")
            mmv = E.now("mm")
            E.wait("vector", "mm", mmv)
            E.op("vector", lambda e, c=c: e.tensor_copy(
                CTI[:, c * 128:(c + 1) * 128], self.ps[6][:, 0:128]), "vec")
            self.edone("vector", 6)
            vc = E.now("vec")
            E.wait("scalar", "vec", vc)
            E.op("scalar", lambda e, c=c: e.activation(
                WX[:, c * 64:(c + 1) * 64], CTI[:, c * 128:c * 128 + 64], AF.Exp),
                "act")
            self.acq(6)
            E.op("tensor", lambda e, c=c: e.matmul(
                self.ps[6][:, 0:64], self.onesf[:, :],
                self.tots[0:1, c * 64:(c + 1) * 64], start=True, stop=True), "mm")
            mmv = E.now("mm")
            E.wait("vector", "mm", mmv)
            E.wait("vector", "vec", vc)
            E.op("vector", lambda e, c=c: e.tensor_tensor(
                self.smallf[:, 0:64], self.ps[6][:, 0:64],
                CTI[:, c * 128 + 64:c * 128 + 128], ALU.add), "vec")
            vv = E.now("vec")
            E.wait("scalar", "vec", vv)
            E.wait("scalar", "vec", vch_prev)
            E.op("scalar", lambda e: e.activation(self.ebcT[:, :], self.ps[6][:, 0:64],
                                                  AF.Exp), "act")
            self.edone("scalar", 6)
            E.op("scalar", lambda e: e.activation(self.wtile[:, :],
                                                  self.smallf[:, 0:64], AF.Exp), "act")
            a_pre = E.now("act")
            E.wait("sync", "mm", E.now("mm"))
            E.wait("sync", "vec", vch_prev)
            alx = E.dma("sync", XT[:, :], d["d_xbc"][0:4096, c * 128:(c + 1) * 128],
                        transpose=True)
            alb = E.dma("sync", self.bfm[:, :],
                        d["d_xbc"][4096:4224, c * 128:(c + 1) * 128])
            alc = E.dma("sync", CFM[:, c * 128:(c + 1) * 128],
                        d["d_xbc"][4224:4352, c * 128:(c + 1) * 128])
            albt = E.dma("sync", self.btm[:, :],
                         d["d_xbc"][4096:4224, c * 128:(c + 1) * 128], transpose=True)
            alx = alb = alc = albt
            E.wait("vector", "act", a_pre)
            E.wait("vector", "al", albt)
            for g in range(8):
                E.op("vector", lambda e, g=g: e.tensor_tensor(
                    XW[:, g * 512:(g + 1) * 512].rearrange("p (a b) -> p a b", b=64),
                    XT[:, g * 512:(g + 1) * 512].rearrange("p (a b) -> p a b", b=64),
                    self.wtile[:, g * 8:(g + 1) * 8].unsqueeze(2)
                        .broadcast_to([128, 8, 64]), ALU.mult), "vec")
            vxw = E.now("vec")
            self.acq(4)
            E.wait("tensor", "al", max(alb, alc))
            E.op("tensor", lambda e, c=c: e.matmul(
                self.ps[4][:, 0:128], self.bfm[:, :], CFM[:, c * 128:(c + 1) * 128],
                start=True, stop=True), "mm")
            mmv = E.now("mm")
            E.wait("vector", "mm", mmv)
            E.op("vector", lambda e: e.tensor_tensor(
                self.bcm[:, :], self.ps[4][:, 0:128], self.consts[:, 256:384],
                ALU.mult), "vec")
            self.edone("vector", 4)
            vbc = E.now("vec")
            mmy_prev = E.now("mm")
            vsc_prev = E.now("vec")
            aexp_prev = E.now("act")
            for g in range(8):
                dsl = (0, 1) if g % 2 == 0 else (2, 3)
                for k in range(8):
                    h = 8 * g + k
                    s = dsl[k // 4]
                    if k % 4 == 0:
                        self.acq(s)
                    E.op("tensor", lambda e, h=h, k=k, s=s, c=c: e.matmul(
                        self.ps[s][:, (k % 4) * 128:(k % 4) * 128 + 128],
                        OH[0:64, h * 128:(h + 1) * 128], ACU(c),
                        start=True, stop=True), "mm" if k == 7 else None)
                mmv = E.now("mm")
                E.wait("vector", "mm", mmv)
                E.wait("vector", "act", aexp_prev)
                for k in range(8):
                    h = 8 * g + k
                    s = dsl[k // 4]
                    E.op("vector", lambda e, h=h, k=k, s=s, c=c: e.tensor_scalar(
                        ARG[:, k * 128:(k + 1) * 128],
                        self.ps[s][:, (k % 4) * 128:(k % 4) * 128 + 128],
                        CTI[:, c * 128 + 64 + h:c * 128 + 64 + h + 1], 30.0,
                        ALU.add, ALU.min), "vec")
                    if k % 4 == 3:
                        self.edone("vector", s)
                varg = E.now("vec")
                E.wait("scalar", "vec", varg)
                E.op("scalar", lambda e: e.activation(EXD[:, :], ARG[:, :],
                                                      AF.Exp), "act")
                aexp = E.now("act")
                aexp_prev = aexp
                E.wait("vector", "act", aexp)
                E.wait("vector", "vec", vbc)
                E.wait("vector", "mm", mmy_prev)
                if g == 0:
                    E.wait("vector", "ys", st_prev)
                E.op("vector", lambda e: e.tensor_tensor(
                    SCO[:, :].rearrange("p (a b) -> p a b", b=128),
                    EXD[:, :].rearrange("p (a b) -> p a b", b=128),
                    self.bcm[:, :].unsqueeze(1).broadcast_to([128, 8, 128]),
                    ALU.mult), "vec")
                vsc = E.now("vec")
                self.acq(5)
                for k in range(8):
                    E.op("tensor", lambda e, g=g, k=k, c=c: e.matmul(
                        self.ps[5][:, k * 64:(k + 1) * 64],
                        CFM[:, c * 128:(c + 1) * 128],
                        STATE[:, g * 512 + k * 64:g * 512 + (k + 1) * 64],
                        start=True, stop=True), None)
                self.acq(4)
                E.wait("tensor", "vec", vsc)
                for k in range(8):
                    E.op("tensor", lambda e, g=g, k=k: e.matmul(
                        self.ps[4][:, k * 64:(k + 1) * 64],
                        SCO[:, k * 128:(k + 1) * 128],
                        XT[:, g * 512 + k * 64:g * 512 + (k + 1) * 64],
                        start=True, stop=True), "mm" if k == 7 else None)
                mmy = E.now("mm")
                mmy_prev = mmy
                E.wait("vector", "mm", mmy)
                E.op("vector", lambda e, g=g, c=c: e.tensor_tensor(
                    self.smallf[:, 0:512].rearrange("p (a b) -> p a b", b=64),
                    self.ps[5][:, 0:512].rearrange("p (a b) -> p a b", b=64),
                    WX[:, c * 64:(c + 1) * 64][:, g * 8:(g + 1) * 8].unsqueeze(2)
                        .broadcast_to([128, 8, 64]), ALU.mult), "vec")
                self.edone("vector", 5)
                E.op("vector", lambda e, g=g: e.tensor_tensor(
                    YSB[:, g * 512:(g + 1) * 512], self.smallf[:, 0:512],
                    self.ps[4][:, 0:512], ALU.add), "vec")
                self.edone("vector", 4)
                E.op("vector", lambda e, g=g: e.tensor_tensor(
                    self.smallf[:, 512:1024].rearrange("p (a b) -> p a b", b=64),
                    STATE[:, g * 512:(g + 1) * 512].rearrange("p (a b) -> p a b", b=64),
                    self.ebcT[:, g * 8:(g + 1) * 8].unsqueeze(2)
                        .broadcast_to([128, 8, 64]), ALU.mult), "vec")
                vdec = E.now("vec")
                self.acq(7)
                E.wait("tensor", "al", albt)
                for k in range(8):
                    E.op("tensor", lambda e, g=g, k=k: e.matmul(
                        self.ps[7][:, k * 64:(k + 1) * 64], self.btm[:, :],
                        XW[:, g * 512 + k * 64:g * 512 + (k + 1) * 64],
                        start=True, stop=True), "mm" if k == 7 else None)
                mms = E.now("mm")
                E.wait("vector", "mm", mms)
                E.op("vector", lambda e, g=g: e.tensor_tensor(
                    STATE[:, g * 512:(g + 1) * 512], self.smallf[:, 512:1024],
                    self.ps[7][:, 0:512], ALU.add), "vec")
                self.edone("vector", 7)
                vsc_prev = E.now("vec")
            vch = E.now("vec")
            vch_prev = vch
            E.wait("sync", "vec", vch)
            E.dma("sync", d["d_y"][c * 128:(c + 1) * 128, :], YSB[:, :], sem="ys")
            st_prev = E.now("ys")
        E.wait("sync", "vec", E.now("vec"))
        E.dma("sync", d["cc_s_in"][:, :], STATE[:, :], sem="ys")
        E.dma("sync", d["cc_l_in"][:, :], self.tots[0:1, 1088:1152], sem="ys")
        self.pack_st = E.now("ys")

    def ph_exchange_z(self):
        E, d = self.E, self.d
        E.wait("gpsimd", "ys", self.pack_st)
        E.op("gpsimd", lambda e: e.collective_compute(
            "AllGather", ALU.bypass, replica_groups=[list(range(8))],
            ins=[d["cc_s_in"][:, :]], outs=[d["cc_s_out"][:, :]]), "cc")
        E.op("gpsimd", lambda e: e.collective_compute(
            "AllGather", ALU.bypass, replica_groups=[list(range(8))],
            ins=[d["cc_l_in"][:, :]], outs=[d["cc_l_out"][:, :]]), "cc")
        SZ = self.bv(0, 32768)
        self._phv = E.now("vec")
        pan_mm = {0: E.now("mm"), 1: E.now("mm")}
        for zc in range(D_INNER // 128):
            p = zc % 2
            pan = self.wst[:, p * 2048:p * 2048 + 2048]
            wl = self.wload(pan.rearrange("p (k q) -> p k q", q=128),
                            d["w_z"][zc].transpose([1, 0, 2]), pan_mm[p],
                            sem=f"w{p}")
            for (o, sz) in _splits(1024):
                pairs = [(pan[:, kc * 128:kc * 128 + 128],
                          self.act[:, kc * NT + HALO + o:kc * NT + HALO + o + sz])
                         for kc in range(KD)]
                s, mmv = self.job(pairs, lambda s, sz=sz: self.ps[s][:, 0:sz],
                                  pre=[wl, ("ys", self.pack_st)])
                pan_mm[p] = mmv
                E.wait("scalar", "mm", mmv)
                E.op("scalar", lambda e, s=s, zc=zc, o=o, sz=sz: e.activation(
                    SZ[:, zc * 1024 + o:zc * 1024 + o + sz], self.ps[s][:, 0:sz],
                    AF.Silu), "act")
                self.edone("scalar", s)
        self.z_done = E.now("act")

    def ph_gating(self):
        E, d = self.E, self.d
        SZ = self.bv(0, 32768)
        VP = self.bv(32768, 32768)
        E.wait("sync", "ys", self.corr_st)
        self.wait_stores("sync")
        self.acq(4)
        self.acq(5)
        gn_mm = [E.now("mm"), E.now("mm")]
        v_done = [0, 0]
        for fc in range(32):
            p = fc % 2
            YT = self.wst[:, p * 4096:p * 4096 + 1024]
            XS = self.wst[:, p * 4096 + 1024:p * 4096 + 2048]
            T1 = self.wst[:, p * 4096 + 2048:p * 4096 + 3072]
            E.wait("sync", "vec", v_done[p])
            a1 = E.dma("sync", YT[:, :], d["d_y"][0:1024, fc * 128:(fc + 1) * 128],
                       sem=f"g{p}", transpose=True)
            a2 = E.dma("sync", XS[:, :], d["d_xbc"][fc * 128:(fc + 1) * 128, :],
                       sem=f"g{p}")
            E.wait("vector", f"g{p}", a2)
            E.wait("vector", "act", self.z_done)
            E.wait("vector", "mm", gn_mm[p])
            dc = self.ccols[:, 202 + fc:202 + fc + 1]
            gw = self.ccols[:, 32 + fc:32 + fc + 1]
            vpf = VP[:, fc * 1024:(fc + 1) * 1024]
            E.op("vector", lambda e, XS=XS, YT=YT, T1=T1, dc=dc:
                 e.scalar_tensor_tensor(T1[:, :], XS[:, :], dc, YT[:, :],
                                        ALU.mult, ALU.add), "vec")
            E.op("vector", lambda e, T1=T1, vpf=vpf, fc=fc: e.tensor_tensor(
                vpf, T1[:, :], SZ[:, fc * 1024:(fc + 1) * 1024], ALU.mult), "vec")
            E.op("vector", lambda e, vpf=vpf, T1=T1: e.tensor_tensor(
                T1[:, :], vpf, vpf, ALU.mult), "vec")
            vsq = E.now("vec")
            E.op("vector", lambda e, vpf=vpf, gw=gw: e.tensor_scalar_mul(
                vpf, vpf, gw), "vec")
            E.wait("tensor", "vec", vsq)
            for si, (o, sz) in enumerate(_splits(1024)):
                E.op("tensor", lambda e, si=si, o=o, sz=sz, T1=T1, fc=fc: e.matmul(
                    self.ps[4 + si][0:1, 0:sz], self.consts[:, 448:449],
                    T1[:, o:o + sz], start=(fc == 0), stop=(fc == 31)),
                    "mm" if si == 1 else None)
            gn_mm[p] = E.now("mm")
            v_done[p] = E.now("vec")
        mmgn = E.now("mm")
        E.wait("scalar", "mm", mmgn)
        for si in range(2):
            E.op("scalar", lambda e, si=si: e.activation(
                self.rows[0:1, si * 512:(si + 1) * 512],
                self.ps[4 + si][0:1, 0:512], AF.Ln, scale=1.0 / D_INNER, bias=EPS),
                "act")
            self.edone("scalar", 4 + si)
            E.op("scalar", lambda e, si=si: e.activation(
                self.rows[0:1, si * 512:(si + 1) * 512],
                self.rows[0:1, si * 512:(si + 1) * 512], AF.Exp, scale=-0.5), "act")
        agn = E.now("act")
        E.wait("tensor", "act", agn)
        self.acq(6)
        for i in range(8):
            E.op("tensor", lambda e, i=i: e.matmul(
                self.ps[6][:, i:i + 1], self.rows[0:1, i * 128:(i + 1) * 128],
                self.eye_f[0:1, 0:1], start=True, stop=True),
                "mm" if i == 7 else None)
        mmv = E.now("mm")
        E.wait("vector", "mm", mmv)
        E.op("vector", lambda e: e.tensor_copy(self.gcol[:, :], self.ps[6][:, 0:8]),
             "vec")
        self.edone("vector", 6)
        self.gn_done = E.now("vec")

    def ph_combine(self):
        E, d = self.E, self.d
        SIN = self.bv(40960, 4096)
        SL = [self.bv(45056, 4096), self.bv(49152, 4096)]
        self.wjrow = self.bv(53248, 4096)
        E.wait("gpsimd", "cc", E.now("cc"))
        all_ = E.dma("gpsimd", self.ldece[0:8, :], d["cc_l_out"][:, :], sem="lde")
        sl_al = [None] * 8
        sl_lastread = [0, 0]
        self.acq(6)
        E.wait("tensor", "lde", all_)
        E.op("tensor", lambda e: e.matmul(self.ps[6][0:64, 0:8], self.ldece[:, :],
                                          self.msel9[:, :], start=True, stop=True),
             "mm")
        mmv = E.now("mm")
        E.wait("scalar", "mm", mmv)
        E.op("scalar", lambda e: e.activation(self.wjT[:, :], self.ps[6][0:64, 0:8],
                                              AF.Exp), "act")
        self.edone("scalar", 6)
        awj = E.now("act")
        E.op("vector", lambda e: e.memset(SIN[:, :], 0.0), "vec")
        E.wait("tensor", "act", awj)
        for j in range(2):
            E.wait("gpsimd", "vec", sl_lastread[j % 2])
            sl_al[j] = E.dma("gpsimd", SL[j % 2][:, :],
                             d["cc_s_out"][j * 128:(j + 1) * 128, :], sem=f"sl{j % 2}")
        for j in range(8):
            self.acq(6)
            E.op("tensor", lambda e, j=j: e.transpose(
                self.ps[6][0:1, 0:64], self.wjT[:, j:j + 1],
                self.eye_f[0:64, 0:64]), "mm")
            mmv = E.now("mm")
            E.wait("vector", "mm", mmv)
            E.op("vector", lambda e: e.tensor_copy(self.rows[0:1, 0:64],
                                                   self.ps[6][0:1, 0:64]), "vec")
            self.edone("vector", 6)
            E.op("vector", lambda e: e.tensor_copy(
                self.wjrow[0:1, :].rearrange("p (a b) -> p a b", b=64),
                self.rows[0:1, 0:64].unsqueeze(2).broadcast_to([1, 64, 64])), "vec")
            vrow = E.now("vec")
            for g in range(8):
                s = g % 2
                self.acq(s)
                E.wait("tensor", "vec", vrow)
                E.op("tensor", lambda e, g=g, s=s: e.matmul(
                    self.ps[s][:, 0:512], self.onesb[:, :],
                    self.wjrow[0:1, g * 512:(g + 1) * 512], start=True, stop=True),
                    "mm")
                mmv = E.now("mm")
                E.wait("vector", "mm", mmv)
                E.wait("vector", f"sl{j % 2}", sl_al[j])
                E.op("vector", lambda e, g=g, s=s, j=j: e.tensor_tensor(
                    self.smallf[:, 0:512], SL[j % 2][:, g * 512:(g + 1) * 512],
                    self.ps[s][:, 0:512], ALU.mult), "vec")
                self.edone("vector", s)
                E.op("vector", lambda e, g=g: e.tensor_tensor(
                    SIN[:, g * 512:(g + 1) * 512], SIN[:, g * 512:(g + 1) * 512],
                    self.smallf[:, 0:512], ALU.add), "vec")
            sl_lastread[j % 2] = E.now("vec")
            if j + 2 < 8:
                E.wait("gpsimd", "vec", sl_lastread[(j + 2) % 2])
                sl_al[j + 2] = E.dma("gpsimd", SL[(j + 2) % 2][:, :],
                                     d["cc_s_out"][(j + 2) * 128:(j + 3) * 128, :],
                                     sem=f"sl{j % 2}")
        self.sin_done = E.now("vec")

    def ph_correction(self):
        E, d = self.E, self.d
        SIN = self.bv(40960, 4096)
        CTI = self.bv(32768, 2048, F32)
        CFM = self.bv(35840, 1024)
        YSB = self.bv(36864, 4096)
        v_prev = E.now("vec")
        for c in range(NCH):
            E.wait("sync", "vec", v_prev)
            E.wait("sync", "ys", E.now("ys"))
            aly = E.dma("sync", YSB[:, :], d["d_y"][c * 128:(c + 1) * 128, :])
            self.acq(6)
            E.op("tensor", lambda e, c=c: e.matmul(
                self.ps[6][:, 0:64], self.onesf[:, :],
                self.tots[0:1, 512 + c * 64:512 + (c + 1) * 64],
                start=True, stop=True), "mm")
            mmv = E.now("mm")
            E.wait("vector", "mm", mmv)
            E.op("vector", lambda e, c=c: e.tensor_tensor(
                self.smallf[:, 1024 + 0:1024 + 64], self.ps[6][:, 0:64],
                CTI[:, c * 128:c * 128 + 64], ALU.add), "vec")
            self.edone("vector", 6)
            E.wait("scalar", "vec", E.now("vec"))
            E.wait("scalar", "vec", v_prev)
            E.op("scalar", lambda e: e.activation(
                self.wtile[:, :], self.smallf[:, 1024:1088], AF.Exp), "act")
            acw = E.now("act")
            for g in range(8):
                s = g % 2
                self.acq(s)
                E.wait("tensor", "vec", self.sin_done)
                for k in range(8):
                    E.op("tensor", lambda e, g=g, k=k, s=s, c=c: e.matmul(
                        self.ps[s][:, k * 64:(k + 1) * 64],
                        CFM[:, c * 128:(c + 1) * 128],
                        SIN[:, g * 512 + k * 64:g * 512 + (k + 1) * 64],
                        start=True, stop=True), "mm" if k == 7 else None)
                mmv = E.now("mm")
                E.wait("vector", "mm", mmv)
                E.wait("vector", "act", acw)
                E.wait("vector", "al", aly)
                E.op("vector", lambda e, g=g, s=s: e.tensor_tensor(
                    self.smallf[:, 0:512].rearrange("p (a b) -> p a b", b=64),
                    self.ps[s][:, 0:512].rearrange("p (a b) -> p a b", b=64),
                    self.wtile[:, g * 8:(g + 1) * 8].unsqueeze(2)
                        .broadcast_to([128, 8, 64]), ALU.mult), "vec")
                self.edone("vector", s)
                E.op("vector", lambda e, g=g: e.tensor_tensor(
                    YSB[:, g * 512:(g + 1) * 512], YSB[:, g * 512:(g + 1) * 512],
                    self.smallf[:, 0:512], ALU.add), "vec")
            v_prev = E.now("vec")
            E.wait("sync", "vec", v_prev)
            E.dma("sync", d["d_y"][c * 128:(c + 1) * 128, :], YSB[:, :], sem="ys")
        self.corr_st = E.now("ys")

    def ph_outproj(self):
        E, d = self.E, self.d
        VP = self.bv(32768, 32768)
        vp_v = E.now("vec")
        self._phv = vp_v
        pan_mm = {0: E.now("mm"), 1: E.now("mm")}
        OS = self.wst[:, 0:2048]
        ost_st = {0: 0, 1: 0}
        for dc in range(KD):
            p = dc % 2
            pan = self.act[:, p * 4096:p * 4096 + 4096]
            wl = self.wload(pan.rearrange("p (k q) -> p k q", q=128),
                            d["w_op"][dc].transpose([1, 0, 2]), pan_mm[p],
                            sem=f"w{p}")
            for si, (o, sz) in enumerate(_splits(1024)):
                pairs = [(pan[:, kc * 128:kc * 128 + 128],
                          VP[:, kc * 1024 + o:kc * 1024 + o + sz])
                         for kc in range(D_INNER // 128)]
                s, mmv = self.job(pairs, lambda s, sz=sz: self.ps[s][:, 0:sz],
                                  pre=[wl, ("vec", vp_v)])
                pan_mm[p] = mmv
                ost = OS[:, (si % 2) * 1024:(si % 2) * 1024 + sz]
                E.wait("vector", "mm", mmv)
                E.wait("vector", f"fs{si % 2}", ost_st[si % 2])
                E.op("vector", lambda e, s=s, sz=sz, ost=ost: e.tensor_copy(
                    ost, self.ps[s][:, 0:sz]), "vec")
                self.edone("vector", s)
                E.wait("sync", "vec", E.now("vec"))
                E.dma("sync", d["d_mo"][dc * 128:dc * 128 + 128, o:o + sz], ost,
                      sem=f"fs{si % 2}")
                ost_st[si % 2] = E.now(f"fs{si % 2}")
        self.mo_st = None

    def ph_h(self):
        E, d = self.E, self.d
        SA = self.bv(49152, 4096, F32)
        SB = self.bv(53248, 4096, F32)
        B1 = self.bv(61440, 2048)
        self.wait_stores("sync")
        E.wait("sync", "vec", self.gn_done)
        for i, (o, sz) in enumerate(T8):
            E.wait("sync", "act", E.now("act"))
            E.wait("sync", "ts", E.now("ts"))
            av = E.dma("sync", SA[0:sz, :], d["d_xa"][HALO + o:HALO + o + sz, :])
            avT = E.dma("sync", B1[0:128, :], d["d_mo"][:, o:o + 128], transpose=True)
            ti = i + 1
            m = self.cols[:, 4 * ti:4 * ti + 1]
            r = self.cols[:, 4 * ti + 2:4 * ti + 3]
            E.wait("vector", "al", max(av, avT))
            E.wait("vector", "ts", E.now("ts"))
            E.op("vector", lambda e, sz=sz, m=m, r=r: e.tensor_scalar(
                SB[0:sz, :], SA[0:sz, :], m[0:sz, :], r[0:sz, :],
                ALU.subtract, ALU.mult), "vec")
            E.op("vector", lambda e, sz=sz, i=i: e.scalar_tensor_tensor(
                SB[0:sz, :], B1[0:sz, :], self.gcol[:, i:i + 1], SB[0:sz, :],
                ALU.mult, ALU.add), "vec")
            E.wait("sync", "vec", E.now("vec"))
            E.dma("sync", d["d_h"][o:o + sz, :], SB[0:sz, :], sem="ts")
            q = self.cols[:, 36 + 2 * i:36 + 2 * i + 1]
            rs = self.cols[:, 36 + 2 * i + 1:36 + 2 * i + 2]
            E.wait("scalar", "vec", E.now("vec"))
            E.op("scalar", lambda e, sz=sz, q=q: e.activation(
                SA[0:sz, :], SB[0:sz, :], AF.Square, accum_out=q[0:sz, :]), "act")
            E.op("scalar", lambda e, sz=sz, q=q, rs=rs: e.activation(
                rs[0:sz, :], q[0:sz, :], AF.Ln, scale=1.0 / D_MODEL, bias=EPS), "act")
            E.op("scalar", lambda e, sz=sz, rs=rs: e.activation(
                rs[0:sz, :], rs[0:sz, :], AF.Exp, scale=-0.5), "act")
            E.wait("vector", "act", E.now("act"))
            E.op("vector", lambda e, sz=sz, rs=rs: e.tensor_scalar_mul(
                B1[0:sz, :], SB[0:sz, :], rs[0:sz, :]), "vec")
            E.wait("sync", "vec", E.now("vec"))
            E.dma("sync", d["d_h2n"][o:o + sz, :], B1[0:sz, :], sem="ts")

    def ph_swiglu(self, fills):
        E, d = self.E, self.d
        al_f, vec_f = fills
        pre_f = [("al", al_f), ("vec", vec_f)]
        GU = self.bv(0, 65536)
        sil_rd = {}
        self._phv = E.now("vec")
        pan_mm = {0: E.now("mm"), 1: E.now("mm"), 2: E.now("mm"), 3: E.now("mm")}
        for hc in range(D_FF // 128):
            p = hc % 2
            gp_ = self.wst[:, p * 4096:p * 4096 + 2048]
            up_ = self.wst[:, p * 4096 + 2048:p * 4096 + 4096]
            wlg = self.wload(gp_.rearrange("p (k q) -> p k q", q=128),
                             d["w_gate"][hc].transpose([1, 0, 2]), pan_mm[2 * p],
                             sem=f"w{2 * p}")
            wlu = self.wload(up_.rearrange("p (k q) -> p k q", q=128),
                             d["w_up"][hc].transpose([1, 0, 2]), pan_mm[2 * p + 1],
                             sem=f"w{2 * p + 1}")
            for si, (o, sz) in enumerate(_splits(1024)):
                gpairs = [(gp_[:, kc * 128:kc * 128 + 128],
                           self.act[:, kc * NT + o:kc * NT + o + sz])
                          for kc in range(KD)]
                upairs = [(up_[:, kc * 128:kc * 128 + 128],
                           self.act[:, kc * NT + o:kc * NT + o + sz])
                          for kc in range(KD)]
                sA, mmA = self.job(gpairs, lambda s, sz=sz: self.ps[s][:, 0:sz],
                                   pre=pre_f + [wlg])
                sB, mmB = self.job(upairs, lambda s, sz=sz: self.ps[s][:, 0:sz],
                                   pre=[wlu])
                pan_mm[2 * p] = mmA
                pan_mm[2 * p + 1] = mmB
                sl = self.silub[:, o:o + sz]
                E.wait("scalar", "mm", mmA)
                E.wait("scalar", "vec", sil_rd.get(o, 0))
                E.op("scalar", lambda e, sA=sA, sz=sz, sl=sl: e.activation(
                    sl, self.ps[sA][:, 0:sz], AF.Silu), "act")
                self.edone("scalar", sA)
                E.wait("vector", "mm", mmB)
                E.wait("vector", "act", E.now("act"))
                E.op("vector", lambda e, sB=sB, sz=sz, sl=sl, hc=hc, o=o:
                     e.tensor_tensor(GU[:, hc * 1024 + o:hc * 1024 + o + sz],
                                     self.ps[sB][:, 0:sz], sl, ALU.mult), "vec")
                self.edone("vector", sB)
                sil_rd[o] = E.now("vec")
        vgu = E.now("vec")
        self._phv = vgu
        pan_mm2 = {0: E.now("mm"), 1: E.now("mm")}
        OS = self.wst[:, 0:2048]
        ost_st = {0: 0, 1: 0}
        for dc in range(KD):
            p = dc % 2
            pan = self.act[:, p * 8192:p * 8192 + 8192]
            wld = self.wload(pan.rearrange("p (k q) -> p k q", q=128),
                             d["w_down"][dc].transpose([1, 0, 2]), pan_mm2[p],
                             sem=f"w{p}")
            for si, (o, sz) in enumerate(_splits(1024)):
                pairs = [(pan[:, hc * 128:hc * 128 + 128],
                          GU[:, hc * 1024 + o:hc * 1024 + o + sz])
                         for hc in range(D_FF // 128)]
                s, mmv = self.job(pairs, lambda s, sz=sz: self.ps[s][:, 0:sz],
                                  pre=[wld, ("vec", vgu)])
                pan_mm2[p] = mmv
                ost = OS[:, (si % 2) * 1024:(si % 2) * 1024 + sz]
                E.wait("vector", "mm", mmv)
                E.wait("vector", f"fs{si % 2}", ost_st[si % 2])
                E.op("vector", lambda e, s=s, sz=sz, ost=ost: e.tensor_copy(
                    ost, self.ps[s][:, 0:sz]), "vec")
                self.edone("vector", s)
                E.wait("sync", "vec", E.now("vec"))
                E.dma("sync", d["d_sw"][dc * 128:dc * 128 + 128, o:o + sz], ost,
                      sem=f"fs{si % 2}")
                ost_st[si % 2] = E.now(f"fs{si % 2}")

    def ph_xb(self):
        E, d = self.E, self.d
        SA = self.bv(49152, 4096, F32)
        SB = self.bv(53248, 4096, F32)
        SC = self.bv(57344 + 2048, 4096, F32)
        SC = self.bv(45056, 4096, F32)
        B1 = self.bv(61440, 2048)
        self.wait_stores("sync")
        for i, (o, sz) in enumerate(T8):
            E.wait("sync", "act", E.now("act"))
            E.wait("sync", "vec", E.now("vec"))
            E.wait("sync", "ts", E.now("ts"))
            av = E.dma("sync", SA[0:sz, :], d["d_xa"][HALO + o:HALO + o + sz, :])
            av2 = E.dma("sync", SC[0:sz, :], d["d_h"][o:o + sz, :])
            avT = E.dma("sync", B1[0:128, :], d["d_sw"][:, o:o + 128], transpose=True)
            E.wait("vector", "al", avT)
            E.op("vector", lambda e, sz=sz: e.tensor_tensor(
                SA[0:sz, :], SA[0:sz, :], SC[0:sz, :], ALU.add), "vec")
            E.op("vector", lambda e, sz=sz: e.tensor_tensor(
                SA[0:sz, :], SA[0:sz, :], B1[0:sz, :], ALU.add), "vec")
            E.wait("sync", "vec", E.now("vec"))
            E.dma("sync", d["d_xb"][o:o + sz, :], SA[0:sz, :], sem="ts")
            m, q, r, v = self.ln_stats(SA, sz, 52 + 4 * i)
            E.wait("vector", "act", E.now("act"))
            E.wait("vector", "ts", E.now("ts"))
            E.op("vector", lambda e, sz=sz, m=m, r=r: e.tensor_scalar(
                B1[0:sz, :], SA[0:sz, :], m[0:sz, :], r[0:sz, :],
                ALU.subtract, ALU.mult), "vec")
            E.wait("sync", "vec", E.now("vec"))
            E.dma("sync", d["d_xn3"][o:o + sz, :], B1[0:sz, :], sem="ts")

    def ph_final(self):
        E, d = self.E, self.d
        SA = self.bv(49152, 4096, F32)
        B1 = self.bv(61440, 2048)
        self.wait_stores("sync")
        for i, (o, sz) in enumerate(T8):
            E.wait("sync", "vec", E.now("vec"))
            E.wait("sync", "ts", E.now("ts"))
            av = E.dma("sync", SA[0:sz, :], d["d_xb"][o:o + sz, :])
            avT = E.dma("sync", B1[0:128, :], d["d_f2"][:, o:o + 128], transpose=True)
            E.wait("vector", "al", avT)
            E.op("vector", lambda e, sz=sz: e.tensor_tensor(
                SA[0:sz, :], SA[0:sz, :], B1[0:sz, :], ALU.add), "vec")
            E.wait("sync", "vec", E.now("vec"))
            E.dma("sync", self.out_t[o:o + sz, :], SA[0:sz, :], sem="ts")
        E.wait("sync", "ts", E.now("ts"))

    def ph_debug(self):
        E = self.E
        self.wait_stores("sync")
        E.wait("sync", "vec", E.now("vec"))
        E.wait("sync", "act", E.now("act"))
        E.wait("sync", "mm", E.now("mm"))
        for name, t in self.dbg_t.items():
            s = self.d[name]
            E.dma("sync", t[:, :], s[:, :], sem="ts")
        E.wait("sync", "ts", E.now("ts"))


_DEBUG = ()
_STAGES = 99


def host_shared(inputs):
    def tile_lhsT(w):
        K, M = w.shape
        return np.ascontiguousarray(
            w.reshape(K // 128, 128, M // 128, 128).transpose(2, 0, 1, 3))

    ip = np.asarray(inputs["L0_attn_in_proj_w"])
    cw = np.asarray(inputs["L0_attn_conv_w"])[:, 0, :]
    diag = np.zeros((NFC, K_CONV, 128, 128), np.float32)
    idx = np.arange(128)
    diag[:, :, idx, idx] = cw.reshape(NFC, 128, K_CONV).transpose(0, 2, 1)

    def colmat(v):
        v = np.asarray(v, np.float32)
        return np.ascontiguousarray(v.reshape(-1, 128).T)

    ccols = np.zeros((128, 256), np.float32)
    ccols[:, 0:16] = colmat(inputs["L0_attn_ln1_w"])
    ccols[:, 16:32] = colmat(inputs["L0_attn_ln2_w"])
    ccols[:, 32:64] = colmat(inputs["L0_attn_gnorm_w"])
    ccols[:, 64:108] = colmat(inputs["L0_ffn_s_gate"])
    ccols[:, 108:152] = colmat(inputs["L0_ffn_s_up"])
    ccols[:, 152:168] = colmat(inputs["L0_ffn_s_down"])
    ccols[:, 168:202] = colmat(inputs["L0_attn_conv_b"])
    ccols[:, 202:234] = colmat(np.repeat(np.asarray(inputs["L0_attn_D"], np.float32),
                                         D_HEAD))
    ccols[:, 250] = EPS

    dcol = np.zeros((64, 2), np.float32)
    dcol[:, 0] = np.asarray(inputs["L0_attn_dt_bias"], np.float32)
    dcol[:, 1] = np.asarray(inputs["L0_attn_A_log"], np.float32)

    consts = np.zeros((128, 512), np.float32)
    consts[0, 0:128] = 1.0
    consts[0, 128:256] = -128.0
    consts[:, 256:384] = np.triu(np.ones((128, 128), np.float32))
    consts[:, 384:512] = 0.0
    consts[:, 448] = 1.0

    onehot = np.zeros((64, 64, 128), np.float32)
    onehot[np.arange(64), np.arange(64), :] = 1.0

    sh = {
        "w_xbc": tile_lhsT(np.ascontiguousarray(ip[:, D_INNER:D_INNER + CONV_DIM])),
        "w_dt": np.ascontiguousarray(ip[:, -N_HEADS:].reshape(KD, 128, N_HEADS)),
        "w_z": tile_lhsT(np.ascontiguousarray(ip[:, :D_INNER])),
        "w_gate": tile_lhsT(np.asarray(inputs["L0_attn_gate_w"])),
        "w_up": tile_lhsT(np.asarray(inputs["L0_attn_up_w"])),
        "w_down": tile_lhsT(np.asarray(inputs["L0_attn_down_w"])),
        "w_op": tile_lhsT(np.asarray(inputs["L0_attn_out_proj_w"])),
        "wq_g": tile_lhsT(np.ascontiguousarray(np.asarray(inputs["L0_ffn_wq_gate"]).T)),
        "wq_u": tile_lhsT(np.ascontiguousarray(np.asarray(inputs["L0_ffn_wq_up"]).T)),
        "wq_d": tile_lhsT(np.ascontiguousarray(np.asarray(inputs["L0_ffn_wq_down"]).T)),
        "w_conv": diag,
        "ccols_in": ccols,
        "dcol_in": dcol,
        "onehot": np.ascontiguousarray(onehot.reshape(64, 64 * 128)),
        "ones_f": np.ones((1, 128), np.float32),
        "ones_b": np.ones((1, 128), BF),
        "consts_in": consts.astype(BF),
        "eye_f_in": np.eye(128, dtype=np.float32),
    }
    return sh


def host_core(inputs, core_id):
    x = np.asarray(inputs["L0_attn_hidden_states"])
    b, seg = divmod(core_id, 4)
    t0 = seg * 1024
    halo = (np.zeros((HALO, D_MODEL), np.float32) if seg == 0
            else np.ascontiguousarray(np.asarray(x[b, t0 - HALO:t0], np.float32)))
    xin = np.concatenate([halo, np.asarray(x[b, t0:t0 + 1024], np.float32)], axis=0)

    msel = np.zeros((9, 8), np.float32)
    msel[8, :] = NEG
    for j in range(8):
        jb, jseg = divmod(j, 4)
        if jb == b and jseg < seg:
            msel[8, j] = 0.0
            for i in range(8):
                ib, iseg = divmod(i, 4)
                if ib == b and jseg < iseg < seg:
                    msel[i, j] = 1.0
    return {"x_in": np.ascontiguousarray(xin), "msel_in": msel}


def kernel(**inputs):
    sh = host_shared(inputs)
    in_maps = []
    for c in range(8):
        m = dict(sh)
        m.update(host_core(inputs, c))
        in_maps.append(m)
    b = Build(debug=_DEBUG, stages=_STAGES)
    nc = b.build()
    res = run_bass_kernel_spmd(nc, in_maps, core_ids=list(range(8)))
    out = np.zeros((2, 4096, D_MODEL), np.float32)
    for c in range(8):
        bb, seg = divmod(c, 4)
        out[bb, seg * 1024:(seg + 1) * 1024] = res.results[c]["out"]
    if _DEBUG:
        kernel.dbg = [{k: v for k, v in r.items() if k.startswith("dbg_")}
                      for r in res.results]
    return out
